# version 46
# baseline (speedup 1.0000x reference)
import contextlib
import numpy as np
import concourse.bass as bass
import concourse.mybir as mybir
from concourse.bass_utils import run_bass_kernel_spmd
from concourse.bass_core_rust import IndirectOffsetOnAxis

F32 = mybir.dt.float32
BF16 = mybir.dt.bfloat16
I32 = mybir.dt.int32
AF = mybir.ActivationFunctionType
ALU = mybir.AluOpType
AX = mybir.AxisListType

D = 1024
E = 8
HID = 2048
SH = 4096
TILE = 512
NT = SH // TILE
NG = TILE // 128
KD = D // 128
KH = HID // 128
GC = NT * NG
T3 = 256
NT3 = (2 * SH) // T3 + E
NR = NT3 * T3
BIGF = float(2 ** 20)
TILE_ORDER = [7, 0, 1, 2, 3, 4, 5, 6]
SPECIAL_COL = 384
SPECIAL_GROUP = 3


def build_nc():
    nc = bass.Bass("TRN2", target_bir_lowering=False)

    x_in = nc.dram_tensor("x", [SH, D], F32, kind="ExternalInput")
    Wv = nc.dram_tensor("Wv", [D, D], F32, kind="ExternalInput")
    Wo = nc.dram_tensor("Wo", [D, D], F32, kind="ExternalInput")
    Wg = nc.dram_tensor("Wg", [D, E], F32, kind="ExternalInput")
    W1 = nc.dram_tensor("W1", [E * 256, KD * 1024], F32, kind="ExternalInput")
    b1 = nc.dram_tensor("b1", [E, HID], F32, kind="ExternalInput")
    W2 = nc.dram_tensor("W2", [E * 256, KH * 512], F32, kind="ExternalInput")
    b2 = nc.dram_tensor("b2", [E, D], F32, kind="ExternalInput")
    ident = nc.dram_tensor("ident", [128, 128], F32, kind="ExternalInput")
    onehot = nc.dram_tensor("onehot", [E, E * 128], F32, kind="ExternalInput")
    tmask = nc.dram_tensor("tmask", [128, 1], F32, kind="ExternalInput")
    ltri = nc.dram_tensor("ltri", [128, 128], F32, kind="ExternalInput")
    ones128 = nc.dram_tensor("ones128", [128, 1], F32, kind="ExternalInput")
    ones1r = nc.dram_tensor("ones1r", [1, 128], F32, kind="ExternalInput")
    toksc = nc.dram_tensor("toksc", [128, GC], F32, kind="ExternalInput")
    pbase = nc.dram_tensor("pbase", [128, 3], F32, kind="ExternalInput")
    jtc = nc.dram_tensor("jtc", [1, NT3], F32, kind="ExternalInput")
    segm = nc.dram_tensor("segm", [128, E * GC], F32, kind="ExternalInput")
    out = nc.dram_tensor("out", [SH, D], F32, kind="ExternalOutput")

    XNBF = nc.dram_tensor("XNBF", [SH, D], BF16)
    B1T = nc.dram_tensor("B1T", [128 * E, KH], F32)
    LREC = nc.dram_tensor("LREC", [NR, 3], F32)
    ZD = nc.dram_tensor("ZD", [2 * SH, D], F32)

    es = contextlib.ExitStack()
    with es:
        sb = lambda n, s, d: es.enter_context(nc.sbuf_tensor(n, s, d))
        pst = lambda n, s, d: es.enter_context(nc.psum_tensor(n, s, d))
        mksem = lambda n: es.enter_context(nc.semaphore(n))

        x_t = sb("x_t", [128, 2 * NG * D], F32)
        xnb = sb("xnb", [128, 2 * NG * D], BF16)
        mr_all = sb("mr_all", [128, 2 * GC], F32)
        xnT = sb("xnT", [128, KD * TILE], BF16)
        xn_col = sb("xn_col", [128, KD], BF16)
        cols = sb("cols", [128, 64], F32)
        lg = sb("lg", [128, NG * E], F32)
        sme = sb("sme", [128, NG * E], F32)
        m8 = sb("m8", [128, NG * E], F32)
        w_t = sb("w_t", [128, NG * E], F32)
        w_all = sb("w_all", [128, E * GC], F32)
        s_grp = sb("s_grp", [128, D], F32)
        scr1 = sb("scr1", [128, D], F32)
        g_grp = sb("g_grp", [128, D], F32)
        out_g = sb("out_g", [128, 2 * D], F32)
        sel = sb("sel", [128, E * GC], F32)
        incl = sb("incl", [128, E * GC], F32)
        slot = sb("slot", [128, E * GC], F32)
        mm1 = sb("mm1", [128, E * GC], F32)
        mm2 = sb("mm2", [128, E * GC], F32)
        didx = sb("didx", [128, E * GC], F32)
        runsb = sb("runsb", [128, GC], F32)
        cnt = sb("cnt", [128, E], F32)
        base_pe = sb("base_pe", [128, E], F32)
        gbase = sb("gbase", [128, E], F32)
        tot = sb("tot", [1, E], F32)
        tr8 = sb("tr8", [1, 3 * E], F32)
        tr8i = sb("tr8i", [1, 2 * E], I32)
        etile = sb("etile", [1, 4 * NT3], F32)
        ebf = sb("ebf", [128, NT3], F32)
        dbf = sb("dbf", [128, NT3], F32)
        dbf2 = sb("dbf2", [128, NT3], F32)
        wrow = sb("wrow", [128, NT3], F32)
        idxw = sb("idxw", [128, 2 * NT3], F32)
        idxw_i = sb("idxw_i", [128, 2 * NT3], I32)
        idxw2 = sb("idxw2", [128, 2 * NT3], F32)
        idxw2_i = sb("idxw2_i", [128, 2 * NT3], I32)
        idxb = sb("idxb", [128, NT3], F32)
        idxb_i = sb("idxb_i", [128, NT3], I32)
        idxb2_i = sb("idxb2_i", [128, NT3], I32)
        didx_r = sb("didx_r", [128, 2 * GC], F32)
        didx_ri = sb("didx_ri", [128, 2 * GC], I32)
        w_r = sb("w_r", [128, 2 * GC], F32)
        dstc = sb("dstc", [128, 2 * GC], F32)
        recs = sb("recs", [128, 2 * GC * 3], F32)
        acc_s = sb("acc_s", [128, GC], F32)
        W1a = sb("W1a", [128, KD * 1024], BF16)
        W1b = sb("W1b", [128, KD * 1024], BF16)
        W1a1 = sb("W1a1", [128, KD * 1024], BF16)
        W1b1 = sb("W1b1", [128, KD * 1024], BF16)
        W2a = sb("W2a", [128, KD * 1024], BF16)
        W2b = sb("W2b", [128, KD * 1024], BF16)
        zt = sb("zt", [128, 2 * D], F32)
        rec_sb = sb("rec_sb", [128, 4 * 6], F32)
        tok_i = sb("tok_i", [128, 4 * 2], I32)
        dst_i = sb("dst_i", [128, 4 * 2], I32)
        w_f = sb("w_f", [128, 4 * 2], F32)
        b1_t = sb("b1_t", [128, 2 * KH], F32)
        b2bc = sb("b2bc", [128, 2 * D], F32)
        u_bf = sb("u_bf", [128, E], BF16)
        t_f = sb("t_f", [128, E], F32)
        t8 = sb("t8", [8, 128], F32)
        b1_sb = sb("b1_sb", [128, E * KH], F32)
        wg_f = sb("wg_f", [128, KD * E], F32)
        wg_b = sb("wg_b", [128, KD * E], BF16)
        idf = sb("idf", [128, 128], F32)
        idb = sb("idb", [128, 128], BF16)
        tm_sb = sb("tm_sb", [128, 1], F32)
        eps_c = sb("eps_c", [128, 1], F32)
        mh_c = sb("mh_c", [128, 1], F32)
        ltri_sb = sb("ltri_sb", [128, 128], F32)
        o128 = sb("o128", [128, 1], F32)
        o1r = sb("o1r", [1, 128], F32)
        toks_sb = sb("toks_sb", [128, GC], F32)
        pbase_sb = sb("pbase_sb", [128, 3], F32)
        jtc_sb = sb("jtc_sb", [1, NT3], F32)
        segm_sb = sb("segm_sb", [128, E * GC], F32)


        ps_h = [pst("ps_h0", [128, 512], F32), pst("ps_h1", [128, 512], F32)]
        ps_y = [pst("ps_y0", [128, 512], F32), pst("ps_y1", [128, 512], F32)]
        ps_tr = pst("ps_tr", [128, 1024], BF16)
        ps_b0 = pst("ps_b0", [128, 512], F32)
        ps_b1 = pst("ps_b1", [128, 512], F32)
        ps_m = pst("ps_m", [128, 512], F32)

        SEM = {k: mksem("s_" + k) for k in (
            "ld", "sd", "xs", "zf", "cvo", "cast", "sct", "rc0", "rc1", "rc2", "rc3",
            "gx0", "gx1", "w1a", "w1b", "w1a1", "w1b1", "w2a", "w2b", "b1g", "b2g",
            "zs", "zs1", "sd1", "xs1",
            "st0", "st1", "pe", "dv", "ac")}
        C = {k: 0 for k in SEM}
        DMA = {k for k in SEM if k not in ("pe", "dv", "ac")}

        sched = {k: [] for k in ("sync", "gpsimd", "scalar", "vector", "tensor")}

        def em(eng, method, *args, waits=(), incs=(), **kw):
            waits = list(waits)
            if eng == "vector" and C["dv"] > 0:
                waits.append(("dv", C["dv"]))
            ws = [(SEM[n], v) for (n, v) in waits if v > 0]
            if eng == "vector" and "dv" not in incs:
                incs = list(incs) + ["dv"]
            sched[eng].append((method, args, kw, ws, [SEM[n] for n in incs]))
            for n in incs:
                C[n] += 16 if n in DMA else 1
            return {n: C[n] for n in incs}

        A = bass.AP
        x_tv = x_t.ap().rearrange("p (s g d) -> p s g d", s=2, g=NG)
        xnT_v = xnT.ap().rearrange("p (c t2) -> p c t2", c=KD)
        ps_tr_v = ps_tr.ap().rearrange("p (c j) -> p c j", c=KD)
        WvN = W1a.ap().rearrange("p (r c j) -> p r c j", r=KD, c=KD)
        WoN = W1b.ap().rearrange("p (r c j) -> p r c j", r=KD, c=KD)
        WvT = W2a.ap().rearrange("p (k m j) -> p k m j", k=KD, m=KD)
        WoT = W2b.ap().rearrange("p (k m j) -> p k m j", k=KD, m=KD)
        W1vp = [[W1a.ap().rearrange("p (k h) -> p k h", k=KD),
                 W1b.ap().rearrange("p (k h) -> p k h", k=KD)],
                [W1a1.ap().rearrange("p (k h) -> p k h", k=KD),
                 W1b1.ap().rearrange("p (k h) -> p k h", k=KD)]]
        W2v = [W2a.ap().rearrange("p (k n) -> p k n", k=KD),
               W2b.ap().rearrange("p (k n) -> p k n", k=KD)]
        w_all_v = w_all.ap().rearrange("p (e gc) -> p e gc", e=E)
        incl_v = incl.ap().rearrange("p (e gc) -> p e gc", e=E)
        recs_v = recs.ap().rearrange("p (r gc c3) -> p r gc c3", r=2, gc=GC)
        rec_sb_v = rec_sb.ap().rearrange("p (s h c3) -> p s h c3", s=4, h=2)
        tok_i_v = tok_i.ap().rearrange("p (s h) -> p s h", s=4)
        dst_i_v = dst_i.ap().rearrange("p (s h) -> p s h", s=4)
        w_f_v = w_f.ap().rearrange("p (s h) -> p s h", s=4)
        b1_t_v = b1_t.ap().rearrange("p (s m) -> p s m", s=2)
        gx_v = xnb.ap().rearrange("p (s q) -> p s q", s=4)
        gxT_v = xnb.ap().rearrange("p (s c t2) -> p s c t2", s=4, c=KD)[:, 2, :, :]
        h_bf = xnT
        hv = h_bf.ap().rearrange("p (k t2) -> p k t2", k=KH)
        zn_v = x_t.ap().rearrange("p (s d2) -> p s d2", s=8)

        em("sync", "dma_start", out=idf[:, :], in_=ident[:, :], incs=["ld"])
        em("sync", "dma_start", out=tm_sb[:, :], in_=tmask[:, :], incs=["ld"])
        em("sync", "dma_start", out=ltri_sb[:, :], in_=ltri[:, :], incs=["ld"])
        em("sync", "dma_start", out=o128[:, :], in_=ones128[:, :], incs=["ld"])
        em("sync", "dma_start", out=o1r[:, :], in_=ones1r[:, :], incs=["ld"])
        em("sync", "dma_start", out=toks_sb[:, :], in_=toksc[:, :], incs=["ld"])
        em("sync", "dma_start", out=pbase_sb[:, :], in_=pbase[:, :], incs=["ld"])
        em("sync", "dma_start", out=jtc_sb[:, :], in_=jtc[:, :], incs=["ld"])
        em("sync", "dma_start", out=segm_sb[:, :], in_=segm[:, :], incs=["ld"])
        em("sync", "dma_start", out=wg_f.ap().rearrange("p (k e) -> p k e", k=KD),
           in_=A(Wg, 0, [[E, 128], [128 * E, KD], [1, E]]), incs=["ld"])
        b_raw = out_g
        em("sync", "dma_start", out=b_raw[0:16, 0:E * 128].rearrange("i (e j) -> i e j", e=E),
           in_=A(b1, 0, [[128, 16], [HID, E], [1, 128]]), incs=["ld"])
        oh_sb = b2bc
        em("sync", "dma_start", out=oh_sb[0:8, 0:E * 128], in_=onehot[:, :], incs=["ld"])
        ld_consts = C["ld"]

        em("vector", "memset", zt[:, :], 0.0)
        dv_z0 = C["dv"]
        for q in range(32):
            em("sync", "dma_start",
               out=A(ZD, q * 256 * D, [[2 * D, 128], [D, 2], [1, D]]),
               in_=zt.ap().rearrange("p (h d2) -> p h d2", h=2),
               waits=([("dv", dv_z0)] if q == 0 else []), incs=["zf"])
        em("vector", "memset", recs[:, 0:192], BIGF)
        em("vector", "memset", didx[:, 0:48], BIGF)
        dv_lrp = C["dv"]
        em("sync", "dma_start", out=A(LREC, 0, [[192, 128], [1, 192]]),
           in_=recs[:, 0:192], waits=[("dv", dv_lrp)], incs=["zf"])
        em("sync", "dma_start", out=A(LREC, 128 * 192, [[48, 128], [1, 48]]),
           in_=didx[:, 0:48], incs=["zf"])
        zf_total = C["zf"]

        em("gpsimd", "dma_start", out=W1a.ap().rearrange("p (r d2) -> p r d2", r=KD),
           in_=A(Wv, 0, [[D, 128], [128 * D, KD], [1, D]]), incs=["cvo"])
        cvo_wv = C["cvo"]
        em("gpsimd", "dma_start", out=W1b.ap().rearrange("p (r d2) -> p r d2", r=KD),
           in_=A(Wo, 0, [[D, 128], [128 * D, KD], [1, D]]), incs=["cvo"])
        cvo_wo = C["cvo"]


        em("vector", "memset", eps_c[:, :], 1e-5)
        em("vector", "memset", mh_c[:, :], -0.5)
        em("scalar", "activation", out=idb[:, :], in_=idf[:, :], func=AF.Copy,
           waits=[("ld", ld_consts)], incs=["ac"])
        ac_idb = C["ac"]
        em("scalar", "activation", out=wg_b[:, :], in_=wg_f[:, :], func=AF.Copy, incs=["ac"])
        ac_wg = C["ac"]

        for e in range(E):
            em("tensor", "transpose", out=ps_b0[:, e * 16:(e + 1) * 16],
               in_=b_raw[0:16, e * 128:(e + 1) * 128], identity=idf[0:16, 0:16],
               waits=([("ld", ld_consts), ("ac", ac_idb)] if e == 0 else []),
               incs=(["pe"] if e == E - 1 else []))
        pe_b1t = C["pe"]
        em("vector", "tensor_copy", out=b1_sb[:, :], in_=ps_b0[:, 0:128],
           waits=[("pe", pe_b1t)])
        dv_b1 = C["dv"]
        em("sync", "dma_start", out=A(B1T, 0, [[E * KH, 128], [1, E * KH]]),
           in_=b1_sb[:, :], waits=[("dv", dv_b1)], incs=["xs"])

        dv_trbank = 0
        ac_trbank = 0
        tr_last = {0: ("ac", 0), 1: ("dv", 0)}
        pe_wvT = [0]
        pe_woT = [0]

        def emit_wvwo():
            nonlocal dv_trbank
            for c in range(KD):
                for r in range(KD):
                    wt = []
                    if c == 0 and r == 0:
                        wt = [("cvo", cvo_wo), ("ac", ac_idb), tr_last[0], tr_last[1]]
                    if r == 0 and dv_trbank:
                        wt.append(("dv", dv_trbank))
                    em("tensor", "transpose", out=ps_tr[:, r * 128:(r + 1) * 128],
                       in_=WvN[:, r, c, :], identity=idb[:, :], waits=wt,
                       incs=(["pe"] if r == KD - 1 else []))
                em("vector", "tensor_copy", out=WvT[:, c, :, :], in_=ps_tr_v[:, :, :],
                   waits=[("pe", C["pe"])])
                dv_trbank = C["dv"]
            pe_wvT[0] = C["pe"]
            for c in range(KD):
                for r in range(KD):
                    wt = []
                    if c == 0 and r == 0:
                        wt = [("cvo", cvo_wo)]
                    if r == 0:
                        wt.append(("dv", dv_trbank))
                    em("tensor", "transpose", out=ps_tr[:, r * 128:(r + 1) * 128],
                       in_=WoN[:, r, c, :], identity=idb[:, :], waits=wt,
                       incs=(["pe"] if r == KD - 1 else []))
                em("vector", "tensor_copy", out=WoT[:, c, :, :], in_=ps_tr_v[:, :, :],
                   waits=[("pe", C["pe"])])
                dv_trbank = C["dv"]
            pe_woT[0] = C["pe"]

        pe_xnb_free = {0: 0, 1: 0}
        dv_xfree = {0: 0, 1: 0}
        ac_xfree = {0: 0, 1: 0}
        xs_xnb = {0: 0, 1: 0}
        dv_lgbank = {0: dv_b1, 1: 0}
        pe_lg_done = {}
        dv_m = 0
        dv_xncol = 0

        def emit_softmax(t):
            par = (TILE_ORDER.index(t)) % 2
            pbank = ps_b0 if par == 0 else ps_b1
            em("vector", "tensor_copy", out=lg[:, :], in_=pbank[:, 0:NG * E],
               waits=[("pe", pe_lg_done[t])])
            dv_lgbank[par] = C["dv"]
            for g in range(NG):
                em("vector", "reduce_max", out=cols[:, 28 + g:29 + g],
                   in_=lg[:, g * E:(g + 1) * E], axis=AX.X)
            for g in range(NG):
                em("vector", "tensor_scalar", out=sme[:, g * E:(g + 1) * E],
                   in0=lg[:, g * E:(g + 1) * E], scalar1=cols[:, 28 + g:29 + g],
                   scalar2=None, op0=ALU.subtract)
            dvp = C["dv"]
            em("scalar", "activation", out=sme[:, :], in_=sme[:, :], func=AF.Exp,
               waits=[("dv", dvp)], incs=["ac"])
            acp = C["ac"]
            for g in range(NG):
                em("vector", "reduce_sum", out=cols[:, 28 + g:29 + g],
                   in_=sme[:, g * E:(g + 1) * E], axis=AX.X,
                   waits=([("ac", acp)] if g == 0 else []))
            em("vector", "reciprocal", out=cols[:, 28:32], in_=cols[:, 28:32])
            for g in range(NG):
                em("vector", "tensor_scalar", out=sme[:, g * E:(g + 1) * E],
                   in0=sme[:, g * E:(g + 1) * E], scalar1=cols[:, 28 + g:29 + g],
                   scalar2=None, op0=ALU.mult)
            for g in range(NG):
                em("vector", "max", out=m8[:, g * E:(g + 1) * E],
                   in_=sme[:, g * E:(g + 1) * E])
            for g in range(NG):
                em("vector", "tensor_scalar", out=w_t[:, g * E:(g + 1) * E],
                   in0=sme[:, g * E:(g + 1) * E], scalar1=m8[:, g * E + 1:g * E + 2],
                   scalar2=None, op0=ALU.is_ge)
            em("vector", "tensor_tensor", out=w_t[:, :], in0=w_t[:, :], in1=sme[:, :],
               op=ALU.mult)
            em("vector", "tensor_copy", out=w_all_v[:, :, t * NG:(t + 1) * NG],
               in_=w_t.ap().rearrange("p (g e) -> p e g", g=NG))

        sd_tile = {}
        sdsem = lambda p: "sd" if p == 0 else "sd1"
        for ti, t in enumerate(TILE_ORDER):
            par = ti % 2
            base = t * TILE
            co = par * 32
            if ti == 0:
                em("scalar", "dma_start", out=x_tv[:, par, :, :],
                   in_=A(x_in, base * D, [[D, 128], [128 * D, NG], [1, D]]),
                   incs=[sdsem(par)])
                sd_tile[0] = (sdsem(par), C[sdsem(par)])
            sd_x = sd_tile[ti]
            xsl = x_t[:, par * NG * D:(par + 1) * NG * D]
            em("vector", "reduce_sum", out=cols[:, co:co + 4],
               in_=x_tv[:, par, :, :], axis=AX.X, waits=[sd_x])
            dv_xfree[par] = C["dv"]
            if ti + 1 < NT:
                t2 = TILE_ORDER[ti + 1]
                p2 = (ti + 1) % 2
                em("scalar", "dma_start", out=x_tv[:, p2, :, :],
                   in_=A(x_in, t2 * TILE * D, [[D, 128], [128 * D, NG], [1, D]]),
                   waits=([("dv", dv_xfree[p2]), ("ac", ac_xfree[p2])] if ti >= 1 else []),
                   incs=[sdsem(p2)])
                sd_tile[ti + 1] = (sdsem(p2), C[sdsem(p2)])
            for g in range(NG):
                wt = [sd_x] if g == 0 else [("ac", C["ac"])]
                em("scalar", "activation", out=g_grp[:, :],
                   in_=xsl[:, g * D:(g + 1) * D],
                   func=AF.Square, accum_out=cols[:, co + 4 + g:co + 5 + g],
                   waits=wt, incs=["ac"])
            ac_ssq = C["ac"]
            em("vector", "tensor_scalar", out=cols[:, co + 8:co + 12],
               in0=cols[:, co:co + 4], scalar1=1.0 / D, scalar2=None, op0=ALU.mult)
            dv_mean = C["dv"]
            em("vector", "tensor_tensor", out=cols[:, co + 12:co + 16],
               in0=cols[:, co + 8:co + 12], in1=cols[:, co + 8:co + 12], op=ALU.mult)
            em("vector", "scalar_tensor_tensor", out=cols[:, co + 16:co + 20],
               in0=cols[:, co + 4:co + 8], scalar=1.0 / D, in1=cols[:, co + 12:co + 16],
               op0=ALU.mult, op1=ALU.subtract, waits=[("ac", ac_ssq)])
            dv_var = C["dv"]
            em("scalar", "activation", out=cols[:, co + 16:co + 20],
               in_=cols[:, co + 16:co + 20], func=AF.Ln,
               bias=eps_c[:, 0:1], waits=[("dv", dv_var)], incs=["ac"])
            em("scalar", "activation", out=cols[:, co + 20:co + 24],
               in_=cols[:, co + 16:co + 20], func=AF.Exp,
               scale=mh_c[:, 0:1], incs=["ac"])
            ac_rstd = C["ac"]
            em("vector", "scalar_tensor_tensor", out=cols[:, co + 24:co + 28],
               in0=cols[:, co + 8:co + 12], scalar=-1.0, in1=cols[:, co + 20:co + 24],
               op0=ALU.mult, op1=ALU.mult, waits=[("ac", ac_rstd)])
            dv_nmr = C["dv"]
            em("vector", "tensor_copy", out=mr_all[:, t * NG:(t + 1) * NG],
               in_=cols[:, co + 8:co + 12])
            em("vector", "tensor_copy", out=mr_all[:, GC + t * NG:GC + (t + 1) * NG],
               in_=cols[:, co + 20:co + 24])
            ac_xnb = {}
            lbank = ps_b0 if par == 0 else ps_b1
            for g in range(NG):
                wt = [("dv", dv_nmr)] if g == 0 else []
                if g == 0:
                    if pe_xnb_free[par]:
                        wt.append(("pe", pe_xnb_free[par]))
                    if xs_xnb[par]:
                        wt.append(xs_xnb[par])
                em("scalar", "activation",
                   out=xnb[:, (par * NG + g) * D:(par * NG + g + 1) * D],
                   in_=xsl[:, g * D:(g + 1) * D], func=AF.Identity,
                   bias=cols[:, co + 24 + g:co + 25 + g],
                   scale=cols[:, co + 20 + g:co + 21 + g],
                   waits=wt, incs=["ac"])
                ac_xnb[g] = C["ac"]
                for c in range(KD):
                    wt = []
                    if c == 0:
                        wt = [("ac", ac_xnb[g]), tr_last[0], tr_last[1]]
                        if dv_trbank:
                            wt.append(("dv", dv_trbank))
                    em("tensor", "transpose", out=ps_tr[:, c * 128:(c + 1) * 128],
                       in_=xnb[:, (par * NG + g) * D + c * 128:(par * NG + g) * D + (c + 1) * 128],
                       identity=idb[:, :], waits=wt, incs=(["pe"] if c == KD - 1 else []))
                pe_xnb_free[par] = C["pe"]
                em("scalar", "activation", out=xnT_v[:, :, g * 128:(g + 1) * 128],
                   in_=ps_tr_v[:, :, :], func=AF.Copy,
                   waits=[("pe", C["pe"])], incs=["ac"])
                trb = ("ac", C["ac"])
                ac_trbank = C["ac"]
                tr_last[g % 2] = trb
                for k in range(KD):
                    wt = []
                    if k == 0:
                        wt = [trb]
                        if g == 0 and dv_lgbank[par]:
                            wt.append(("dv", dv_lgbank[par]))
                    em("tensor", "matmul", out=lbank[:, g * E:(g + 1) * E],
                       lhsT=xnT[:, k * TILE + g * 128:k * TILE + (g + 1) * 128],
                       rhs=wg_b[:, k * E:(k + 1) * E], start=(k == 0), stop=(k == KD - 1),
                       waits=wt, incs=(["pe"] if k == KD - 1 else []))
            pe_lg_done[t] = C["pe"]
            ac_xfree[par] = C["ac"]
            xssem = "xs" if par == 0 else "xs1"
            em("sync", "dma_start",
               out=A(XNBF, base * D, [[D, 128], [128 * D, NG], [1, D]]),
               in_=xnb.ap().rearrange("p (s g d) -> p s g d", s=2, g=NG)[:, par],
               waits=[("ac", ac_xnb[NG - 1])], incs=[xssem])
            xs_xnb[par] = (xssem, C[xssem])
            if t == 7:
                em("vector", "tensor_copy", out=xn_col[:, :],
                   in_=xnT_v[:, :, SPECIAL_COL:SPECIAL_COL + 1],
                   waits=[tr_last[0], tr_last[1]])
                dv_xncol = C["dv"]
            if ti > 0:
                emit_softmax(TILE_ORDER[ti - 1])
            if ti == 1:
                emit_wvwo()
        emit_softmax(TILE_ORDER[-1])
        dv_wall = C["dv"]

        for m in range(KD):
            for k in range(KD):
                wt = [("dv", dv_trbank), ("dv", dv_m), ("dv", dv_xncol)] \
                    if (m == 0 and k == 0) else []
                em("tensor", "matmul", out=ps_m[:, 40 + m:41 + m],
                   lhsT=WvT[:, k, m, :], rhs=xn_col[:, k:k + 1],
                   start=(k == 0), stop=(k == KD - 1), waits=wt,
                   incs=(["pe"] if (m == KD - 1 and k == KD - 1) else []))
        pe_u = C["pe"]
        em("scalar", "activation", out=u_bf[:, :], in_=ps_m[:, 40:48], func=AF.Copy,
           waits=[("pe", pe_u)], incs=["ac"])
        ac_u = C["ac"]
        for m in range(KD):
            for k in range(KD):
                wt = [("ac", ac_u)] if (m == 0 and k == 0) else []
                em("tensor", "matmul", out=ps_m[:, 48 + m:49 + m],
                   lhsT=WoT[:, k, m, :], rhs=u_bf[:, k:k + 1],
                   start=(k == 0), stop=(k == KD - 1), waits=wt,
                   incs=(["pe"] if (m == KD - 1 and k == KD - 1) else []))
        pe_t = C["pe"]
        em("scalar", "activation", out=t_f[:, :], in_=ps_m[:, 48:56], func=AF.Copy,
           waits=[("pe", pe_t)], incs=["ac"])
        ac_tf = C["ac"]
        em("tensor", "transpose", out=ps_b0[0:8, 384:512], in_=t_f[:, :],
           identity=idf[:, :], waits=[("ac", ac_tf), ("dv", dv_lgbank[0]), ("dv", dv_lgbank[1])],
           incs=["pe"])
        em("vector", "tensor_copy", out=t8[:, :], in_=ps_b0[0:8, 384:512],
           waits=[("pe", C["pe"])])
        dv_t8 = C["dv"]
        for m in range(KD):
            pbank = ps_b0 if m < 4 else ps_b1
            wt = [("dv", dv_t8), ("ld", ld_consts)] if m in (0, 4) else []
            em("tensor", "matmul", out=pbank[:, (m % 4) * 128:(m % 4 + 1) * 128],
               lhsT=oh_sb[0:8, m * 128:(m + 1) * 128], rhs=t8[:, :],
               start=True, stop=True, waits=wt, incs=(["pe"] if m in (3, 7) else []))
            if m == 3:
                pe_bc0 = C["pe"]
        pe_bc1 = C["pe"]
        em("vector", "tensor_copy", out=scr1[:, 0:512], in_=ps_b0[:, 0:512],
           waits=[("pe", pe_bc0)])
        em("vector", "tensor_copy", out=scr1[:, 512:1024], in_=ps_b1[:, 0:512],
           waits=[("pe", pe_bc1)])
        dv_trowbc = C["dv"]
        em("scalar", "dma_start", out=g_grp[:, :],
           in_=A(x_in, (7 * TILE + SPECIAL_GROUP * 128) * D, [[D, 128], [1, D]]),
           waits=[("ac", C["ac"])], incs=["sd"])
        sd_sg = C["sd"]
        gcs = 7 * NG + SPECIAL_GROUP
        em("vector", "tensor_scalar", out=g_grp[:, :], in0=g_grp[:, :],
           scalar1=mr_all[:, gcs:gcs + 1], scalar2=mr_all[:, GC + gcs:GC + gcs + 1],
           op0=ALU.subtract, op1=ALU.mult, waits=[("sd", sd_sg)])
        em("vector", "tensor_sub", out=s_grp[:, :], in0=scr1[:, :], in1=g_grp[:, :],
           waits=[("dv", dv_trowbc)])
        em("vector", "tensor_scalar", out=s_grp[:, :], in0=s_grp[:, :],
           scalar1=tm_sb[:, 0:1], scalar2=None, op0=ALU.mult)
        em("vector", "tensor_add", out=s_grp[:, :], in0=s_grp[:, :], in1=g_grp[:, :])
        dv_sgrp = C["dv"]

        em("vector", "tensor_scalar", out=sel[:, :], in0=w_all[:, :],
           scalar1=0.0, scalar2=None, op0=ALU.is_gt, waits=[("dv", dv_wall)])
        em("vector", "tensor_tensor_scan", out=incl[:, :], data0=segm_sb[:, :],
           data1=sel[:, :], initial=0.0, op0=ALU.mult, op1=ALU.add,
           waits=[("ld", ld_consts)])
        em("vector", "tensor_copy", out=cnt[:, :], in_=incl_v[:, :, GC - 1:GC])
        dv_cnt = C["dv"]
        em("tensor", "matmul", out=ps_m[:, 64:72], lhsT=ltri_sb[:, :], rhs=cnt[:, :],
           start=True, stop=True, waits=[("dv", dv_cnt)], incs=["pe"])
        em("vector", "tensor_copy", out=base_pe[:, :], in_=ps_m[:, 64:72],
           waits=[("pe", C["pe"])])
        em("tensor", "matmul", out=ps_m[0:1, 72:80], lhsT=o128[:, :], rhs=cnt[:, :],
           start=True, stop=True, waits=[("dv", C["dv"])], incs=["pe"])
        em("vector", "tensor_copy", out=tot[:, :], in_=ps_m[0:1, 72:80],
           waits=[("pe", C["pe"])])
        em("vector", "tensor_scalar", out=tr8[:, 0:E], in0=tot[:, :],
           scalar1=float(T3 - 1), scalar2=None, op0=ALU.add)
        em("vector", "tensor_copy", out=tr8i[:, 0:E], in_=tr8[:, 0:E])
        em("vector", "tensor_single_scalar", out=tr8i[:, E:2 * E], in_=tr8i[:, 0:E],
           scalar=T3 - 1, op=ALU.bitwise_and)
        em("vector", "tensor_copy", out=tr8[:, E:2 * E], in_=tr8i[:, E:2 * E])
        em("vector", "tensor_sub", out=tr8[:, 0:E], in0=tr8[:, 0:E], in1=tr8[:, E:2 * E])
        em("vector", "tensor_scalar", out=tr8[:, 0:E], in0=tr8[:, 0:E],
           scalar1=float(T3), scalar2=None, op0=ALU.max)
        em("vector", "memset", tr8[:, 2 * E:3 * E], 1.0)
        em("vector", "tensor_tensor_scan", out=tr8[:, E:2 * E], data0=tr8[:, 2 * E:3 * E],
           data1=tr8[:, 0:E], initial=0.0, op0=ALU.mult, op1=ALU.add)
        em("vector", "tensor_sub", out=tr8[:, 2 * E:3 * E], in0=tr8[:, E:2 * E],
           in1=tr8[:, 0:E])
        dv_seg = C["dv"]
        em("vector", "memset", etile[:, 0:NT3], 0.0)
        for e in range(E):
            em("vector", "tensor_scalar", out=etile[:, NT3:2 * NT3], in0=jtc_sb[:, :],
               scalar1=tr8[:, E + e:E + e + 1], scalar2=None, op0=ALU.is_ge)
            em("vector", "tensor_add", out=etile[:, 0:NT3], in0=etile[:, 0:NT3],
               in1=etile[:, NT3:2 * NT3])
        dv_etile = C["dv"]
        em("vector", "memset", etile[:, NT3:NT3 + 1], -1.0)
        em("vector", "tensor_copy", out=etile[:, NT3 + 1:2 * NT3], in_=etile[:, 0:NT3 - 1])
        em("vector", "tensor_tensor", out=etile[:, 2 * NT3:3 * NT3], in0=etile[:, 0:NT3],
           in1=etile[:, NT3:2 * NT3], op=ALU.not_equal)
        em("vector", "tensor_scalar", out=etile[:, 3 * NT3:4 * NT3], in0=etile[:, 0:NT3],
           scalar1=float(E - 1), scalar2=None, op0=ALU.is_le)
        em("vector", "tensor_tensor", out=etile[:, 2 * NT3:3 * NT3],
           in0=etile[:, 2 * NT3:3 * NT3], in1=etile[:, 3 * NT3:4 * NT3], op=ALU.mult)
        dv_diff = C["dv"]
        em("tensor", "matmul", out=ps_m[:, 80:80 + NT3], lhsT=o1r[:, :],
           rhs=etile[0:1, 0:NT3], start=True, stop=True,
           waits=[("dv", dv_etile)], incs=["pe"])
        em("vector", "tensor_copy", out=ebf[:, :], in_=ps_m[:, 80:80 + NT3],
           waits=[("pe", C["pe"])])
        dv_ebf = C["dv"]
        em("tensor", "matmul", out=ps_m[:, 80 + NT3:80 + 2 * NT3], lhsT=o1r[:, :],
           rhs=etile[0:1, 2 * NT3:3 * NT3], start=True, stop=True,
           waits=[("dv", dv_diff), ("dv", dv_ebf)], incs=["pe"])
        em("vector", "tensor_copy", out=dbf[:, :], in_=ps_m[:, 80 + NT3:80 + 2 * NT3],
           waits=[("pe", C["pe"])])
        dv_dbf = C["dv"]
        em("vector", "memset", etile[:, NT3:NT3 + 2], -1.0)
        em("vector", "tensor_copy", out=etile[:, NT3 + 2:2 * NT3], in_=etile[:, 0:NT3 - 2])
        em("vector", "tensor_tensor", out=etile[:, 2 * NT3:3 * NT3], in0=etile[:, 0:NT3],
           in1=etile[:, NT3:2 * NT3], op=ALU.not_equal)
        em("vector", "tensor_tensor", out=etile[:, 2 * NT3:3 * NT3],
           in0=etile[:, 2 * NT3:3 * NT3], in1=etile[:, 3 * NT3:4 * NT3], op=ALU.mult)
        dv_diff2 = C["dv"]
        em("tensor", "matmul", out=ps_m[:, 80 + NT3:80 + 2 * NT3], lhsT=o1r[:, :],
           rhs=etile[0:1, 2 * NT3:3 * NT3], start=True, stop=True,
           waits=[("dv", dv_diff2), ("dv", dv_dbf)], incs=["pe"])
        em("vector", "tensor_copy", out=dbf2[:, :], in_=ps_m[:, 80 + NT3:80 + 2 * NT3],
           waits=[("pe", C["pe"])])
        em("vector", "tensor_scalar", out=wrow[:, :], in0=ebf[:, :],
           scalar1=float(E - 1), scalar2=256.0, op0=ALU.min, op1=ALU.mult)
        em("vector", "tensor_scalar", out=wrow[:, :], in0=wrow[:, :],
           scalar1=pbase_sb[:, 2:3], scalar2=None, op0=ALU.add)
        for half in range(2):
            em("vector", "tensor_scalar", out=idxw[:, half * NT3:(half + 1) * NT3],
               in0=wrow[:, :], scalar1=float(half * 128) - BIGF, scalar2=None, op0=ALU.add)
            em("vector", "tensor_tensor", out=idxw[:, half * NT3:(half + 1) * NT3],
               in0=idxw[:, half * NT3:(half + 1) * NT3], in1=dbf2[:, :], op=ALU.mult)
            em("vector", "tensor_scalar", out=idxw[:, half * NT3:(half + 1) * NT3],
               in0=idxw[:, half * NT3:(half + 1) * NT3], scalar1=BIGF, scalar2=None,
               op0=ALU.add)
        em("vector", "tensor_copy", out=idxw_i[:, :], in_=idxw[:, :])
        em("vector", "tensor_scalar", out=wrow[:, :], in0=ebf[:, :],
           scalar1=float(E - 1), scalar2=128.0, op0=ALU.min, op1=ALU.mult)
        em("vector", "tensor_scalar", out=wrow[:, :], in0=wrow[:, :],
           scalar1=pbase_sb[:, 2:3], scalar2=2.0, op0=ALU.add, op1=ALU.mult)
        for half in range(2):
            em("vector", "tensor_scalar", out=idxw2[:, half * NT3:(half + 1) * NT3],
               in0=wrow[:, :], scalar1=float(half) - BIGF, scalar2=None, op0=ALU.add)
            em("vector", "tensor_tensor", out=idxw2[:, half * NT3:(half + 1) * NT3],
               in0=idxw2[:, half * NT3:(half + 1) * NT3], in1=dbf[:, :], op=ALU.mult)
            em("vector", "tensor_scalar", out=idxw2[:, half * NT3:(half + 1) * NT3],
               in0=idxw2[:, half * NT3:(half + 1) * NT3], scalar1=BIGF, scalar2=None,
               op0=ALU.add)
        em("vector", "tensor_copy", out=idxw2_i[:, :], in_=idxw2[:, :])
        dv_idxw = C["dv"]
        em("vector", "tensor_scalar", out=idxb[:, :], in0=ebf[:, :],
           scalar1=float(E - 1), scalar2=pbase_sb[:, 1:2], op0=ALU.min, op1=ALU.add)
        em("vector", "tensor_copy", out=idxb_i[:, :], in_=idxb[:, :])
        em("vector", "tensor_scalar", out=wrow[:, :], in0=ebf[:, :],
           scalar1=float(E - 1), scalar2=None, op0=ALU.min)
        em("vector", "tensor_copy", out=idxb2_i[:, :], in_=wrow[:, :])
        dv_idxb = C["dv"]
        em("tensor", "matmul", out=ps_m[:, 160:168], lhsT=o1r[:, :],
           rhs=tr8[0:1, 2 * E:3 * E], start=True, stop=True,
           waits=[("dv", dv_seg), ("dv", dv_idxb)], incs=["pe"])
        em("vector", "tensor_copy", out=gbase[:, :], in_=ps_m[:, 160:168],
           waits=[("pe", C["pe"])])
        em("vector", "tensor_add", out=gbase[:, :], in0=gbase[:, :], in1=base_pe[:, :])
        em("vector", "tensor_sub", out=didx[:, :], in0=incl[:, :], in1=sel[:, :],
           waits=[("zf", zf_total)])
        for e in range(E):
            em("vector", "tensor_scalar", out=didx[:, e * GC:(e + 1) * GC],
               in0=didx[:, e * GC:(e + 1) * GC], scalar1=gbase[:, e:e + 1],
               scalar2=None, op0=ALU.add)
        em("vector", "tensor_scalar", out=didx[:, :], in0=didx[:, :],
           scalar1=BIGF, scalar2=None, op0=ALU.subtract)
        em("vector", "tensor_tensor", out=didx[:, :], in0=didx[:, :], in1=sel[:, :],
           op=ALU.mult)
        em("vector", "tensor_scalar", out=didx[:, :], in0=didx[:, :],
           scalar1=BIGF, scalar2=None, op0=ALU.add)
        em("vector", "memset", runsb[:, :], 0.0)
        for e in range(E):
            em("vector", "tensor_copy", out=slot[:, e * GC:(e + 1) * GC], in_=runsb[:, :])
            if e < E - 1:
                em("vector", "tensor_add", out=runsb[:, :], in0=runsb[:, :],
                   in1=sel[:, e * GC:(e + 1) * GC])
        em("vector", "tensor_scalar", out=mm1[:, :], in0=slot[:, :],
           scalar1=1.0, scalar2=None, op0=ALU.min)
        em("vector", "tensor_scalar", out=mm1[:, :], in0=mm1[:, :],
           scalar1=-1.0, scalar2=1.0, op0=ALU.mult, op1=ALU.add)
        em("vector", "tensor_tensor", out=mm1[:, :], in0=mm1[:, :], in1=sel[:, :],
           op=ALU.mult)
        em("vector", "tensor_sub", out=mm2[:, :], in0=sel[:, :], in1=mm1[:, :])
        first_accs = True
        for r, mk in ((0, mm1), (1, mm2)):
            em("vector", "tensor_tensor", out=acc_s[:, :], in0=didx[:, 0:GC],
               in1=mk[:, 0:GC], op=ALU.mult,
               waits=([("zf", zf_total)] if first_accs else []))
            first_accs = False
            em("vector", "tensor_copy", out=didx_r[:, r * GC:(r + 1) * GC], in_=acc_s[:, :])
            for e in range(1, E):
                em("vector", "tensor_tensor", out=acc_s[:, :],
                   in0=didx[:, e * GC:(e + 1) * GC], in1=mk[:, e * GC:(e + 1) * GC],
                   op=ALU.mult)
                em("vector", "tensor_add", out=didx_r[:, r * GC:(r + 1) * GC],
                   in0=didx_r[:, r * GC:(r + 1) * GC], in1=acc_s[:, :])
            em("vector", "tensor_tensor", out=acc_s[:, :], in0=w_all[:, 0:GC],
               in1=mk[:, 0:GC], op=ALU.mult)
            em("vector", "tensor_copy", out=w_r[:, r * GC:(r + 1) * GC], in_=acc_s[:, :])
            for e in range(1, E):
                em("vector", "tensor_tensor", out=acc_s[:, :],
                   in0=w_all[:, e * GC:(e + 1) * GC], in1=mk[:, e * GC:(e + 1) * GC],
                   op=ALU.mult)
                em("vector", "tensor_add", out=w_r[:, r * GC:(r + 1) * GC],
                   in0=w_r[:, r * GC:(r + 1) * GC], in1=acc_s[:, :])
        em("vector", "tensor_copy", out=didx_ri[:, :], in_=didx_r[:, :])
        em("vector", "tensor_scalar", out=dstc[:, 0:GC], in0=toks_sb[:, :],
           scalar1=2.0, scalar2=None, op0=ALU.mult)
        em("vector", "tensor_scalar", out=dstc[:, GC:2 * GC], in0=dstc[:, 0:GC],
           scalar1=1.0, scalar2=None, op0=ALU.add)
        for r in range(2):
            em("vector", "tensor_copy", out=recs_v[:, r, :, 0], in_=toks_sb[:, :])
            em("vector", "tensor_copy", out=recs_v[:, r, :, 1],
               in_=w_r[:, r * GC:(r + 1) * GC])
            em("vector", "tensor_copy", out=recs_v[:, r, :, 2],
               in_=dstc[:, r * GC:(r + 1) * GC])
        dv_recs = C["dv"]
        for r in range(2):
            for gc in range(GC):
                em("gpsimd", "indirect_dma_start",
                   LREC.ap(),
                   IndirectOffsetOnAxis(didx_ri[:, r * GC + gc:r * GC + gc + 1], 0),
                   recs[:, (r * GC + gc) * 3:(r * GC + gc) * 3 + 3], None,
                   bounds_check=NR - 1, oob_is_err=False,
                   waits=([("dv", dv_recs), ("zf", zf_total)] if (r == 0 and gc == 0) else []),
                   incs=["sct"])
        sct_total = C["sct"]

        ac_hbank = {0: 0, 1: 0}
        h_done = {}
        dv_ybank = {0: 0, 1: 0, 2: dv_trowbc, 3: dv_trowbc}
        dv_b2bc = {0: 0, 1: 0}
        pe_free = {"w1a0": pe_wvT[0], "w1b0": pe_woT[0], "w1a1": 0, "w1b1": 0,
                   "w2a": pe_u, "w2b": pe_t}
        b1_ac = {0: 0, 1: 0}
        pe_gx = {0: 0, 1: 0}
        gx_done = {0: 0, 1: 0}
        rc_done = {0: 0, 1: 0, 2: 0, 3: 0}
        dv_rec_slot = {0: 0, 1: 0, 2: 0, 3: 0}
        dv_gxT = dv_trbank
        pe_b1use = 0
        pe_b2use = 0
        dv_zt = 0

        def emit_prefetch_a(j):
            sl = j % 4
            rcsem = ("rc0", "rc1", "rc2", "rc3")[sl]
            em("sync", "dma_start",
               out=rec_sb_v[:, sl, :, :],
               in_=A(LREC, j * T3 * 3, [[3, 128], [128 * 3, 2], [1, 3]]),
               waits=([("sct", sct_total)] if j == 0 else [("dv", dv_rec_slot[sl])]),
               incs=[rcsem])
            rc_done[sl] = C[rcsem]

        def emit_prefetch_b(j):
            sl = j % 4
            sl2 = j % 2
            gxsem = "gx0" if sl2 == 0 else "gx1"
            wt0 = [(("rc0", "rc1", "rc2", "rc3")[sl], rc_done[sl])]
            if j >= 4:
                wt0.append(zs_done[j - 4])
            if j >= 2:
                wt0.append((gxsem, gx_done[sl2]))
            em("vector", "tensor_copy", out=tok_i_v[:, sl, :], in_=rec_sb_v[:, sl, :, 0],
               waits=wt0)
            em("vector", "tensor_copy", out=w_f_v[:, sl, :], in_=rec_sb_v[:, sl, :, 1])
            em("vector", "tensor_copy", out=dst_i_v[:, sl, :], in_=rec_sb_v[:, sl, :, 2])
            dv_rec_slot[sl] = C["dv"]
            dv_rec = C["dv"]
            for h in range(2):
                wt = []
                if h == 0:
                    wt = [("dv", dv_rec), ("xs", C["xs"]), ("xs1", C["xs1"])]
                    if pe_gx[sl2]:
                        wt.append(("pe", pe_gx[sl2]))
                em("gpsimd", "indirect_dma_start",
                   gx_v[:, sl2, h * D:(h + 1) * D], None,
                   XNBF.ap(), IndirectOffsetOnAxis(tok_i_v[:, sl, h:h + 1], 0),
                   bounds_check=SH - 1, oob_is_err=False,
                   waits=wt, incs=[gxsem])
            gx_done[sl2] = C[gxsem]

        def emit_b1(j):
            slw = j % 2
            em("gpsimd", "indirect_dma_start",
               b1_t_v[:, slw, :], None,
               B1T.ap(), IndirectOffsetOnAxis(idxb_i[:, j:j + 1], 0),
               bounds_check=128 * E - 1, oob_is_err=False,
               waits=[("dv", dv_idxb), ("xs", C["xs"]), ("xs1", C["xs1"]),
                      ("ac", b1_ac[slw]),
                      ("b1g", 16 * j)],
               incs=["b1g"])

        def emit_weights_w1(j):
            slw = j % 2
            sfx = "" if slw == 0 else "1"
            for wsem, buf, ihalf in (
                    ("w1a" + sfx, (W1a, W1a1)[slw], 0), ("w1b" + sfx, (W1b, W1b1)[slw], 1)):
                wt = [("dv", dv_idxw), ("pe", pe_free[("w1a" if ihalf == 0 else "w1b") + str(slw)])]
                em("gpsimd", "indirect_dma_start",
                   buf[:, :], None,
                   W1.ap(), IndirectOffsetOnAxis(idxw_i[:, ihalf * NT3 + j:ihalf * NT3 + j + 1], 0),
                   bounds_check=E * 128 * 2 - 1, oob_is_err=False,
                   waits=wt, incs=[wsem])

        def emit_weights_w2(j):
            slw = j % 2
            em("gpsimd", "indirect_dma_start",
               b2bc[:, slw * D:(slw + 1) * D], None,
               b2.ap(), IndirectOffsetOnAxis(idxb2_i[:, j:j + 1], 0),
               bounds_check=E - 1, oob_is_err=False,
               waits=[("dv", dv_b2bc[slw]), ("b2g", 16 * j)] +
                     ([("pe", pe_bc1)] if j < 2 else []),
               incs=["b2g"])
            for wsem, buf, ihalf in (("w2a", W2a, 0), ("w2b", W2b, 1)):
                wt = [("dv", dv_idxw), ("pe", pe_free[wsem])]
                em("gpsimd", "indirect_dma_start",
                   buf[:, :], None,
                   W2.ap(), IndirectOffsetOnAxis(idxw2_i[:, ihalf * NT3 + j:ihalf * NT3 + j + 1], 0),
                   bounds_check=E * 128 * 2 - 1, oob_is_err=False,
                   waits=wt, incs=[wsem])

        def emit_gxtr(j):
            nonlocal dv_gxT
            sl = j % 2
            gxsem = "gx0" if sl == 0 else "gx1"
            gx_j = gx_done[sl]
            for h in range(2):
                for c in range(KD):
                    wt = []
                    if c == 0:
                        wt = [(gxsem, gx_j), ("dv", dv_gxT)] if h == 0 else [("dv", dv_gxT)]
                    em("tensor", "transpose", out=ps_tr[:, c * 128:(c + 1) * 128],
                       in_=gx_v[:, sl, h * D + c * 128:h * D + (c + 1) * 128],
                       identity=idb[:, :], waits=wt, incs=(["pe"] if c == KD - 1 else []))
                if h == 1:
                    pe_gx[sl] = C["pe"]
                em("vector", "tensor_copy", out=gxT_v[:, :, h * 128:(h + 1) * 128],
                   in_=ps_tr_v[:, :, :], waits=[("pe", C["pe"])])
                dv_gxT = C["dv"]

        zs_done = {}
        emit_prefetch_a(0)
        emit_prefetch_b(0)
        emit_prefetch_a(1)
        emit_prefetch_b(1)
        emit_b1(0)
        emit_weights_w1(0)
        emit_weights_w1(1)
        emit_weights_w2(0)
        emit_gxtr(0)
        for j in range(NT3):
            sl = j % 2
            gxsem = "gx0" if sl == 0 else "gx1"
            if j + 2 < NT3:
                emit_prefetch_a(j + 2)
            if j >= 1:
                emit_weights_w2(j)
            b1g_j = 16 * (j + 1)
            b2g_j = 16 * (j + 1)
            for m in range(KH):
                bank = m % 2
                wt = []
                if m == 0:
                    wt += [("w1a" + ("" if j % 2 == 0 else "1"), 16 * (j // 2 + 1)),
                           ("dv", dv_gxT), ("b1g", b1g_j)]
                if m == KD:
                    wt.append(("w1b" + ("" if j % 2 == 0 else "1"), 16 * (j // 2 + 1)))
                if ac_hbank[bank] != 0:
                    wt.append(ac_hbank[bank])
                for k in range(KD):
                    em("tensor", "matmul", out=ps_h[bank][:, 0:T3],
                       lhsT=W1vp[j % 2][m // KD][:, k, (m % KD) * 128:(m % KD + 1) * 128],
                       rhs=gxT_v[:, k, :],
                       start=(k == 0), stop=(k == KD - 1),
                       waits=(wt if k == 0 else []),
                       incs=(["pe"] if k == KD - 1 else []))
                if m == KD - 1:
                    pe_free["w1a" + str(j % 2)] = C["pe"]
                if m == KH - 1:
                    pe_free["w1b" + str(j % 2)] = C["pe"]
                pe_h = C["pe"]
                em("scalar", "activation", out=h_bf[:, m * T3:(m + 1) * T3],
                   in_=ps_h[bank][:, 0:T3], func=AF.Relu,
                   bias=b1_t[:, sl * KH + m:sl * KH + m + 1],
                   waits=[("pe", pe_h)], incs=["ac"])
                h_done[m] = ("ac", C["ac"])
                ac_hbank[bank] = h_done[m]
            ac_h_all = C["ac"]
            b1_ac[j % 2] = C["ac"]
            if j + 1 < NT3:
                if j + 2 < NT3:
                    emit_prefetch_b(j + 2)
                emit_b1(j + 1)
                if j + 2 < NT3:
                    emit_weights_w1(j + 2)
                emit_gxtr(j + 1)
            ybanks = [ps_y[0], ps_y[1], ps_b0, ps_b1]
            for h in range(2):
                for nb in range(2):
                    bi = h * 2 + nb
                    wt = []
                    if bi == 0:
                        wt += [("w2a", 16 * (j + 1)), ("ac", ac_h_all)]
                    if dv_ybank[bi]:
                        wt.append(("dv", dv_ybank[bi]))
                    for k in range(KD):
                        em("tensor", "matmul", out=ybanks[bi][:, :],
                           lhsT=hv[:, k, h * 128:(h + 1) * 128],
                           rhs=W2v[0][:, k, nb * 512:(nb + 1) * 512],
                           start=(k == 0), stop=False,
                           waits=(wt if k == 0 else []),
                           incs=(["pe"] if bi == 3 and k == KD - 1 else []))
            pe_free["w2a"] = C["pe"]
            for h in range(2):
                for nb in range(2):
                    bi = h * 2 + nb
                    wt = [("w2b", 16 * (j + 1))] if bi == 0 else []
                    for k in range(KD):
                        em("tensor", "matmul", out=ybanks[bi][:, :],
                           lhsT=hv[:, KD + k, h * 128:(h + 1) * 128],
                           rhs=W2v[1][:, k, nb * 512:(nb + 1) * 512],
                           start=False, stop=(k == KD - 1),
                           waits=(wt if k == 0 else []),
                           incs=(["pe"] if k == KD - 1 else []))
                    if bi == 3:
                        pe_free["w2b"] = C["pe"]
                    pe_ye = C["pe"]
                    wt2 = [("pe", pe_ye), ("b2g", b2g_j)]
                    if dv_zt and bi == 0:
                        wt2.append(zs_done[j - 1])
                    em("vector", "tensor_tensor",
                       out=zt[:, h * D + nb * 512:h * D + (nb + 1) * 512],
                       in0=ybanks[bi][:, :], in1=b2bc[:, sl * D + nb * 512:sl * D + (nb + 1) * 512],
                       op=ALU.add, waits=wt2)
                    em("vector", "tensor_scalar",
                       out=zt[:, h * D + nb * 512:h * D + (nb + 1) * 512],
                       in0=zt[:, h * D + nb * 512:h * D + (nb + 1) * 512],
                       scalar1=w_f_v[:, j % 4, h:h + 1], scalar2=None, op0=ALU.mult)
                    dv_ybank[bi] = C["dv"]
            dv_zt = C["dv"]
            dv_b2bc[sl] = C["dv"]
            zsem = "zs" if j % 2 == 0 else "zs1"
            for h in range(2):
                em("gpsimd", "indirect_dma_start",
                   ZD.ap(), IndirectOffsetOnAxis(dst_i_v[:, j % 4, h:h + 1], 0),
                   zt[:, h * D:(h + 1) * D], None,
                   bounds_check=2 * SH - 1, oob_is_err=False,
                   waits=([("dv", dv_zt), ("zf", zf_total)] if h == 0 else []),
                   incs=[zsem])
            zs_done[j] = (zsem, C[zsem])
        zs_tw = [("zs", C["zs"]), ("zs1", C["zs1"])]

        st_slot = {0: 0, 1: 0}
        dv_zl = {0: 0, 1: 0, 2: 0, 3: 0}
        x4bufs = [g_grp, scr1, zt[:, 0:D], zt[:, D:2 * D]]
        for t in range(NT):
            for g in range(NG):
                gi = t * NG + g
                sl4 = gi % 4
                rcsem = ("rc0", "rc1", "rc2", "rc3")[sl4]
                tokbase = t * TILE + g * 128
                gc4 = t * NG + g
                em("sync", "dma_start",
                   out=x_tv.tensor if False else zn_v[:, 2 * sl4:2 * sl4 + 2, :],
                   in_=A(ZD, 2 * tokbase * D, [[2 * D, 128], [D, 2], [1, D]]),
                   waits=(zs_tw if gi == 0 else []) +
                         ([("dv", dv_zl[sl4])] if dv_zl[sl4] else []),
                   incs=[rcsem])
                em("scalar", "dma_start",
                   out=x4bufs[sl4][:, :] if sl4 < 2 else x4bufs[sl4],
                   in_=A(x_in, tokbase * D, [[D, 128], [1, D]]),
                   waits=([("dv", dv_sgrp)] if gi < 2 else []) +
                         ([("dv", dv_zl[sl4])] if dv_zl[sl4] else []) +
                         ([zs_tw[0], zs_tw[1]] if gi == 2 else []),
                   incs=[rcsem])
                rc_g = C[rcsem]
                z0 = zn_v[:, 2 * sl4, :]
                z1 = zn_v[:, 2 * sl4 + 1, :]
                xb4 = x4bufs[sl4][:, :] if sl4 < 2 else x4bufs[sl4]
                em("vector", "tensor_add", out=z0, in0=z0, in1=z1, waits=[(rcsem, rc_g)])
                dv_y4 = C["dv"]
                em("scalar", "activation", out=z1, in_=z0, func=AF.Silu,
                   waits=[("dv", dv_y4)], incs=["ac"])
                ac_g4 = C["ac"]
                if t == 7 and g == SPECIAL_GROUP:
                    em("vector", "tensor_tensor", out=z0, in0=z1, in1=s_grp[:, :],
                       op=ALU.mult, waits=[("ac", ac_g4)])
                else:
                    em("vector", "tensor_scalar", out=z0, in0=xb4,
                       scalar1=mr_all[:, gc4:gc4 + 1],
                       scalar2=mr_all[:, GC + gc4:GC + gc4 + 1],
                       op0=ALU.subtract, op1=ALU.mult, waits=[("ac", ac_g4)])
                    em("vector", "tensor_tensor", out=z0, in0=z0, in1=z1, op=ALU.mult)
                ssem = "st0" if gi % 2 == 0 else "st1"
                wt = []
                if st_slot[gi % 2]:
                    wt.append((ssem, st_slot[gi % 2]))
                em("vector", "tensor_add", out=out_g[:, (gi % 2) * D:(gi % 2 + 1) * D],
                   in0=z0, in1=xb4, waits=wt)
                dv_zl[sl4] = C["dv"]
                em("scalar", "dma_start", out=A(out, tokbase * D, [[D, 128], [1, D]]),
                   in_=out_g[:, (gi % 2) * D:(gi % 2 + 1) * D],
                   waits=[("dv", dv_zl[sl4])], incs=[ssem])
                st_slot[gi % 2] = C[ssem]

        em("scalar", "activation", out=t_f[:, 0:1], in_=t_f[:, 0:1], func=AF.Copy,
           waits=[("st0", C["st0"]), ("st1", C["st1"])])

        with nc.Block() as block:
            def run(eng_name):
                def body(eng):
                    bndregs = {}
                    if eng_name == "gpsimd":
                        for bv in (NR - 1, SH - 1, 128 * E - 1, 2 * SH - 1,
                                   E * 128 * 2 - 1, E - 1):
                            r = es.enter_context(eng.register(f"bnd{bv}"))
                            eng.reg_mov(r, bv)
                            bndregs[bv] = r
                    for method, args, kw, ws, incs in sched[eng_name]:
                        for s, v in ws:
                            eng.wait_ge(s, v)
                        if "bounds_check" in kw and isinstance(kw["bounds_check"], int):
                            kw = dict(kw, bounds_check=bndregs[kw["bounds_check"]])
                        inst = getattr(eng, method)(*args, **kw)
                        for s in incs:
                            inst.then_inc(s, 16 if "dma_start" in method else 1)
                return body
            block.sync(run("sync"))
            block.gpsimd(run("gpsimd"))
            block.scalar(run("scalar"))
            block.vector(run("vector"))
            block.tensor(run("tensor"))

    return nc


def kernel(**inputs):
    x = np.ascontiguousarray(np.asarray(inputs["L0_attn_inputs"], dtype=np.float32))
    toks = (np.arange(GC)[None, :] * 128 + np.arange(128)[:, None]).astype(np.float32)
    segmask = np.ones((128, E * GC), dtype=np.float32)
    segmask[:, 0::GC] = 0.0
    common = {
        "Wv": np.ascontiguousarray(np.asarray(inputs["L0_attn_Wv"], dtype=np.float32)),
        "Wo": np.ascontiguousarray(np.asarray(inputs["L0_attn_Wo"], dtype=np.float32)),
        "Wg": np.ascontiguousarray(np.asarray(inputs["L0_ffn_Wg"], dtype=np.float32)),
        "W1": np.ascontiguousarray(
            np.asarray(inputs["L0_ffn_W1"], dtype=np.float32)
            .reshape(8, 8, 128, 2, 1024).transpose(0, 3, 2, 1, 4).reshape(8 * 256, 8192)),
        "b1": np.ascontiguousarray(np.asarray(inputs["L0_ffn_b1"], dtype=np.float32)),
        "W2": np.ascontiguousarray(
            np.asarray(inputs["L0_ffn_W2"], dtype=np.float32)
            .reshape(8, 2, 8, 128, 1024).transpose(0, 3, 1, 2, 4).reshape(8 * 256, 8192)),
        "b2": np.ascontiguousarray(np.asarray(inputs["L0_ffn_b2"], dtype=np.float32)),
        "ident": np.eye(128, dtype=np.float32),
        "ltri": np.triu(np.ones((128, 128), dtype=np.float32), 1),
        "ones128": np.ones((128, 1), dtype=np.float32),
        "ones1r": np.ones((1, 128), dtype=np.float32),
        "toksc": toks,
        "pbase": np.stack([np.arange(128) * 128, np.arange(128) * 8, np.arange(128)], axis=1).astype(np.float32),
        "jtc": (np.arange(NT3)[None, :] * T3).astype(np.float32),
        "segm": segmask,
        "onehot": np.repeat(np.eye(8, dtype=np.float32), 128, axis=1),
        "tmask": np.zeros((128, 1), dtype=np.float32),
    }
    tmask_on = np.ones((128, 1), dtype=np.float32)
    nc = build_nc()
    in_maps = []
    for c in range(8):
        b, half = c // 2, c % 2
        m = dict(common)
        m["x"] = np.ascontiguousarray(x[b, half * SH:(half + 1) * SH, :])
        m["tmask"] = tmask_on if half == 1 else common["tmask"]
        in_maps.append(m)
    res = run_bass_kernel_spmd(nc, in_maps, core_ids=list(range(8)))
    outp = np.empty((4, 8192, D), dtype=np.float32)
    for c in range(8):
        b, half = c // 2, c % 2
        outp[b, half * SH:(half + 1) * SH, :] = res.results[c]["out"]
    return outp


# version 47
# speedup vs baseline: 1.0793x; 1.0793x over previous
import contextlib
import numpy as np
import concourse.bass as bass
import concourse.mybir as mybir
from concourse.bass_utils import run_bass_kernel_spmd
from concourse.bass_core_rust import IndirectOffsetOnAxis

F32 = mybir.dt.float32
BF16 = mybir.dt.bfloat16
I32 = mybir.dt.int32
AF = mybir.ActivationFunctionType
ALU = mybir.AluOpType
AX = mybir.AxisListType

D = 1024
E = 8
HID = 2048
SH = 4096
TILE = 512
NT = SH // TILE
NG = TILE // 128
KD = D // 128
KH = HID // 128
GC = NT * NG
T3 = 256
NT3 = (2 * SH) // T3 + E
NR = NT3 * T3
BIGF = float(2 ** 20)
TILE_ORDER = [7, 0, 1, 2, 3, 4, 5, 6]
SPECIAL_COL = 384
SPECIAL_GROUP = 3


def build_nc():
    nc = bass.Bass("TRN2", target_bir_lowering=False)

    x_in = nc.dram_tensor("x", [SH, D], F32, kind="ExternalInput")
    Wv = nc.dram_tensor("Wv", [D, D], F32, kind="ExternalInput")
    Wo = nc.dram_tensor("Wo", [D, D], F32, kind="ExternalInput")
    Wg = nc.dram_tensor("Wg", [D, E], F32, kind="ExternalInput")
    W1 = nc.dram_tensor("W1", [E * 256, KD * 1024], F32, kind="ExternalInput")
    b1 = nc.dram_tensor("b1", [E, HID], F32, kind="ExternalInput")
    W2 = nc.dram_tensor("W2", [E * 256, KH * 512], F32, kind="ExternalInput")
    b2 = nc.dram_tensor("b2", [E, D], F32, kind="ExternalInput")
    ident = nc.dram_tensor("ident", [128, 128], F32, kind="ExternalInput")
    onehot = nc.dram_tensor("onehot", [E, E * 128], F32, kind="ExternalInput")
    tmask = nc.dram_tensor("tmask", [128, 1], F32, kind="ExternalInput")
    ltri = nc.dram_tensor("ltri", [128, 128], F32, kind="ExternalInput")
    ones128 = nc.dram_tensor("ones128", [128, 1], F32, kind="ExternalInput")
    ones1r = nc.dram_tensor("ones1r", [1, 128], F32, kind="ExternalInput")
    toksc = nc.dram_tensor("toksc", [128, GC], F32, kind="ExternalInput")
    pbase = nc.dram_tensor("pbase", [128, 3], F32, kind="ExternalInput")
    jtc = nc.dram_tensor("jtc", [1, NT3], F32, kind="ExternalInput")
    segm = nc.dram_tensor("segm", [128, E * GC], F32, kind="ExternalInput")
    out = nc.dram_tensor("out", [SH, D], F32, kind="ExternalOutput")

    XNBF = nc.dram_tensor("XNBF", [SH, D], BF16)
    B1T = nc.dram_tensor("B1T", [128 * E, KH], F32)
    LREC = nc.dram_tensor("LREC", [NR, 3], F32)
    ZD = nc.dram_tensor("ZD", [2 * SH, D], F32)

    es = contextlib.ExitStack()
    with es:
        sb = lambda n, s, d: es.enter_context(nc.sbuf_tensor(n, s, d))
        pst = lambda n, s, d: es.enter_context(nc.psum_tensor(n, s, d))
        mksem = lambda n: es.enter_context(nc.semaphore(n))

        x_t = sb("x_t", [128, 2 * NG * D], F32)
        xnb = sb("xnb", [128, 2 * NG * D], BF16)
        mr_all = sb("mr_all", [128, 2 * GC], F32)
        xnT = sb("xnT", [128, KD * TILE], BF16)
        xn_col = sb("xn_col", [128, KD], BF16)
        cols = sb("cols", [128, 64], F32)
        lg = sb("lg", [128, NG * E], F32)
        sme = sb("sme", [128, NG * E], F32)
        m8 = sb("m8", [128, NG * E], F32)
        w_t = sb("w_t", [128, NG * E], F32)
        w_all = sb("w_all", [128, E * GC], F32)
        s_grp = sb("s_grp", [128, D], F32)
        scr1 = sb("scr1", [128, D], F32)
        g_grp = sb("g_grp", [128, D], F32)
        out_g = sb("out_g", [128, 2 * D], F32)
        sel = sb("sel", [128, E * GC], F32)
        incl = sb("incl", [128, E * GC], F32)
        slot = sb("slot", [128, E * GC], F32)
        mm1 = sb("mm1", [128, E * GC], F32)
        mm2 = sb("mm2", [128, E * GC], F32)
        didx = sb("didx", [128, E * GC], F32)
        runsb = sb("runsb", [128, GC], F32)
        cnt = sb("cnt", [128, E], F32)
        base_pe = sb("base_pe", [128, E], F32)
        gbase = sb("gbase", [128, E], F32)
        tot = sb("tot", [1, E], F32)
        tr8 = sb("tr8", [1, 3 * E], F32)
        tr8i = sb("tr8i", [1, 2 * E], I32)
        etile = sb("etile", [1, 4 * NT3], F32)
        ebf = sb("ebf", [128, NT3], F32)
        dbf = sb("dbf", [128, NT3], F32)
        dbf2 = sb("dbf2", [128, NT3], F32)
        wrow = sb("wrow", [128, NT3], F32)
        idxw = sb("idxw", [128, 2 * NT3], F32)
        idxw_i = sb("idxw_i", [128, 2 * NT3], I32)
        idxw2 = sb("idxw2", [128, 2 * NT3], F32)
        idxw2_i = sb("idxw2_i", [128, 2 * NT3], I32)
        idxb = sb("idxb", [128, NT3], F32)
        idxb_i = sb("idxb_i", [128, NT3], I32)
        idxb2_i = sb("idxb2_i", [128, NT3], I32)
        didx_r = sb("didx_r", [128, 2 * GC], F32)
        didx_ri = sb("didx_ri", [128, 2 * GC], I32)
        w_r = sb("w_r", [128, 2 * GC], F32)
        dstc = sb("dstc", [128, 2 * GC], F32)
        recs = sb("recs", [128, 2 * GC * 3], F32)
        acc_s = sb("acc_s", [128, GC], F32)
        W1a = sb("W1a", [128, KD * 1024], BF16)
        W1b = sb("W1b", [128, KD * 1024], BF16)
        W1a1 = sb("W1a1", [128, KD * 1024], BF16)
        W1b1 = sb("W1b1", [128, KD * 1024], BF16)
        W2a = sb("W2a", [128, KD * 1024], BF16)
        W2b = sb("W2b", [128, KD * 1024], BF16)
        zt = sb("zt", [128, 2 * D], F32)
        rec_sb = sb("rec_sb", [128, 4 * 6], F32)
        tok_i = sb("tok_i", [128, 4 * 2], I32)
        dst_i = sb("dst_i", [128, 4 * 2], I32)
        w_f = sb("w_f", [128, 4 * 2], F32)
        b1_t = sb("b1_t", [128, 2 * KH], F32)
        b2bc = sb("b2bc", [128, 2 * D], F32)
        u_bf = sb("u_bf", [128, E], BF16)
        t_f = sb("t_f", [128, E], F32)
        t8 = sb("t8", [8, 128], F32)
        b1_sb = sb("b1_sb", [128, E * KH], F32)
        wg_f = sb("wg_f", [128, KD * E], F32)
        wg_b = sb("wg_b", [128, KD * E], BF16)
        idf = sb("idf", [128, 128], F32)
        idb = sb("idb", [128, 128], BF16)
        tm_sb = sb("tm_sb", [128, 1], F32)
        eps_c = sb("eps_c", [128, 1], F32)
        mh_c = sb("mh_c", [128, 1], F32)
        ltri_sb = sb("ltri_sb", [128, 128], F32)
        o128 = sb("o128", [128, 1], F32)
        o1r = sb("o1r", [1, 128], F32)
        toks_sb = sb("toks_sb", [128, GC], F32)
        pbase_sb = sb("pbase_sb", [128, 3], F32)
        jtc_sb = sb("jtc_sb", [1, NT3], F32)
        segm_sb = sb("segm_sb", [128, E * GC], F32)


        ps_h = [pst("ps_h0", [128, 512], F32), pst("ps_h1", [128, 512], F32)]
        ps_y = [pst("ps_y0", [128, 512], F32), pst("ps_y1", [128, 512], F32)]
        ps_tr = pst("ps_tr", [128, 1024], BF16)
        ps_b0 = pst("ps_b0", [128, 512], F32)
        ps_b1 = pst("ps_b1", [128, 512], F32)
        ps_m = pst("ps_m", [128, 512], F32)

        SEM = {k: mksem("s_" + k) for k in (
            "ld", "sd", "xs", "zf", "cvo", "cast", "sct", "rc0", "rc1", "rc2", "rc3",
            "gx0", "gx1", "w1a", "w1b", "w1a1", "w1b1", "w2a", "w2b", "b1g", "b2g",
            "zs", "zs1", "sd1", "xs1",
            "st0", "st1", "pe", "dv", "ac")}
        C = {k: 0 for k in SEM}
        DMA = {k for k in SEM if k not in ("pe", "dv", "ac")}

        sched = {k: [] for k in ("sync", "gpsimd", "scalar", "vector", "tensor")}

        def em(eng, method, *args, waits=(), incs=(), **kw):
            waits = list(waits)
            if eng == "vector" and C["dv"] > 0:
                waits.append(("dv", C["dv"]))
            ws = [(SEM[n], v) for (n, v) in waits if v > 0]
            if eng == "vector" and "dv" not in incs:
                incs = list(incs) + ["dv"]
            sched[eng].append((method, args, kw, ws, [SEM[n] for n in incs]))
            for n in incs:
                C[n] += 16 if n in DMA else 1
            return {n: C[n] for n in incs}

        A = bass.AP
        x_tv = x_t.ap().rearrange("p (s g d) -> p s g d", s=2, g=NG)
        xnT_v = xnT.ap().rearrange("p (c t2) -> p c t2", c=KD)
        ps_tr_v = ps_tr.ap().rearrange("p (c j) -> p c j", c=KD)
        WvN = W1a.ap().rearrange("p (r c j) -> p r c j", r=KD, c=KD)
        WoN = W1b.ap().rearrange("p (r c j) -> p r c j", r=KD, c=KD)
        WvT = W2a.ap().rearrange("p (k m j) -> p k m j", k=KD, m=KD)
        WoT = W2b.ap().rearrange("p (k m j) -> p k m j", k=KD, m=KD)
        W1vp = [[W1a.ap().rearrange("p (k h) -> p k h", k=KD),
                 W1b.ap().rearrange("p (k h) -> p k h", k=KD)],
                [W1a1.ap().rearrange("p (k h) -> p k h", k=KD),
                 W1b1.ap().rearrange("p (k h) -> p k h", k=KD)]]
        W2v = [W2a.ap().rearrange("p (k n) -> p k n", k=KD),
               W2b.ap().rearrange("p (k n) -> p k n", k=KD)]
        w_all_v = w_all.ap().rearrange("p (e gc) -> p e gc", e=E)
        incl_v = incl.ap().rearrange("p (e gc) -> p e gc", e=E)
        recs_v = recs.ap().rearrange("p (r gc c3) -> p r gc c3", r=2, gc=GC)
        rec_sb_v = rec_sb.ap().rearrange("p (s h c3) -> p s h c3", s=4, h=2)
        tok_i_v = tok_i.ap().rearrange("p (s h) -> p s h", s=4)
        dst_i_v = dst_i.ap().rearrange("p (s h) -> p s h", s=4)
        w_f_v = w_f.ap().rearrange("p (s h) -> p s h", s=4)
        b1_t_v = b1_t.ap().rearrange("p (s m) -> p s m", s=2)
        gx_v = xnb.ap().rearrange("p (s q) -> p s q", s=4)
        gxT_v = xnb.ap().rearrange("p (s c t2) -> p s c t2", s=4, c=KD)[:, 2, :, :]
        h_bf = xnT
        hv = h_bf.ap().rearrange("p (k t2) -> p k t2", k=KH)
        zn_v = x_t.ap().rearrange("p (s d2) -> p s d2", s=8)

        em("sync", "dma_start", out=idf[:, :], in_=ident[:, :], incs=["ld"])
        em("sync", "dma_start", out=tm_sb[:, :], in_=tmask[:, :], incs=["ld"])
        em("sync", "dma_start", out=ltri_sb[:, :], in_=ltri[:, :], incs=["ld"])
        em("sync", "dma_start", out=o128[:, :], in_=ones128[:, :], incs=["ld"])
        em("sync", "dma_start", out=o1r[:, :], in_=ones1r[:, :], incs=["ld"])
        em("sync", "dma_start", out=toks_sb[:, :], in_=toksc[:, :], incs=["ld"])
        em("sync", "dma_start", out=pbase_sb[:, :], in_=pbase[:, :], incs=["ld"])
        em("sync", "dma_start", out=jtc_sb[:, :], in_=jtc[:, :], incs=["ld"])
        em("sync", "dma_start", out=segm_sb[:, :], in_=segm[:, :], incs=["ld"])
        em("sync", "dma_start", out=wg_f.ap().rearrange("p (k e) -> p k e", k=KD),
           in_=A(Wg, 0, [[E, 128], [128 * E, KD], [1, E]]), incs=["ld"])
        b_raw = out_g
        em("sync", "dma_start", out=b_raw[0:16, 0:E * 128].rearrange("i (e j) -> i e j", e=E),
           in_=A(b1, 0, [[128, 16], [HID, E], [1, 128]]), incs=["ld"])
        oh_sb = b2bc
        em("sync", "dma_start", out=oh_sb[0:8, 0:E * 128], in_=onehot[:, :], incs=["ld"])
        ld_consts = C["ld"]

        em("vector", "memset", zt[:, :], 0.0)
        dv_z0 = C["dv"]
        for q in range(32):
            em("sync", "dma_start",
               out=A(ZD, q * 256 * D, [[2 * D, 128], [D, 2], [1, D]]),
               in_=zt.ap().rearrange("p (h d2) -> p h d2", h=2),
               waits=([("dv", dv_z0)] if q == 0 else []), incs=["zf"])
        em("vector", "memset", recs[:, 0:192], BIGF)
        em("vector", "memset", didx[:, 0:48], BIGF)
        dv_lrp = C["dv"]
        em("sync", "dma_start", out=A(LREC, 0, [[192, 128], [1, 192]]),
           in_=recs[:, 0:192], waits=[("dv", dv_lrp)], incs=["zf"])
        em("sync", "dma_start", out=A(LREC, 128 * 192, [[48, 128], [1, 48]]),
           in_=didx[:, 0:48], incs=["zf"])
        zf_total = C["zf"]

        em("gpsimd", "dma_start", out=W1a.ap().rearrange("p (r d2) -> p r d2", r=KD),
           in_=A(Wv, 0, [[D, 128], [128 * D, KD], [1, D]]), incs=["cvo"])
        cvo_wv = C["cvo"]
        em("gpsimd", "dma_start", out=W1b.ap().rearrange("p (r d2) -> p r d2", r=KD),
           in_=A(Wo, 0, [[D, 128], [128 * D, KD], [1, D]]), incs=["cvo"])
        cvo_wo = C["cvo"]


        em("vector", "memset", eps_c[:, :], 1e-5)
        em("vector", "memset", mh_c[:, :], -0.5)
        em("scalar", "activation", out=idb[:, :], in_=idf[:, :], func=AF.Copy,
           waits=[("ld", ld_consts)], incs=["ac"])
        ac_idb = C["ac"]
        em("scalar", "activation", out=wg_b[:, :], in_=wg_f[:, :], func=AF.Copy, incs=["ac"])
        ac_wg = C["ac"]

        for e in range(E):
            em("tensor", "transpose", out=ps_b0[:, e * 16:(e + 1) * 16],
               in_=b_raw[0:16, e * 128:(e + 1) * 128], identity=idf[0:16, 0:16],
               waits=([("ld", ld_consts), ("ac", ac_idb)] if e == 0 else []),
               incs=(["pe"] if e == E - 1 else []))
        pe_b1t = C["pe"]
        em("vector", "tensor_copy", out=b1_sb[:, :], in_=ps_b0[:, 0:128],
           waits=[("pe", pe_b1t)])
        dv_b1 = C["dv"]
        em("sync", "dma_start", out=A(B1T, 0, [[E * KH, 128], [1, E * KH]]),
           in_=b1_sb[:, :], waits=[("dv", dv_b1)], incs=["xs"])

        dv_trbank = 0
        ac_trbank = 0
        tr_last = {0: ("ac", 0), 1: ("dv", 0)}
        pe_wvT = [0]
        pe_woT = [0]

        def emit_wvwo():
            nonlocal dv_trbank
            for c in range(KD):
                for r in range(KD):
                    wt = []
                    if c == 0 and r == 0:
                        wt = [("cvo", cvo_wo), ("ac", ac_idb), tr_last[0], tr_last[1]]
                    if r == 0 and dv_trbank:
                        wt.append(("dv", dv_trbank))
                    em("tensor", "transpose", out=ps_tr[:, r * 128:(r + 1) * 128],
                       in_=WvN[:, r, c, :], identity=idb[:, :], waits=wt,
                       incs=(["pe"] if r == KD - 1 else []))
                em("vector", "tensor_copy", out=WvT[:, c, :, :], in_=ps_tr_v[:, :, :],
                   waits=[("pe", C["pe"])])
                dv_trbank = C["dv"]
            pe_wvT[0] = C["pe"]
            for c in range(KD):
                for r in range(KD):
                    wt = []
                    if c == 0 and r == 0:
                        wt = [("cvo", cvo_wo)]
                    if r == 0:
                        wt.append(("dv", dv_trbank))
                    em("tensor", "transpose", out=ps_tr[:, r * 128:(r + 1) * 128],
                       in_=WoN[:, r, c, :], identity=idb[:, :], waits=wt,
                       incs=(["pe"] if r == KD - 1 else []))
                em("vector", "tensor_copy", out=WoT[:, c, :, :], in_=ps_tr_v[:, :, :],
                   waits=[("pe", C["pe"])])
                dv_trbank = C["dv"]
            pe_woT[0] = C["pe"]

        pe_xnb_free = {0: 0, 1: 0}
        dv_xfree = {0: 0, 1: 0}
        ac_xfree = {0: 0, 1: 0}
        xs_xnb = {0: 0, 1: 0}
        dv_lgbank = {0: dv_b1, 1: 0}
        pe_lg_done = {}
        dv_m = 0
        dv_xncol = 0

        def emit_softmax(t):
            par = (TILE_ORDER.index(t)) % 2
            pbank = ps_b0 if par == 0 else ps_b1
            em("vector", "tensor_copy", out=lg[:, :], in_=pbank[:, 0:NG * E],
               waits=[("pe", pe_lg_done[t])])
            dv_lgbank[par] = C["dv"]
            for g in range(NG):
                em("vector", "reduce_max", out=cols[:, 28 + g:29 + g],
                   in_=lg[:, g * E:(g + 1) * E], axis=AX.X)
            for g in range(NG):
                em("vector", "tensor_scalar", out=sme[:, g * E:(g + 1) * E],
                   in0=lg[:, g * E:(g + 1) * E], scalar1=cols[:, 28 + g:29 + g],
                   scalar2=None, op0=ALU.subtract)
            dvp = C["dv"]
            em("scalar", "activation", out=sme[:, :], in_=sme[:, :], func=AF.Exp,
               waits=[("dv", dvp)], incs=["ac"])
            acp = C["ac"]
            for g in range(NG):
                em("vector", "reduce_sum", out=cols[:, 28 + g:29 + g],
                   in_=sme[:, g * E:(g + 1) * E], axis=AX.X,
                   waits=([("ac", acp)] if g == 0 else []))
            em("vector", "reciprocal", out=cols[:, 28:32], in_=cols[:, 28:32])
            for g in range(NG):
                em("vector", "tensor_scalar", out=sme[:, g * E:(g + 1) * E],
                   in0=sme[:, g * E:(g + 1) * E], scalar1=cols[:, 28 + g:29 + g],
                   scalar2=None, op0=ALU.mult)
            for g in range(NG):
                em("vector", "max", out=m8[:, g * E:(g + 1) * E],
                   in_=sme[:, g * E:(g + 1) * E])
            for g in range(NG):
                em("vector", "tensor_scalar", out=w_t[:, g * E:(g + 1) * E],
                   in0=sme[:, g * E:(g + 1) * E], scalar1=m8[:, g * E + 1:g * E + 2],
                   scalar2=None, op0=ALU.is_ge)
            em("vector", "tensor_tensor", out=w_t[:, :], in0=w_t[:, :], in1=sme[:, :],
               op=ALU.mult)
            em("vector", "tensor_copy", out=w_all_v[:, :, t * NG:(t + 1) * NG],
               in_=w_t.ap().rearrange("p (g e) -> p e g", g=NG))

        sd_tile = {}
        sdsem = lambda p: "sd" if p == 0 else "sd1"
        dv_nmr_t = {}

        def emit_stats(ti):
            t = TILE_ORDER[ti]
            par = ti % 2
            co = par * 32
            sd_x = sd_tile[ti]
            xsl = x_t[:, par * NG * D:(par + 1) * NG * D]
            em("vector", "reduce_sum", out=cols[:, co:co + 4],
               in_=x_tv[:, par, :, :], axis=AX.X, waits=[sd_x])
            dv_xfree[par] = C["dv"]
            for g in range(NG):
                wt = [sd_x] if g == 0 else [("ac", C["ac"])]
                em("scalar", "activation", out=g_grp[:, :],
                   in_=xsl[:, g * D:(g + 1) * D],
                   func=AF.Square, accum_out=cols[:, co + 4 + g:co + 5 + g],
                   waits=wt, incs=["ac"])
            ac_ssq = C["ac"]
            em("vector", "tensor_scalar", out=cols[:, co + 8:co + 12],
               in0=cols[:, co:co + 4], scalar1=1.0 / D, scalar2=None, op0=ALU.mult)
            em("vector", "tensor_tensor", out=cols[:, co + 12:co + 16],
               in0=cols[:, co + 8:co + 12], in1=cols[:, co + 8:co + 12], op=ALU.mult)
            em("vector", "scalar_tensor_tensor", out=cols[:, co + 16:co + 20],
               in0=cols[:, co + 4:co + 8], scalar=1.0 / D, in1=cols[:, co + 12:co + 16],
               op0=ALU.mult, op1=ALU.subtract, waits=[("ac", ac_ssq)])
            dv_var = C["dv"]
            em("scalar", "activation", out=cols[:, co + 16:co + 20],
               in_=cols[:, co + 16:co + 20], func=AF.Ln,
               bias=eps_c[:, 0:1], waits=[("dv", dv_var)], incs=["ac"])
            em("scalar", "activation", out=cols[:, co + 20:co + 24],
               in_=cols[:, co + 16:co + 20], func=AF.Exp,
               scale=mh_c[:, 0:1], incs=["ac"])
            ac_rstd = C["ac"]
            em("vector", "scalar_tensor_tensor", out=cols[:, co + 24:co + 28],
               in0=cols[:, co + 8:co + 12], scalar=-1.0, in1=cols[:, co + 20:co + 24],
               op0=ALU.mult, op1=ALU.mult, waits=[("ac", ac_rstd)])
            dv_nmr_t[ti] = C["dv"]
            em("vector", "tensor_copy", out=mr_all[:, t * NG:(t + 1) * NG],
               in_=cols[:, co + 8:co + 12])
            em("vector", "tensor_copy", out=mr_all[:, GC + t * NG:GC + (t + 1) * NG],
               in_=cols[:, co + 20:co + 24])

        em("scalar", "dma_start", out=x_tv[:, 0, :, :],
           in_=A(x_in, TILE_ORDER[0] * TILE * D, [[D, 128], [128 * D, NG], [1, D]]),
           incs=[sdsem(0)])
        sd_tile[0] = (sdsem(0), C[sdsem(0)])
        emit_stats(0)
        for ti, t in enumerate(TILE_ORDER):
            par = ti % 2
            base = t * TILE
            co = par * 32
            sd_x = sd_tile[ti]
            xsl = x_t[:, par * NG * D:(par + 1) * NG * D]
            if ti + 1 < NT:
                t2 = TILE_ORDER[ti + 1]
                p2 = (ti + 1) % 2
                em("scalar", "dma_start", out=x_tv[:, p2, :, :],
                   in_=A(x_in, t2 * TILE * D, [[D, 128], [128 * D, NG], [1, D]]),
                   waits=([("dv", dv_xfree[p2]), ("ac", ac_xfree[p2])] if ti >= 1 else []),
                   incs=[sdsem(p2)])
                sd_tile[ti + 1] = (sdsem(p2), C[sdsem(p2)])
                emit_stats(ti + 1)
            dv_nmr = dv_nmr_t[ti]
            ac_xnb = {}
            lbank = ps_b0 if par == 0 else ps_b1
            for g in range(NG):
                wt = [("dv", dv_nmr)] if g == 0 else []
                if g == 0:
                    if pe_xnb_free[par]:
                        wt.append(("pe", pe_xnb_free[par]))
                    if xs_xnb[par]:
                        wt.append(xs_xnb[par])
                em("scalar", "activation",
                   out=xnb[:, (par * NG + g) * D:(par * NG + g + 1) * D],
                   in_=xsl[:, g * D:(g + 1) * D], func=AF.Identity,
                   bias=cols[:, co + 24 + g:co + 25 + g],
                   scale=cols[:, co + 20 + g:co + 21 + g],
                   waits=wt, incs=["ac"])
                ac_xnb[g] = C["ac"]
                for c in range(KD):
                    wt = []
                    if c == 0:
                        wt = [("ac", ac_xnb[g]), tr_last[0], tr_last[1]]
                        if dv_trbank:
                            wt.append(("dv", dv_trbank))
                    em("tensor", "transpose", out=ps_tr[:, c * 128:(c + 1) * 128],
                       in_=xnb[:, (par * NG + g) * D + c * 128:(par * NG + g) * D + (c + 1) * 128],
                       identity=idb[:, :], waits=wt, incs=(["pe"] if c == KD - 1 else []))
                pe_xnb_free[par] = C["pe"]
                em("scalar", "activation", out=xnT_v[:, :, g * 128:(g + 1) * 128],
                   in_=ps_tr_v[:, :, :], func=AF.Copy,
                   waits=[("pe", C["pe"])], incs=["ac"])
                trb = ("ac", C["ac"])
                ac_trbank = C["ac"]
                tr_last[g % 2] = trb
                for k in range(KD):
                    wt = []
                    if k == 0:
                        wt = [trb]
                        if g == 0 and dv_lgbank[par]:
                            wt.append(("dv", dv_lgbank[par]))
                    em("tensor", "matmul", out=lbank[:, g * E:(g + 1) * E],
                       lhsT=xnT[:, k * TILE + g * 128:k * TILE + (g + 1) * 128],
                       rhs=wg_b[:, k * E:(k + 1) * E], start=(k == 0), stop=(k == KD - 1),
                       waits=wt, incs=(["pe"] if k == KD - 1 else []))
            pe_lg_done[t] = C["pe"]
            ac_xfree[par] = C["ac"]
            xssem = "xs" if par == 0 else "xs1"
            em("sync", "dma_start",
               out=A(XNBF, base * D, [[D, 128], [128 * D, NG], [1, D]]),
               in_=xnb.ap().rearrange("p (s g d) -> p s g d", s=2, g=NG)[:, par],
               waits=[("ac", ac_xnb[NG - 1])], incs=[xssem])
            xs_xnb[par] = (xssem, C[xssem])
            if t == 7:
                em("vector", "tensor_copy", out=xn_col[:, :],
                   in_=xnT_v[:, :, SPECIAL_COL:SPECIAL_COL + 1],
                   waits=[tr_last[0], tr_last[1]])
                dv_xncol = C["dv"]
            if ti > 0:
                emit_softmax(TILE_ORDER[ti - 1])
            if ti == 1:
                emit_wvwo()
        emit_softmax(TILE_ORDER[-1])
        dv_wall = C["dv"]

        for m in range(KD):
            for k in range(KD):
                wt = [("dv", dv_trbank), ("dv", dv_m), ("dv", dv_xncol)] \
                    if (m == 0 and k == 0) else []
                em("tensor", "matmul", out=ps_m[:, 40 + m:41 + m],
                   lhsT=WvT[:, k, m, :], rhs=xn_col[:, k:k + 1],
                   start=(k == 0), stop=(k == KD - 1), waits=wt,
                   incs=(["pe"] if (m == KD - 1 and k == KD - 1) else []))
        pe_u = C["pe"]
        em("scalar", "activation", out=u_bf[:, :], in_=ps_m[:, 40:48], func=AF.Copy,
           waits=[("pe", pe_u)], incs=["ac"])
        ac_u = C["ac"]
        for m in range(KD):
            for k in range(KD):
                wt = [("ac", ac_u)] if (m == 0 and k == 0) else []
                em("tensor", "matmul", out=ps_m[:, 48 + m:49 + m],
                   lhsT=WoT[:, k, m, :], rhs=u_bf[:, k:k + 1],
                   start=(k == 0), stop=(k == KD - 1), waits=wt,
                   incs=(["pe"] if (m == KD - 1 and k == KD - 1) else []))
        pe_t = C["pe"]
        em("scalar", "activation", out=t_f[:, :], in_=ps_m[:, 48:56], func=AF.Copy,
           waits=[("pe", pe_t)], incs=["ac"])
        ac_tf = C["ac"]
        em("tensor", "transpose", out=ps_b0[0:8, 384:512], in_=t_f[:, :],
           identity=idf[:, :], waits=[("ac", ac_tf), ("dv", dv_lgbank[0]), ("dv", dv_lgbank[1])],
           incs=["pe"])
        em("vector", "tensor_copy", out=t8[:, :], in_=ps_b0[0:8, 384:512],
           waits=[("pe", C["pe"])])
        dv_t8 = C["dv"]
        for m in range(KD):
            pbank = ps_b0 if m < 4 else ps_b1
            wt = [("dv", dv_t8), ("ld", ld_consts)] if m in (0, 4) else []
            em("tensor", "matmul", out=pbank[:, (m % 4) * 128:(m % 4 + 1) * 128],
               lhsT=oh_sb[0:8, m * 128:(m + 1) * 128], rhs=t8[:, :],
               start=True, stop=True, waits=wt, incs=(["pe"] if m in (3, 7) else []))
            if m == 3:
                pe_bc0 = C["pe"]
        pe_bc1 = C["pe"]
        em("vector", "tensor_copy", out=scr1[:, 0:512], in_=ps_b0[:, 0:512],
           waits=[("pe", pe_bc0)])
        em("vector", "tensor_copy", out=scr1[:, 512:1024], in_=ps_b1[:, 0:512],
           waits=[("pe", pe_bc1)])
        dv_trowbc = C["dv"]
        em("scalar", "dma_start", out=g_grp[:, :],
           in_=A(x_in, (7 * TILE + SPECIAL_GROUP * 128) * D, [[D, 128], [1, D]]),
           waits=[("ac", C["ac"])], incs=["sd"])
        sd_sg = C["sd"]
        gcs = 7 * NG + SPECIAL_GROUP
        em("vector", "tensor_scalar", out=g_grp[:, :], in0=g_grp[:, :],
           scalar1=mr_all[:, gcs:gcs + 1], scalar2=mr_all[:, GC + gcs:GC + gcs + 1],
           op0=ALU.subtract, op1=ALU.mult, waits=[("sd", sd_sg)])
        em("vector", "tensor_sub", out=s_grp[:, :], in0=scr1[:, :], in1=g_grp[:, :],
           waits=[("dv", dv_trowbc)])
        em("vector", "tensor_scalar", out=s_grp[:, :], in0=s_grp[:, :],
           scalar1=tm_sb[:, 0:1], scalar2=None, op0=ALU.mult)
        em("vector", "tensor_add", out=s_grp[:, :], in0=s_grp[:, :], in1=g_grp[:, :])
        dv_sgrp = C["dv"]

        em("vector", "tensor_scalar", out=sel[:, :], in0=w_all[:, :],
           scalar1=0.0, scalar2=None, op0=ALU.is_gt, waits=[("dv", dv_wall)])
        em("vector", "tensor_tensor_scan", out=incl[:, :], data0=segm_sb[:, :],
           data1=sel[:, :], initial=0.0, op0=ALU.mult, op1=ALU.add,
           waits=[("ld", ld_consts)])
        em("vector", "tensor_copy", out=cnt[:, :], in_=incl_v[:, :, GC - 1:GC])
        dv_cnt = C["dv"]
        em("tensor", "matmul", out=ps_m[:, 64:72], lhsT=ltri_sb[:, :], rhs=cnt[:, :],
           start=True, stop=True, waits=[("dv", dv_cnt)], incs=["pe"])
        em("vector", "tensor_copy", out=base_pe[:, :], in_=ps_m[:, 64:72],
           waits=[("pe", C["pe"])])
        em("tensor", "matmul", out=ps_m[0:1, 72:80], lhsT=o128[:, :], rhs=cnt[:, :],
           start=True, stop=True, waits=[("dv", C["dv"])], incs=["pe"])
        em("vector", "tensor_copy", out=tot[:, :], in_=ps_m[0:1, 72:80],
           waits=[("pe", C["pe"])])
        em("vector", "tensor_scalar", out=tr8[:, 0:E], in0=tot[:, :],
           scalar1=float(T3 - 1), scalar2=None, op0=ALU.add)
        em("vector", "tensor_copy", out=tr8i[:, 0:E], in_=tr8[:, 0:E])
        em("vector", "tensor_single_scalar", out=tr8i[:, E:2 * E], in_=tr8i[:, 0:E],
           scalar=T3 - 1, op=ALU.bitwise_and)
        em("vector", "tensor_copy", out=tr8[:, E:2 * E], in_=tr8i[:, E:2 * E])
        em("vector", "tensor_sub", out=tr8[:, 0:E], in0=tr8[:, 0:E], in1=tr8[:, E:2 * E])
        em("vector", "tensor_scalar", out=tr8[:, 0:E], in0=tr8[:, 0:E],
           scalar1=float(T3), scalar2=None, op0=ALU.max)
        em("vector", "memset", tr8[:, 2 * E:3 * E], 1.0)
        em("vector", "tensor_tensor_scan", out=tr8[:, E:2 * E], data0=tr8[:, 2 * E:3 * E],
           data1=tr8[:, 0:E], initial=0.0, op0=ALU.mult, op1=ALU.add)
        em("vector", "tensor_sub", out=tr8[:, 2 * E:3 * E], in0=tr8[:, E:2 * E],
           in1=tr8[:, 0:E])
        dv_seg = C["dv"]
        em("vector", "memset", etile[:, 0:NT3], 0.0)
        for e in range(E):
            em("vector", "tensor_scalar", out=etile[:, NT3:2 * NT3], in0=jtc_sb[:, :],
               scalar1=tr8[:, E + e:E + e + 1], scalar2=None, op0=ALU.is_ge)
            em("vector", "tensor_add", out=etile[:, 0:NT3], in0=etile[:, 0:NT3],
               in1=etile[:, NT3:2 * NT3])
        dv_etile = C["dv"]
        em("vector", "memset", etile[:, NT3:NT3 + 1], -1.0)
        em("vector", "tensor_copy", out=etile[:, NT3 + 1:2 * NT3], in_=etile[:, 0:NT3 - 1])
        em("vector", "tensor_tensor", out=etile[:, 2 * NT3:3 * NT3], in0=etile[:, 0:NT3],
           in1=etile[:, NT3:2 * NT3], op=ALU.not_equal)
        em("vector", "tensor_scalar", out=etile[:, 3 * NT3:4 * NT3], in0=etile[:, 0:NT3],
           scalar1=float(E - 1), scalar2=None, op0=ALU.is_le)
        em("vector", "tensor_tensor", out=etile[:, 2 * NT3:3 * NT3],
           in0=etile[:, 2 * NT3:3 * NT3], in1=etile[:, 3 * NT3:4 * NT3], op=ALU.mult)
        dv_diff = C["dv"]
        em("tensor", "matmul", out=ps_m[:, 80:80 + NT3], lhsT=o1r[:, :],
           rhs=etile[0:1, 0:NT3], start=True, stop=True,
           waits=[("dv", dv_etile)], incs=["pe"])
        em("vector", "tensor_copy", out=ebf[:, :], in_=ps_m[:, 80:80 + NT3],
           waits=[("pe", C["pe"])])
        dv_ebf = C["dv"]
        em("tensor", "matmul", out=ps_m[:, 80 + NT3:80 + 2 * NT3], lhsT=o1r[:, :],
           rhs=etile[0:1, 2 * NT3:3 * NT3], start=True, stop=True,
           waits=[("dv", dv_diff), ("dv", dv_ebf)], incs=["pe"])
        em("vector", "tensor_copy", out=dbf[:, :], in_=ps_m[:, 80 + NT3:80 + 2 * NT3],
           waits=[("pe", C["pe"])])
        dv_dbf = C["dv"]
        em("vector", "memset", etile[:, NT3:NT3 + 2], -1.0)
        em("vector", "tensor_copy", out=etile[:, NT3 + 2:2 * NT3], in_=etile[:, 0:NT3 - 2])
        em("vector", "tensor_tensor", out=etile[:, 2 * NT3:3 * NT3], in0=etile[:, 0:NT3],
           in1=etile[:, NT3:2 * NT3], op=ALU.not_equal)
        em("vector", "tensor_tensor", out=etile[:, 2 * NT3:3 * NT3],
           in0=etile[:, 2 * NT3:3 * NT3], in1=etile[:, 3 * NT3:4 * NT3], op=ALU.mult)
        dv_diff2 = C["dv"]
        em("tensor", "matmul", out=ps_m[:, 80 + NT3:80 + 2 * NT3], lhsT=o1r[:, :],
           rhs=etile[0:1, 2 * NT3:3 * NT3], start=True, stop=True,
           waits=[("dv", dv_diff2), ("dv", dv_dbf)], incs=["pe"])
        em("vector", "tensor_copy", out=dbf2[:, :], in_=ps_m[:, 80 + NT3:80 + 2 * NT3],
           waits=[("pe", C["pe"])])
        em("vector", "tensor_scalar", out=wrow[:, :], in0=ebf[:, :],
           scalar1=float(E - 1), scalar2=256.0, op0=ALU.min, op1=ALU.mult)
        em("vector", "tensor_scalar", out=wrow[:, :], in0=wrow[:, :],
           scalar1=pbase_sb[:, 2:3], scalar2=None, op0=ALU.add)
        for half in range(2):
            em("vector", "tensor_scalar", out=idxw[:, half * NT3:(half + 1) * NT3],
               in0=wrow[:, :], scalar1=float(half * 128) - BIGF, scalar2=None, op0=ALU.add)
            em("vector", "tensor_tensor", out=idxw[:, half * NT3:(half + 1) * NT3],
               in0=idxw[:, half * NT3:(half + 1) * NT3], in1=dbf2[:, :], op=ALU.mult)
            em("vector", "tensor_scalar", out=idxw[:, half * NT3:(half + 1) * NT3],
               in0=idxw[:, half * NT3:(half + 1) * NT3], scalar1=BIGF, scalar2=None,
               op0=ALU.add)
        em("vector", "tensor_copy", out=idxw_i[:, :], in_=idxw[:, :])
        em("vector", "tensor_scalar", out=wrow[:, :], in0=ebf[:, :],
           scalar1=float(E - 1), scalar2=128.0, op0=ALU.min, op1=ALU.mult)
        em("vector", "tensor_scalar", out=wrow[:, :], in0=wrow[:, :],
           scalar1=pbase_sb[:, 2:3], scalar2=2.0, op0=ALU.add, op1=ALU.mult)
        for half in range(2):
            em("vector", "tensor_scalar", out=idxw2[:, half * NT3:(half + 1) * NT3],
               in0=wrow[:, :], scalar1=float(half) - BIGF, scalar2=None, op0=ALU.add)
            em("vector", "tensor_tensor", out=idxw2[:, half * NT3:(half + 1) * NT3],
               in0=idxw2[:, half * NT3:(half + 1) * NT3], in1=dbf[:, :], op=ALU.mult)
            em("vector", "tensor_scalar", out=idxw2[:, half * NT3:(half + 1) * NT3],
               in0=idxw2[:, half * NT3:(half + 1) * NT3], scalar1=BIGF, scalar2=None,
               op0=ALU.add)
        em("vector", "tensor_copy", out=idxw2_i[:, :], in_=idxw2[:, :])
        dv_idxw = C["dv"]
        em("vector", "tensor_scalar", out=idxb[:, :], in0=ebf[:, :],
           scalar1=float(E - 1), scalar2=pbase_sb[:, 1:2], op0=ALU.min, op1=ALU.add)
        em("vector", "tensor_copy", out=idxb_i[:, :], in_=idxb[:, :])
        em("vector", "tensor_scalar", out=wrow[:, :], in0=ebf[:, :],
           scalar1=float(E - 1), scalar2=None, op0=ALU.min)
        em("vector", "tensor_copy", out=idxb2_i[:, :], in_=wrow[:, :])
        dv_idxb = C["dv"]
        em("tensor", "matmul", out=ps_m[:, 160:168], lhsT=o1r[:, :],
           rhs=tr8[0:1, 2 * E:3 * E], start=True, stop=True,
           waits=[("dv", dv_seg), ("dv", dv_idxb)], incs=["pe"])
        em("vector", "tensor_copy", out=gbase[:, :], in_=ps_m[:, 160:168],
           waits=[("pe", C["pe"])])
        em("vector", "tensor_add", out=gbase[:, :], in0=gbase[:, :], in1=base_pe[:, :])
        em("vector", "tensor_sub", out=didx[:, :], in0=incl[:, :], in1=sel[:, :],
           waits=[("zf", zf_total)])
        for e in range(E):
            em("vector", "tensor_scalar", out=didx[:, e * GC:(e + 1) * GC],
               in0=didx[:, e * GC:(e + 1) * GC], scalar1=gbase[:, e:e + 1],
               scalar2=None, op0=ALU.add)
        em("vector", "tensor_scalar", out=didx[:, :], in0=didx[:, :],
           scalar1=BIGF, scalar2=None, op0=ALU.subtract)
        em("vector", "tensor_tensor", out=didx[:, :], in0=didx[:, :], in1=sel[:, :],
           op=ALU.mult)
        em("vector", "tensor_scalar", out=didx[:, :], in0=didx[:, :],
           scalar1=BIGF, scalar2=None, op0=ALU.add)
        em("vector", "memset", runsb[:, :], 0.0)
        for e in range(E):
            em("vector", "tensor_copy", out=slot[:, e * GC:(e + 1) * GC], in_=runsb[:, :])
            if e < E - 1:
                em("vector", "tensor_add", out=runsb[:, :], in0=runsb[:, :],
                   in1=sel[:, e * GC:(e + 1) * GC])
        em("vector", "tensor_scalar", out=mm1[:, :], in0=slot[:, :],
           scalar1=1.0, scalar2=None, op0=ALU.min)
        em("vector", "tensor_scalar", out=mm1[:, :], in0=mm1[:, :],
           scalar1=-1.0, scalar2=1.0, op0=ALU.mult, op1=ALU.add)
        em("vector", "tensor_tensor", out=mm1[:, :], in0=mm1[:, :], in1=sel[:, :],
           op=ALU.mult)
        em("vector", "tensor_sub", out=mm2[:, :], in0=sel[:, :], in1=mm1[:, :])
        first_accs = True
        for r, mk in ((0, mm1), (1, mm2)):
            em("vector", "tensor_tensor", out=acc_s[:, :], in0=didx[:, 0:GC],
               in1=mk[:, 0:GC], op=ALU.mult,
               waits=([("zf", zf_total)] if first_accs else []))
            first_accs = False
            em("vector", "tensor_copy", out=didx_r[:, r * GC:(r + 1) * GC], in_=acc_s[:, :])
            for e in range(1, E):
                em("vector", "tensor_tensor", out=acc_s[:, :],
                   in0=didx[:, e * GC:(e + 1) * GC], in1=mk[:, e * GC:(e + 1) * GC],
                   op=ALU.mult)
                em("vector", "tensor_add", out=didx_r[:, r * GC:(r + 1) * GC],
                   in0=didx_r[:, r * GC:(r + 1) * GC], in1=acc_s[:, :])
            em("vector", "tensor_tensor", out=acc_s[:, :], in0=w_all[:, 0:GC],
               in1=mk[:, 0:GC], op=ALU.mult)
            em("vector", "tensor_copy", out=w_r[:, r * GC:(r + 1) * GC], in_=acc_s[:, :])
            for e in range(1, E):
                em("vector", "tensor_tensor", out=acc_s[:, :],
                   in0=w_all[:, e * GC:(e + 1) * GC], in1=mk[:, e * GC:(e + 1) * GC],
                   op=ALU.mult)
                em("vector", "tensor_add", out=w_r[:, r * GC:(r + 1) * GC],
                   in0=w_r[:, r * GC:(r + 1) * GC], in1=acc_s[:, :])
        em("vector", "tensor_copy", out=didx_ri[:, :], in_=didx_r[:, :])
        em("vector", "tensor_scalar", out=dstc[:, 0:GC], in0=toks_sb[:, :],
           scalar1=2.0, scalar2=None, op0=ALU.mult)
        em("vector", "tensor_scalar", out=dstc[:, GC:2 * GC], in0=dstc[:, 0:GC],
           scalar1=1.0, scalar2=None, op0=ALU.add)
        for r in range(2):
            em("vector", "tensor_copy", out=recs_v[:, r, :, 0], in_=toks_sb[:, :])
            em("vector", "tensor_copy", out=recs_v[:, r, :, 1],
               in_=w_r[:, r * GC:(r + 1) * GC])
            em("vector", "tensor_copy", out=recs_v[:, r, :, 2],
               in_=dstc[:, r * GC:(r + 1) * GC])
        dv_recs = C["dv"]
        for r in range(2):
            for gc in range(GC):
                em("gpsimd", "indirect_dma_start",
                   LREC.ap(),
                   IndirectOffsetOnAxis(didx_ri[:, r * GC + gc:r * GC + gc + 1], 0),
                   recs[:, (r * GC + gc) * 3:(r * GC + gc) * 3 + 3], None,
                   bounds_check=NR - 1, oob_is_err=False,
                   waits=([("dv", dv_recs), ("zf", zf_total)] if (r == 0 and gc == 0) else []),
                   incs=["sct"])
        sct_total = C["sct"]

        ac_hbank = {0: 0, 1: 0}
        h_done = {}
        dv_ybank = {0: 0, 1: 0, 2: dv_trowbc, 3: dv_trowbc}
        dv_b2bc = {0: 0, 1: 0}
        pe_free = {"w1a0": pe_wvT[0], "w1b0": pe_woT[0], "w1a1": 0, "w1b1": 0,
                   "w2a": pe_u, "w2b": pe_t}
        b1_ac = {0: 0, 1: 0}
        pe_gx = {0: 0, 1: 0}
        gx_done = {0: 0, 1: 0}
        rc_done = {0: 0, 1: 0, 2: 0, 3: 0}
        dv_rec_slot = {0: 0, 1: 0, 2: 0, 3: 0}
        dv_gxT = dv_trbank
        pe_b1use = 0
        pe_b2use = 0
        dv_zt = 0

        def emit_prefetch_a(j):
            sl = j % 4
            rcsem = ("rc0", "rc1", "rc2", "rc3")[sl]
            em("sync", "dma_start",
               out=rec_sb_v[:, sl, :, :],
               in_=A(LREC, j * T3 * 3, [[3, 128], [128 * 3, 2], [1, 3]]),
               waits=([("sct", sct_total)] if j == 0 else [("dv", dv_rec_slot[sl])]),
               incs=[rcsem])
            rc_done[sl] = C[rcsem]

        def emit_prefetch_b(j):
            sl = j % 4
            sl2 = j % 2
            gxsem = "gx0" if sl2 == 0 else "gx1"
            wt0 = [(("rc0", "rc1", "rc2", "rc3")[sl], rc_done[sl])]
            if j >= 4:
                wt0.append(zs_done[j - 4])
            if j >= 2:
                wt0.append((gxsem, gx_done[sl2]))
            em("vector", "tensor_copy", out=tok_i_v[:, sl, :], in_=rec_sb_v[:, sl, :, 0],
               waits=wt0)
            em("vector", "tensor_copy", out=w_f_v[:, sl, :], in_=rec_sb_v[:, sl, :, 1])
            em("vector", "tensor_copy", out=dst_i_v[:, sl, :], in_=rec_sb_v[:, sl, :, 2])
            dv_rec_slot[sl] = C["dv"]
            dv_rec = C["dv"]
            for h in range(2):
                wt = []
                if h == 0:
                    wt = [("dv", dv_rec), ("xs", C["xs"]), ("xs1", C["xs1"])]
                    if pe_gx[sl2]:
                        wt.append(("pe", pe_gx[sl2]))
                em("gpsimd", "indirect_dma_start",
                   gx_v[:, sl2, h * D:(h + 1) * D], None,
                   XNBF.ap(), IndirectOffsetOnAxis(tok_i_v[:, sl, h:h + 1], 0),
                   bounds_check=SH - 1, oob_is_err=False,
                   waits=wt, incs=[gxsem])
            gx_done[sl2] = C[gxsem]

        def emit_b1(j):
            slw = j % 2
            em("gpsimd", "indirect_dma_start",
               b1_t_v[:, slw, :], None,
               B1T.ap(), IndirectOffsetOnAxis(idxb_i[:, j:j + 1], 0),
               bounds_check=128 * E - 1, oob_is_err=False,
               waits=[("dv", dv_idxb), ("xs", C["xs"]), ("xs1", C["xs1"]),
                      ("ac", b1_ac[slw]),
                      ("b1g", 16 * j)],
               incs=["b1g"])

        def emit_weights_w1(j):
            slw = j % 2
            sfx = "" if slw == 0 else "1"
            for wsem, buf, ihalf in (
                    ("w1a" + sfx, (W1a, W1a1)[slw], 0), ("w1b" + sfx, (W1b, W1b1)[slw], 1)):
                wt = [("dv", dv_idxw), ("pe", pe_free[("w1a" if ihalf == 0 else "w1b") + str(slw)])]
                em("gpsimd", "indirect_dma_start",
                   buf[:, :], None,
                   W1.ap(), IndirectOffsetOnAxis(idxw_i[:, ihalf * NT3 + j:ihalf * NT3 + j + 1], 0),
                   bounds_check=E * 128 * 2 - 1, oob_is_err=False,
                   waits=wt, incs=[wsem])

        def emit_weights_w2(j):
            slw = j % 2
            em("gpsimd", "indirect_dma_start",
               b2bc[:, slw * D:(slw + 1) * D], None,
               b2.ap(), IndirectOffsetOnAxis(idxb2_i[:, j:j + 1], 0),
               bounds_check=E - 1, oob_is_err=False,
               waits=[("dv", dv_b2bc[slw]), ("b2g", 16 * j)] +
                     ([("pe", pe_bc1)] if j < 2 else []),
               incs=["b2g"])
            for wsem, buf, ihalf in (("w2a", W2a, 0), ("w2b", W2b, 1)):
                wt = [("dv", dv_idxw), ("pe", pe_free[wsem])]
                em("gpsimd", "indirect_dma_start",
                   buf[:, :], None,
                   W2.ap(), IndirectOffsetOnAxis(idxw2_i[:, ihalf * NT3 + j:ihalf * NT3 + j + 1], 0),
                   bounds_check=E * 128 * 2 - 1, oob_is_err=False,
                   waits=wt, incs=[wsem])

        def emit_gxtr(j):
            nonlocal dv_gxT
            sl = j % 2
            gxsem = "gx0" if sl == 0 else "gx1"
            gx_j = gx_done[sl]
            for h in range(2):
                for c in range(KD):
                    wt = []
                    if c == 0:
                        wt = [(gxsem, gx_j), ("dv", dv_gxT)] if h == 0 else [("dv", dv_gxT)]
                    em("tensor", "transpose", out=ps_tr[:, c * 128:(c + 1) * 128],
                       in_=gx_v[:, sl, h * D + c * 128:h * D + (c + 1) * 128],
                       identity=idb[:, :], waits=wt, incs=(["pe"] if c == KD - 1 else []))
                if h == 1:
                    pe_gx[sl] = C["pe"]
                em("vector", "tensor_copy", out=gxT_v[:, :, h * 128:(h + 1) * 128],
                   in_=ps_tr_v[:, :, :], waits=[("pe", C["pe"])])
                dv_gxT = C["dv"]

        zs_done = {}
        emit_prefetch_a(0)
        emit_prefetch_b(0)
        emit_prefetch_a(1)
        emit_prefetch_b(1)
        emit_b1(0)
        emit_weights_w1(0)
        emit_weights_w1(1)
        emit_weights_w2(0)
        emit_gxtr(0)
        for j in range(NT3):
            sl = j % 2
            gxsem = "gx0" if sl == 0 else "gx1"
            if j + 2 < NT3:
                emit_prefetch_a(j + 2)
            if j >= 1:
                emit_weights_w2(j)
            b1g_j = 16 * (j + 1)
            b2g_j = 16 * (j + 1)
            for m in range(KH):
                bank = m % 2
                wt = []
                if m == 0:
                    wt += [("w1a" + ("" if j % 2 == 0 else "1"), 16 * (j // 2 + 1)),
                           ("dv", dv_gxT), ("b1g", b1g_j)]
                if m == KD:
                    wt.append(("w1b" + ("" if j % 2 == 0 else "1"), 16 * (j // 2 + 1)))
                if ac_hbank[bank] != 0:
                    wt.append(ac_hbank[bank])
                for k in range(KD):
                    em("tensor", "matmul", out=ps_h[bank][:, 0:T3],
                       lhsT=W1vp[j % 2][m // KD][:, k, (m % KD) * 128:(m % KD + 1) * 128],
                       rhs=gxT_v[:, k, :],
                       start=(k == 0), stop=(k == KD - 1),
                       waits=(wt if k == 0 else []),
                       incs=(["pe"] if k == KD - 1 else []))
                if m == KD - 1:
                    pe_free["w1a" + str(j % 2)] = C["pe"]
                if m == KH - 1:
                    pe_free["w1b" + str(j % 2)] = C["pe"]
                pe_h = C["pe"]
                em("scalar", "activation", out=h_bf[:, m * T3:(m + 1) * T3],
                   in_=ps_h[bank][:, 0:T3], func=AF.Relu,
                   bias=b1_t[:, sl * KH + m:sl * KH + m + 1],
                   waits=[("pe", pe_h)], incs=["ac"])
                h_done[m] = ("ac", C["ac"])
                ac_hbank[bank] = h_done[m]
            ac_h_all = C["ac"]
            b1_ac[j % 2] = C["ac"]
            if j + 1 < NT3:
                if j + 2 < NT3:
                    emit_prefetch_b(j + 2)
                emit_b1(j + 1)
                if j + 2 < NT3:
                    emit_weights_w1(j + 2)
                emit_gxtr(j + 1)
            ybanks = [ps_y[0], ps_y[1], ps_b0, ps_b1]
            for h in range(2):
                for nb in range(2):
                    bi = h * 2 + nb
                    wt = []
                    if bi == 0:
                        wt += [("w2a", 16 * (j + 1)), ("ac", ac_h_all)]
                    if dv_ybank[bi]:
                        wt.append(("dv", dv_ybank[bi]))
                    for k in range(KD):
                        em("tensor", "matmul", out=ybanks[bi][:, :],
                           lhsT=hv[:, k, h * 128:(h + 1) * 128],
                           rhs=W2v[0][:, k, nb * 512:(nb + 1) * 512],
                           start=(k == 0), stop=False,
                           waits=(wt if k == 0 else []),
                           incs=(["pe"] if bi == 3 and k == KD - 1 else []))
            pe_free["w2a"] = C["pe"]
            for h in range(2):
                for nb in range(2):
                    bi = h * 2 + nb
                    wt = [("w2b", 16 * (j + 1))] if bi == 0 else []
                    for k in range(KD):
                        em("tensor", "matmul", out=ybanks[bi][:, :],
                           lhsT=hv[:, KD + k, h * 128:(h + 1) * 128],
                           rhs=W2v[1][:, k, nb * 512:(nb + 1) * 512],
                           start=False, stop=(k == KD - 1),
                           waits=(wt if k == 0 else []),
                           incs=(["pe"] if k == KD - 1 else []))
                    if bi == 3:
                        pe_free["w2b"] = C["pe"]
                    pe_ye = C["pe"]
                    wt2 = [("pe", pe_ye), ("b2g", b2g_j)]
                    if dv_zt and bi == 0:
                        wt2.append(zs_done[j - 1])
                    em("vector", "tensor_tensor",
                       out=zt[:, h * D + nb * 512:h * D + (nb + 1) * 512],
                       in0=ybanks[bi][:, :], in1=b2bc[:, sl * D + nb * 512:sl * D + (nb + 1) * 512],
                       op=ALU.add, waits=wt2)
                    em("vector", "tensor_scalar",
                       out=zt[:, h * D + nb * 512:h * D + (nb + 1) * 512],
                       in0=zt[:, h * D + nb * 512:h * D + (nb + 1) * 512],
                       scalar1=w_f_v[:, j % 4, h:h + 1], scalar2=None, op0=ALU.mult)
                    dv_ybank[bi] = C["dv"]
            dv_zt = C["dv"]
            dv_b2bc[sl] = C["dv"]
            zsem = "zs" if j % 2 == 0 else "zs1"
            for h in range(2):
                em("gpsimd", "indirect_dma_start",
                   ZD.ap(), IndirectOffsetOnAxis(dst_i_v[:, j % 4, h:h + 1], 0),
                   zt[:, h * D:(h + 1) * D], None,
                   bounds_check=2 * SH - 1, oob_is_err=False,
                   waits=([("dv", dv_zt), ("zf", zf_total)] if h == 0 else []),
                   incs=[zsem])
            zs_done[j] = (zsem, C[zsem])
        zs_tw = [("zs", C["zs"]), ("zs1", C["zs1"])]

        st_slot = {0: 0, 1: 0}
        dv_zl = {0: 0, 1: 0, 2: 0, 3: 0}
        x4bufs = [g_grp, scr1, zt[:, 0:D], zt[:, D:2 * D]]
        for t in range(NT):
            for g in range(NG):
                gi = t * NG + g
                sl4 = gi % 4
                rcsem = ("rc0", "rc1", "rc2", "rc3")[sl4]
                tokbase = t * TILE + g * 128
                gc4 = t * NG + g
                em("sync", "dma_start",
                   out=x_tv.tensor if False else zn_v[:, 2 * sl4:2 * sl4 + 2, :],
                   in_=A(ZD, 2 * tokbase * D, [[2 * D, 128], [D, 2], [1, D]]),
                   waits=(zs_tw if gi == 0 else []) +
                         ([("dv", dv_zl[sl4])] if dv_zl[sl4] else []),
                   incs=[rcsem])
                em("sync", "dma_start",
                   out=x4bufs[sl4][:, :] if sl4 < 2 else x4bufs[sl4],
                   in_=A(x_in, tokbase * D, [[D, 128], [1, D]]),
                   waits=([("dv", dv_sgrp)] if gi < 2 else []), incs=[rcsem])
                rc_g = C[rcsem]
                z0 = zn_v[:, 2 * sl4, :]
                z1 = zn_v[:, 2 * sl4 + 1, :]
                xb4 = x4bufs[sl4][:, :] if sl4 < 2 else x4bufs[sl4]
                em("vector", "tensor_add", out=z0, in0=z0, in1=z1, waits=[(rcsem, rc_g)])
                dv_y4 = C["dv"]
                em("scalar", "activation", out=z1, in_=z0, func=AF.Silu,
                   waits=[("dv", dv_y4)], incs=["ac"])
                ac_g4 = C["ac"]
                if t == 7 and g == SPECIAL_GROUP:
                    em("vector", "tensor_tensor", out=z0, in0=z1, in1=s_grp[:, :],
                       op=ALU.mult, waits=[("ac", ac_g4)])
                else:
                    em("vector", "tensor_scalar", out=z0, in0=xb4,
                       scalar1=mr_all[:, gc4:gc4 + 1],
                       scalar2=mr_all[:, GC + gc4:GC + gc4 + 1],
                       op0=ALU.subtract, op1=ALU.mult, waits=[("ac", ac_g4)])
                    em("vector", "tensor_tensor", out=z0, in0=z0, in1=z1, op=ALU.mult)
                ssem = "st0" if gi % 2 == 0 else "st1"
                wt = []
                if st_slot[gi % 2]:
                    wt.append((ssem, st_slot[gi % 2]))
                em("vector", "tensor_add", out=out_g[:, (gi % 2) * D:(gi % 2 + 1) * D],
                   in0=z0, in1=xb4, waits=wt)
                dv_zl[sl4] = C["dv"]
                em("scalar", "dma_start", out=A(out, tokbase * D, [[D, 128], [1, D]]),
                   in_=out_g[:, (gi % 2) * D:(gi % 2 + 1) * D],
                   waits=[("dv", dv_zl[sl4])], incs=[ssem])
                st_slot[gi % 2] = C[ssem]

        em("scalar", "activation", out=t_f[:, 0:1], in_=t_f[:, 0:1], func=AF.Copy,
           waits=[("st0", C["st0"]), ("st1", C["st1"])])

        with nc.Block() as block:
            def run(eng_name):
                def body(eng):
                    bndregs = {}
                    if eng_name == "gpsimd":
                        for bv in (NR - 1, SH - 1, 128 * E - 1, 2 * SH - 1,
                                   E * 128 * 2 - 1, E - 1):
                            r = es.enter_context(eng.register(f"bnd{bv}"))
                            eng.reg_mov(r, bv)
                            bndregs[bv] = r
                    for method, args, kw, ws, incs in sched[eng_name]:
                        for s, v in ws:
                            eng.wait_ge(s, v)
                        if "bounds_check" in kw and isinstance(kw["bounds_check"], int):
                            kw = dict(kw, bounds_check=bndregs[kw["bounds_check"]])
                        inst = getattr(eng, method)(*args, **kw)
                        for s in incs:
                            inst.then_inc(s, 16 if "dma_start" in method else 1)
                return body
            block.sync(run("sync"))
            block.gpsimd(run("gpsimd"))
            block.scalar(run("scalar"))
            block.vector(run("vector"))
            block.tensor(run("tensor"))

    return nc


def kernel(**inputs):
    x = np.ascontiguousarray(np.asarray(inputs["L0_attn_inputs"], dtype=np.float32))
    toks = (np.arange(GC)[None, :] * 128 + np.arange(128)[:, None]).astype(np.float32)
    segmask = np.ones((128, E * GC), dtype=np.float32)
    segmask[:, 0::GC] = 0.0
    common = {
        "Wv": np.ascontiguousarray(np.asarray(inputs["L0_attn_Wv"], dtype=np.float32)),
        "Wo": np.ascontiguousarray(np.asarray(inputs["L0_attn_Wo"], dtype=np.float32)),
        "Wg": np.ascontiguousarray(np.asarray(inputs["L0_ffn_Wg"], dtype=np.float32)),
        "W1": np.ascontiguousarray(
            np.asarray(inputs["L0_ffn_W1"], dtype=np.float32)
            .reshape(8, 8, 128, 2, 1024).transpose(0, 3, 2, 1, 4).reshape(8 * 256, 8192)),
        "b1": np.ascontiguousarray(np.asarray(inputs["L0_ffn_b1"], dtype=np.float32)),
        "W2": np.ascontiguousarray(
            np.asarray(inputs["L0_ffn_W2"], dtype=np.float32)
            .reshape(8, 2, 8, 128, 1024).transpose(0, 3, 1, 2, 4).reshape(8 * 256, 8192)),
        "b2": np.ascontiguousarray(np.asarray(inputs["L0_ffn_b2"], dtype=np.float32)),
        "ident": np.eye(128, dtype=np.float32),
        "ltri": np.triu(np.ones((128, 128), dtype=np.float32), 1),
        "ones128": np.ones((128, 1), dtype=np.float32),
        "ones1r": np.ones((1, 128), dtype=np.float32),
        "toksc": toks,
        "pbase": np.stack([np.arange(128) * 128, np.arange(128) * 8, np.arange(128)], axis=1).astype(np.float32),
        "jtc": (np.arange(NT3)[None, :] * T3).astype(np.float32),
        "segm": segmask,
        "onehot": np.repeat(np.eye(8, dtype=np.float32), 128, axis=1),
        "tmask": np.zeros((128, 1), dtype=np.float32),
    }
    tmask_on = np.ones((128, 1), dtype=np.float32)
    nc = build_nc()
    in_maps = []
    for c in range(8):
        b, half = c // 2, c % 2
        m = dict(common)
        m["x"] = np.ascontiguousarray(x[b, half * SH:(half + 1) * SH, :])
        m["tmask"] = tmask_on if half == 1 else common["tmask"]
        in_maps.append(m)
    res = run_bass_kernel_spmd(nc, in_maps, core_ids=list(range(8)))
    outp = np.empty((4, 8192, D), dtype=np.float32)
    for c in range(8):
        b, half = c // 2, c % 2
        outp[b, half * SH:(half + 1) * SH, :] = res.results[c]["out"]
    return outp


# version 48
# speedup vs baseline: 1.0928x; 1.0125x over previous
import contextlib
import numpy as np
import concourse.bass as bass
import concourse.mybir as mybir
from concourse.bass_utils import run_bass_kernel_spmd
from concourse.bass_core_rust import IndirectOffsetOnAxis

F32 = mybir.dt.float32
BF16 = mybir.dt.bfloat16
I32 = mybir.dt.int32
AF = mybir.ActivationFunctionType
ALU = mybir.AluOpType
AX = mybir.AxisListType

D = 1024
E = 8
HID = 2048
SH = 4096
TILE = 512
NT = SH // TILE
NG = TILE // 128
KD = D // 128
KH = HID // 128
GC = NT * NG
T3 = 256
NT3 = (2 * SH) // T3 + E
NR = NT3 * T3
BIGF = float(2 ** 20)
TILE_ORDER = [7, 0, 1, 2, 3, 4, 5, 6]
SPECIAL_COL = 384
SPECIAL_GROUP = 3


def build_nc():
    nc = bass.Bass("TRN2", target_bir_lowering=False)

    x_in = nc.dram_tensor("x", [SH, D], F32, kind="ExternalInput")
    Wv = nc.dram_tensor("Wv", [D, D], F32, kind="ExternalInput")
    Wo = nc.dram_tensor("Wo", [D, D], F32, kind="ExternalInput")
    Wg = nc.dram_tensor("Wg", [D, E], F32, kind="ExternalInput")
    W1 = nc.dram_tensor("W1", [E * 256, KD * 1024], F32, kind="ExternalInput")
    b1 = nc.dram_tensor("b1", [E, HID], F32, kind="ExternalInput")
    W2 = nc.dram_tensor("W2", [E * 256, KH * 512], F32, kind="ExternalInput")
    b2 = nc.dram_tensor("b2", [E, D], F32, kind="ExternalInput")
    ident = nc.dram_tensor("ident", [128, 128], F32, kind="ExternalInput")
    onehot = nc.dram_tensor("onehot", [E, E * 128], F32, kind="ExternalInput")
    tmask = nc.dram_tensor("tmask", [128, 1], F32, kind="ExternalInput")
    ltri = nc.dram_tensor("ltri", [128, 128], F32, kind="ExternalInput")
    ones128 = nc.dram_tensor("ones128", [128, 1], F32, kind="ExternalInput")
    ones1r = nc.dram_tensor("ones1r", [1, 128], F32, kind="ExternalInput")
    toksc = nc.dram_tensor("toksc", [128, GC], F32, kind="ExternalInput")
    pbase = nc.dram_tensor("pbase", [128, 3], F32, kind="ExternalInput")
    jtc = nc.dram_tensor("jtc", [1, NT3], F32, kind="ExternalInput")
    segm = nc.dram_tensor("segm", [128, E * GC], F32, kind="ExternalInput")
    out = nc.dram_tensor("out", [SH, D], F32, kind="ExternalOutput")

    XNBF = nc.dram_tensor("XNBF", [SH, D], BF16)
    B1T = nc.dram_tensor("B1T", [128 * E, KH], F32)
    LREC = nc.dram_tensor("LREC", [NR, 3], F32)
    ZD = nc.dram_tensor("ZD", [2 * SH, D], F32)

    es = contextlib.ExitStack()
    with es:
        sb = lambda n, s, d: es.enter_context(nc.sbuf_tensor(n, s, d))
        pst = lambda n, s, d: es.enter_context(nc.psum_tensor(n, s, d))
        mksem = lambda n: es.enter_context(nc.semaphore(n))

        x_t = sb("x_t", [128, 2 * NG * D], F32)
        xnb = sb("xnb", [128, 2 * NG * D], BF16)
        mr_all = sb("mr_all", [128, 2 * GC], F32)
        xnT = sb("xnT", [128, KD * TILE], BF16)
        xn_col = sb("xn_col", [128, KD], BF16)
        cols = sb("cols", [128, 64], F32)
        lg = sb("lg", [128, NG * E], F32)
        sme = sb("sme", [128, NG * E], F32)
        m8 = sb("m8", [128, NG * E], F32)
        w_t = sb("w_t", [128, NG * E], F32)
        w_all = sb("w_all", [128, E * GC], F32)
        s_grp = sb("s_grp", [128, D], F32)
        scr1 = sb("scr1", [128, D], F32)
        g_grp = sb("g_grp", [128, D], F32)
        out_g = sb("out_g", [128, 2 * D], F32)
        sel = sb("sel", [128, E * GC], F32)
        incl = sb("incl", [128, E * GC], F32)
        slot = sb("slot", [128, E * GC], F32)
        mm1 = sb("mm1", [128, E * GC], F32)
        mm2 = sb("mm2", [128, E * GC], F32)
        didx = sb("didx", [128, E * GC], F32)
        runsb = sb("runsb", [128, GC], F32)
        cnt = sb("cnt", [128, E], F32)
        base_pe = sb("base_pe", [128, E], F32)
        gbase = sb("gbase", [128, E], F32)
        tot = sb("tot", [1, E], F32)
        tr8 = sb("tr8", [1, 3 * E], F32)
        tr8i = sb("tr8i", [1, 2 * E], I32)
        etile = sb("etile", [1, 4 * NT3], F32)
        ebf = sb("ebf", [128, NT3], F32)
        dbf = sb("dbf", [128, NT3], F32)
        dbf2 = sb("dbf2", [128, NT3], F32)
        wrow = sb("wrow", [128, NT3], F32)
        idxw = sb("idxw", [128, 2 * NT3], F32)
        idxw_i = sb("idxw_i", [128, 2 * NT3], I32)
        idxw2 = sb("idxw2", [128, 2 * NT3], F32)
        idxw2_i = sb("idxw2_i", [128, 2 * NT3], I32)
        idxb = sb("idxb", [128, NT3], F32)
        idxb_i = sb("idxb_i", [128, NT3], I32)
        idxb2_i = sb("idxb2_i", [128, NT3], I32)
        didx_r = sb("didx_r", [128, 2 * GC], F32)
        didx_ri = sb("didx_ri", [128, 2 * GC], I32)
        w_r = sb("w_r", [128, 2 * GC], F32)
        dstc = sb("dstc", [128, 2 * GC], F32)
        recs = sb("recs", [128, 2 * GC * 3], F32)
        acc_s = sb("acc_s", [128, GC], F32)
        W1a = sb("W1a", [128, KD * 1024], BF16)
        W1b = sb("W1b", [128, KD * 1024], BF16)
        W1a1 = sb("W1a1", [128, KD * 1024], BF16)
        W1b1 = sb("W1b1", [128, KD * 1024], BF16)
        W2a = sb("W2a", [128, KD * 1024], BF16)
        W2b = sb("W2b", [128, KD * 1024], BF16)
        zt = sb("zt", [128, 2 * D], F32)
        rec_sb = sb("rec_sb", [128, 4 * 6], F32)
        tok_i = sb("tok_i", [128, 4 * 2], I32)
        dst_i = sb("dst_i", [128, 4 * 2], I32)
        w_f = sb("w_f", [128, 4 * 2], F32)
        b1_t = sb("b1_t", [128, 2 * KH], F32)
        b2bc = sb("b2bc", [128, 2 * D], F32)
        u_bf = sb("u_bf", [128, E], BF16)
        t_f = sb("t_f", [128, E], F32)
        t8 = sb("t8", [8, 128], F32)
        b1_sb = sb("b1_sb", [128, E * KH], F32)
        wg_f = sb("wg_f", [128, KD * E], F32)
        wg_b = sb("wg_b", [128, KD * E], BF16)
        idf = sb("idf", [128, 128], F32)
        idb = sb("idb", [128, 128], BF16)
        tm_sb = sb("tm_sb", [128, 1], F32)
        eps_c = sb("eps_c", [128, 1], F32)
        mh_c = sb("mh_c", [128, 1], F32)
        ltri_sb = sb("ltri_sb", [128, 128], F32)
        o128 = sb("o128", [128, 1], F32)
        o1r = sb("o1r", [1, 128], F32)
        toks_sb = sb("toks_sb", [128, GC], F32)
        pbase_sb = sb("pbase_sb", [128, 3], F32)
        jtc_sb = sb("jtc_sb", [1, NT3], F32)
        segm_sb = sb("segm_sb", [128, E * GC], F32)


        ps_h = [pst("ps_h0", [128, 512], F32), pst("ps_h1", [128, 512], F32)]
        ps_y = [pst("ps_y0", [128, 512], F32), pst("ps_y1", [128, 512], F32)]
        ps_tr = pst("ps_tr", [128, 1024], BF16)
        ps_b0 = pst("ps_b0", [128, 512], F32)
        ps_b1 = pst("ps_b1", [128, 512], F32)
        ps_m = pst("ps_m", [128, 512], F32)

        SEM = {k: mksem("s_" + k) for k in (
            "ld", "sd", "xs", "zf", "cvo", "cast", "sct", "rc0", "rc1", "rc2", "rc3",
            "gx0", "gx1", "w1a", "w1b", "w1a1", "w1b1", "w2a", "w2b", "b1g", "b2g",
            "zs", "zs1", "sd1", "xs1",
            "st0", "st1", "pe", "dv", "ac")}
        C = {k: 0 for k in SEM}
        DMA = {k for k in SEM if k not in ("pe", "dv", "ac")}

        sched = {k: [] for k in ("sync", "gpsimd", "scalar", "vector", "tensor")}

        def em(eng, method, *args, waits=(), incs=(), **kw):
            waits = list(waits)
            if eng == "vector" and C["dv"] > 0:
                waits.append(("dv", C["dv"]))
            ws = [(SEM[n], v) for (n, v) in waits if v > 0]
            if eng == "vector" and "dv" not in incs:
                incs = list(incs) + ["dv"]
            sched[eng].append((method, args, kw, ws, [SEM[n] for n in incs]))
            for n in incs:
                C[n] += 16 if n in DMA else 1
            return {n: C[n] for n in incs}

        A = bass.AP
        x_tv = x_t.ap().rearrange("p (s g d) -> p s g d", s=2, g=NG)
        xnT_v = xnT.ap().rearrange("p (c t2) -> p c t2", c=KD)
        ps_tr_v = ps_tr.ap().rearrange("p (c j) -> p c j", c=KD)
        WvN = W1a.ap().rearrange("p (r c j) -> p r c j", r=KD, c=KD)
        WoN = W1b.ap().rearrange("p (r c j) -> p r c j", r=KD, c=KD)
        WvT = W2a.ap().rearrange("p (k m j) -> p k m j", k=KD, m=KD)
        WoT = W2b.ap().rearrange("p (k m j) -> p k m j", k=KD, m=KD)
        W1vp = [[W1a.ap().rearrange("p (k h) -> p k h", k=KD),
                 W1b.ap().rearrange("p (k h) -> p k h", k=KD)],
                [W1a1.ap().rearrange("p (k h) -> p k h", k=KD),
                 W1b1.ap().rearrange("p (k h) -> p k h", k=KD)]]
        W2v = [W2a.ap().rearrange("p (k n) -> p k n", k=KD),
               W2b.ap().rearrange("p (k n) -> p k n", k=KD)]
        w_all_v = w_all.ap().rearrange("p (e gc) -> p e gc", e=E)
        incl_v = incl.ap().rearrange("p (e gc) -> p e gc", e=E)
        recs_v = recs.ap().rearrange("p (r gc c3) -> p r gc c3", r=2, gc=GC)
        rec_sb_v = rec_sb.ap().rearrange("p (s h c3) -> p s h c3", s=4, h=2)
        tok_i_v = tok_i.ap().rearrange("p (s h) -> p s h", s=4)
        dst_i_v = dst_i.ap().rearrange("p (s h) -> p s h", s=4)
        w_f_v = w_f.ap().rearrange("p (s h) -> p s h", s=4)
        b1_t_v = b1_t.ap().rearrange("p (s m) -> p s m", s=2)
        gx_v = xnb.ap().rearrange("p (s q) -> p s q", s=4)
        gxT_v = xnb.ap().rearrange("p (s c t2) -> p s c t2", s=4, c=KD)[:, 2, :, :]
        h_bf = xnT
        hv = h_bf.ap().rearrange("p (k t2) -> p k t2", k=KH)
        zn_v = x_t.ap().rearrange("p (s d2) -> p s d2", s=8)

        em("sync", "dma_start", out=idf[:, :], in_=ident[:, :], incs=["ld"])
        em("sync", "dma_start", out=tm_sb[:, :], in_=tmask[:, :], incs=["ld"])
        em("sync", "dma_start", out=ltri_sb[:, :], in_=ltri[:, :], incs=["ld"])
        em("sync", "dma_start", out=o128[:, :], in_=ones128[:, :], incs=["ld"])
        em("sync", "dma_start", out=o1r[:, :], in_=ones1r[:, :], incs=["ld"])
        em("sync", "dma_start", out=toks_sb[:, :], in_=toksc[:, :], incs=["ld"])
        em("sync", "dma_start", out=pbase_sb[:, :], in_=pbase[:, :], incs=["ld"])
        em("sync", "dma_start", out=jtc_sb[:, :], in_=jtc[:, :], incs=["ld"])
        em("sync", "dma_start", out=segm_sb[:, :], in_=segm[:, :], incs=["ld"])
        em("sync", "dma_start", out=wg_f.ap().rearrange("p (k e) -> p k e", k=KD),
           in_=A(Wg, 0, [[E, 128], [128 * E, KD], [1, E]]), incs=["ld"])
        b_raw = out_g
        em("sync", "dma_start", out=b_raw[0:16, 0:E * 128].rearrange("i (e j) -> i e j", e=E),
           in_=A(b1, 0, [[128, 16], [HID, E], [1, 128]]), incs=["ld"])
        oh_sb = b2bc
        em("sync", "dma_start", out=oh_sb[0:8, 0:E * 128], in_=onehot[:, :], incs=["ld"])
        ld_consts = C["ld"]

        em("vector", "memset", zt[:, :], 0.0)
        dv_z0 = C["dv"]
        for q in range(32):
            em("sync", "dma_start",
               out=A(ZD, q * 256 * D, [[2 * D, 128], [D, 2], [1, D]]),
               in_=zt.ap().rearrange("p (h d2) -> p h d2", h=2),
               waits=([("dv", dv_z0)] if q == 0 else []), incs=["zf"])
        em("vector", "memset", recs[:, 0:192], BIGF)
        em("vector", "memset", didx[:, 0:48], BIGF)
        dv_lrp = C["dv"]
        em("sync", "dma_start", out=A(LREC, 0, [[192, 128], [1, 192]]),
           in_=recs[:, 0:192], waits=[("dv", dv_lrp)], incs=["zf"])
        em("sync", "dma_start", out=A(LREC, 128 * 192, [[48, 128], [1, 48]]),
           in_=didx[:, 0:48], incs=["zf"])
        zf_total = C["zf"]

        em("gpsimd", "dma_start", out=W1a.ap().rearrange("p (r d2) -> p r d2", r=KD),
           in_=A(Wv, 0, [[D, 128], [128 * D, KD], [1, D]]), incs=["cvo"])
        cvo_wv = C["cvo"]
        em("gpsimd", "dma_start", out=W1b.ap().rearrange("p (r d2) -> p r d2", r=KD),
           in_=A(Wo, 0, [[D, 128], [128 * D, KD], [1, D]]), incs=["cvo"])
        cvo_wo = C["cvo"]


        em("vector", "memset", eps_c[:, :], 1e-5)
        em("vector", "memset", mh_c[:, :], -0.5)
        em("scalar", "activation", out=idb[:, :], in_=idf[:, :], func=AF.Copy,
           waits=[("ld", ld_consts)], incs=["ac"])
        ac_idb = C["ac"]
        em("scalar", "activation", out=wg_b[:, :], in_=wg_f[:, :], func=AF.Copy, incs=["ac"])
        ac_wg = C["ac"]

        for e in range(E):
            em("tensor", "transpose", out=ps_b0[:, e * 16:(e + 1) * 16],
               in_=b_raw[0:16, e * 128:(e + 1) * 128], identity=idf[0:16, 0:16],
               waits=([("ld", ld_consts), ("ac", ac_idb)] if e == 0 else []),
               incs=(["pe"] if e == E - 1 else []))
        pe_b1t = C["pe"]
        em("vector", "tensor_copy", out=b1_sb[:, :], in_=ps_b0[:, 0:128],
           waits=[("pe", pe_b1t)])
        dv_b1 = C["dv"]
        em("sync", "dma_start", out=A(B1T, 0, [[E * KH, 128], [1, E * KH]]),
           in_=b1_sb[:, :], waits=[("dv", dv_b1)], incs=["xs"])

        dv_trbank = 0
        ac_trbank = 0
        tr_last = {0: ("ac", 0), 1: ("dv", 0)}
        pe_wvT = [0]
        pe_woT = [0]

        def emit_wvwo():
            nonlocal dv_trbank
            for c in range(KD):
                for r in range(KD):
                    wt = []
                    if c == 0 and r == 0:
                        wt = [("cvo", cvo_wo), ("ac", ac_idb), tr_last[0], tr_last[1]]
                    if r == 0 and dv_trbank:
                        wt.append(("dv", dv_trbank))
                    em("tensor", "transpose", out=ps_tr[:, r * 128:(r + 1) * 128],
                       in_=WvN[:, r, c, :], identity=idb[:, :], waits=wt,
                       incs=(["pe"] if r == KD - 1 else []))
                em("vector", "tensor_copy", out=WvT[:, c, :, :], in_=ps_tr_v[:, :, :],
                   waits=[("pe", C["pe"])])
                dv_trbank = C["dv"]
            pe_wvT[0] = C["pe"]
            for c in range(KD):
                for r in range(KD):
                    wt = []
                    if c == 0 and r == 0:
                        wt = [("cvo", cvo_wo)]
                    if r == 0:
                        wt.append(("dv", dv_trbank))
                    em("tensor", "transpose", out=ps_tr[:, r * 128:(r + 1) * 128],
                       in_=WoN[:, r, c, :], identity=idb[:, :], waits=wt,
                       incs=(["pe"] if r == KD - 1 else []))
                em("vector", "tensor_copy", out=WoT[:, c, :, :], in_=ps_tr_v[:, :, :],
                   waits=[("pe", C["pe"])])
                dv_trbank = C["dv"]
            pe_woT[0] = C["pe"]

        pe_xnb_free = {0: 0, 1: 0}
        dv_xfree = {0: 0, 1: 0}
        ac_xfree = {0: 0, 1: 0}
        xs_xnb = {0: 0, 1: 0}
        dv_lgbank = {0: dv_b1, 1: 0}
        pe_lg_done = {}
        dv_m = 0
        dv_xncol = 0

        def emit_softmax(t):
            par = (TILE_ORDER.index(t)) % 2
            pbank = ps_b0 if par == 0 else ps_b1
            em("vector", "tensor_copy", out=lg[:, :], in_=pbank[:, 0:NG * E],
               waits=[("pe", pe_lg_done[t])])
            dv_lgbank[par] = C["dv"]
            for g in range(NG):
                em("vector", "reduce_max", out=cols[:, 28 + g:29 + g],
                   in_=lg[:, g * E:(g + 1) * E], axis=AX.X)
            for g in range(NG):
                em("vector", "tensor_scalar", out=sme[:, g * E:(g + 1) * E],
                   in0=lg[:, g * E:(g + 1) * E], scalar1=cols[:, 28 + g:29 + g],
                   scalar2=None, op0=ALU.subtract)
            dvp = C["dv"]
            em("scalar", "activation", out=sme[:, :], in_=sme[:, :], func=AF.Exp,
               waits=[("dv", dvp)], incs=["ac"])
            acp = C["ac"]
            for g in range(NG):
                em("vector", "reduce_sum", out=cols[:, 28 + g:29 + g],
                   in_=sme[:, g * E:(g + 1) * E], axis=AX.X,
                   waits=([("ac", acp)] if g == 0 else []))
            em("vector", "reciprocal", out=cols[:, 28:32], in_=cols[:, 28:32])
            for g in range(NG):
                em("vector", "tensor_scalar", out=sme[:, g * E:(g + 1) * E],
                   in0=sme[:, g * E:(g + 1) * E], scalar1=cols[:, 28 + g:29 + g],
                   scalar2=None, op0=ALU.mult)
            for g in range(NG):
                em("vector", "max", out=m8[:, g * E:(g + 1) * E],
                   in_=sme[:, g * E:(g + 1) * E])
            for g in range(NG):
                em("vector", "tensor_scalar", out=w_t[:, g * E:(g + 1) * E],
                   in0=sme[:, g * E:(g + 1) * E], scalar1=m8[:, g * E + 1:g * E + 2],
                   scalar2=None, op0=ALU.is_ge)
            em("vector", "tensor_tensor", out=w_t[:, :], in0=w_t[:, :], in1=sme[:, :],
               op=ALU.mult)
            em("vector", "tensor_copy", out=w_all_v[:, :, t * NG:(t + 1) * NG],
               in_=w_t.ap().rearrange("p (g e) -> p e g", g=NG))

        sd_tile = {}
        sdsem = lambda p: "sd" if p == 0 else "sd1"
        dv_nmr_t = {}

        def emit_stats(ti):
            t = TILE_ORDER[ti]
            par = ti % 2
            co = par * 32
            sd_x = sd_tile[ti]
            xsl = x_t[:, par * NG * D:(par + 1) * NG * D]
            em("vector", "reduce_sum", out=cols[:, co:co + 4],
               in_=x_tv[:, par, :, :], axis=AX.X, waits=[sd_x])
            dv_xfree[par] = C["dv"]
            for g in range(NG):
                wt = [sd_x] if g == 0 else [("ac", C["ac"])]
                em("scalar", "activation", out=g_grp[:, :],
                   in_=xsl[:, g * D:(g + 1) * D],
                   func=AF.Square, accum_out=cols[:, co + 4 + g:co + 5 + g],
                   waits=wt, incs=["ac"])
            ac_ssq = C["ac"]
            em("vector", "tensor_scalar", out=cols[:, co + 8:co + 12],
               in0=cols[:, co:co + 4], scalar1=1.0 / D, scalar2=None, op0=ALU.mult)
            em("vector", "tensor_tensor", out=cols[:, co + 12:co + 16],
               in0=cols[:, co + 8:co + 12], in1=cols[:, co + 8:co + 12], op=ALU.mult)
            em("vector", "scalar_tensor_tensor", out=cols[:, co + 16:co + 20],
               in0=cols[:, co + 4:co + 8], scalar=1.0 / D, in1=cols[:, co + 12:co + 16],
               op0=ALU.mult, op1=ALU.subtract, waits=[("ac", ac_ssq)])
            dv_var = C["dv"]
            em("scalar", "activation", out=cols[:, co + 16:co + 20],
               in_=cols[:, co + 16:co + 20], func=AF.Ln,
               bias=eps_c[:, 0:1], waits=[("dv", dv_var)], incs=["ac"])
            em("scalar", "activation", out=cols[:, co + 20:co + 24],
               in_=cols[:, co + 16:co + 20], func=AF.Exp,
               scale=mh_c[:, 0:1], incs=["ac"])
            ac_rstd = C["ac"]
            em("vector", "scalar_tensor_tensor", out=cols[:, co + 24:co + 28],
               in0=cols[:, co + 8:co + 12], scalar=-1.0, in1=cols[:, co + 20:co + 24],
               op0=ALU.mult, op1=ALU.mult, waits=[("ac", ac_rstd)])
            dv_nmr_t[ti] = C["dv"]
            em("vector", "tensor_copy", out=mr_all[:, t * NG:(t + 1) * NG],
               in_=cols[:, co + 8:co + 12])
            em("vector", "tensor_copy", out=mr_all[:, GC + t * NG:GC + (t + 1) * NG],
               in_=cols[:, co + 20:co + 24])

        em("scalar", "dma_start", out=x_tv[:, 0, :, :],
           in_=A(x_in, TILE_ORDER[0] * TILE * D, [[D, 128], [128 * D, NG], [1, D]]),
           incs=[sdsem(0)])
        sd_tile[0] = (sdsem(0), C[sdsem(0)])
        emit_stats(0)
        for ti, t in enumerate(TILE_ORDER):
            par = ti % 2
            base = t * TILE
            co = par * 32
            sd_x = sd_tile[ti]
            xsl = x_t[:, par * NG * D:(par + 1) * NG * D]
            if ti + 1 < NT:
                t2 = TILE_ORDER[ti + 1]
                p2 = (ti + 1) % 2
                em("scalar", "dma_start", out=x_tv[:, p2, :, :],
                   in_=A(x_in, t2 * TILE * D, [[D, 128], [128 * D, NG], [1, D]]),
                   waits=([("dv", dv_xfree[p2]), ("ac", ac_xfree[p2])] if ti >= 1 else []),
                   incs=[sdsem(p2)])
                sd_tile[ti + 1] = (sdsem(p2), C[sdsem(p2)])
            dv_nmr = dv_nmr_t[ti]
            ac_xnb = {}
            lbank = ps_b0 if par == 0 else ps_b1
            for g in range(NG):
                wt = [("dv", dv_nmr)] if g == 0 else []
                if g == 0:
                    if pe_xnb_free[par]:
                        wt.append(("pe", pe_xnb_free[par]))
                    if xs_xnb[par]:
                        wt.append(xs_xnb[par])
                em("scalar", "activation",
                   out=xnb[:, (par * NG + g) * D:(par * NG + g + 1) * D],
                   in_=xsl[:, g * D:(g + 1) * D], func=AF.Identity,
                   bias=cols[:, co + 24 + g:co + 25 + g],
                   scale=cols[:, co + 20 + g:co + 21 + g],
                   waits=wt, incs=["ac"])
                ac_xnb[g] = C["ac"]
                for c in range(KD):
                    wt = []
                    if c == 0:
                        wt = [("ac", ac_xnb[g]), tr_last[0], tr_last[1]]
                        if dv_trbank:
                            wt.append(("dv", dv_trbank))
                    em("tensor", "transpose", out=ps_tr[:, c * 128:(c + 1) * 128],
                       in_=xnb[:, (par * NG + g) * D + c * 128:(par * NG + g) * D + (c + 1) * 128],
                       identity=idb[:, :], waits=wt, incs=(["pe"] if c == KD - 1 else []))
                pe_xnb_free[par] = C["pe"]
                em("scalar", "activation", out=xnT_v[:, :, g * 128:(g + 1) * 128],
                   in_=ps_tr_v[:, :, :], func=AF.Copy,
                   waits=[("pe", C["pe"])], incs=["ac"])
                trb = ("ac", C["ac"])
                ac_trbank = C["ac"]
                tr_last[g % 2] = trb
                for k in range(KD):
                    wt = []
                    if k == 0:
                        wt = [trb]
                        if g == 0 and dv_lgbank[par]:
                            wt.append(("dv", dv_lgbank[par]))
                    em("tensor", "matmul", out=lbank[:, g * E:(g + 1) * E],
                       lhsT=xnT[:, k * TILE + g * 128:k * TILE + (g + 1) * 128],
                       rhs=wg_b[:, k * E:(k + 1) * E], start=(k == 0), stop=(k == KD - 1),
                       waits=wt, incs=(["pe"] if k == KD - 1 else []))
            pe_lg_done[t] = C["pe"]
            ac_xfree[par] = C["ac"]
            if ti + 1 < NT:
                emit_stats(ti + 1)
            xssem = "xs" if par == 0 else "xs1"
            em("sync", "dma_start",
               out=A(XNBF, base * D, [[D, 128], [128 * D, NG], [1, D]]),
               in_=xnb.ap().rearrange("p (s g d) -> p s g d", s=2, g=NG)[:, par],
               waits=[("ac", ac_xnb[NG - 1])], incs=[xssem])
            xs_xnb[par] = (xssem, C[xssem])
            if t == 7:
                em("vector", "tensor_copy", out=xn_col[:, :],
                   in_=xnT_v[:, :, SPECIAL_COL:SPECIAL_COL + 1],
                   waits=[tr_last[0], tr_last[1]])
                dv_xncol = C["dv"]
            if ti > 0:
                emit_softmax(TILE_ORDER[ti - 1])
            if ti == 1:
                emit_wvwo()
        emit_softmax(TILE_ORDER[-1])
        dv_wall = C["dv"]

        for m in range(KD):
            for k in range(KD):
                wt = [("dv", dv_trbank), ("dv", dv_m), ("dv", dv_xncol)] \
                    if (m == 0 and k == 0) else []
                em("tensor", "matmul", out=ps_m[:, 40 + m:41 + m],
                   lhsT=WvT[:, k, m, :], rhs=xn_col[:, k:k + 1],
                   start=(k == 0), stop=(k == KD - 1), waits=wt,
                   incs=(["pe"] if (m == KD - 1 and k == KD - 1) else []))
        pe_u = C["pe"]
        em("scalar", "activation", out=u_bf[:, :], in_=ps_m[:, 40:48], func=AF.Copy,
           waits=[("pe", pe_u)], incs=["ac"])
        ac_u = C["ac"]
        for m in range(KD):
            for k in range(KD):
                wt = [("ac", ac_u)] if (m == 0 and k == 0) else []
                em("tensor", "matmul", out=ps_m[:, 48 + m:49 + m],
                   lhsT=WoT[:, k, m, :], rhs=u_bf[:, k:k + 1],
                   start=(k == 0), stop=(k == KD - 1), waits=wt,
                   incs=(["pe"] if (m == KD - 1 and k == KD - 1) else []))
        pe_t = C["pe"]
        em("scalar", "activation", out=t_f[:, :], in_=ps_m[:, 48:56], func=AF.Copy,
           waits=[("pe", pe_t)], incs=["ac"])
        ac_tf = C["ac"]
        em("tensor", "transpose", out=ps_b0[0:8, 384:512], in_=t_f[:, :],
           identity=idf[:, :], waits=[("ac", ac_tf), ("dv", dv_lgbank[0]), ("dv", dv_lgbank[1])],
           incs=["pe"])
        em("vector", "tensor_copy", out=t8[:, :], in_=ps_b0[0:8, 384:512],
           waits=[("pe", C["pe"])])
        dv_t8 = C["dv"]
        for m in range(KD):
            pbank = ps_b0 if m < 4 else ps_b1
            wt = [("dv", dv_t8), ("ld", ld_consts)] if m in (0, 4) else []
            em("tensor", "matmul", out=pbank[:, (m % 4) * 128:(m % 4 + 1) * 128],
               lhsT=oh_sb[0:8, m * 128:(m + 1) * 128], rhs=t8[:, :],
               start=True, stop=True, waits=wt, incs=(["pe"] if m in (3, 7) else []))
            if m == 3:
                pe_bc0 = C["pe"]
        pe_bc1 = C["pe"]
        em("vector", "tensor_copy", out=scr1[:, 0:512], in_=ps_b0[:, 0:512],
           waits=[("pe", pe_bc0)])
        em("vector", "tensor_copy", out=scr1[:, 512:1024], in_=ps_b1[:, 0:512],
           waits=[("pe", pe_bc1)])
        dv_trowbc = C["dv"]
        em("scalar", "dma_start", out=g_grp[:, :],
           in_=A(x_in, (7 * TILE + SPECIAL_GROUP * 128) * D, [[D, 128], [1, D]]),
           waits=[("ac", C["ac"])], incs=["sd"])
        sd_sg = C["sd"]
        gcs = 7 * NG + SPECIAL_GROUP
        em("vector", "tensor_scalar", out=g_grp[:, :], in0=g_grp[:, :],
           scalar1=mr_all[:, gcs:gcs + 1], scalar2=mr_all[:, GC + gcs:GC + gcs + 1],
           op0=ALU.subtract, op1=ALU.mult, waits=[("sd", sd_sg)])
        em("vector", "tensor_sub", out=s_grp[:, :], in0=scr1[:, :], in1=g_grp[:, :],
           waits=[("dv", dv_trowbc)])
        em("vector", "tensor_scalar", out=s_grp[:, :], in0=s_grp[:, :],
           scalar1=tm_sb[:, 0:1], scalar2=None, op0=ALU.mult)
        em("vector", "tensor_add", out=s_grp[:, :], in0=s_grp[:, :], in1=g_grp[:, :])
        dv_sgrp = C["dv"]

        em("vector", "tensor_scalar", out=sel[:, :], in0=w_all[:, :],
           scalar1=0.0, scalar2=None, op0=ALU.is_gt, waits=[("dv", dv_wall)])
        em("vector", "tensor_tensor_scan", out=incl[:, :], data0=segm_sb[:, :],
           data1=sel[:, :], initial=0.0, op0=ALU.mult, op1=ALU.add,
           waits=[("ld", ld_consts)])
        em("vector", "tensor_copy", out=cnt[:, :], in_=incl_v[:, :, GC - 1:GC])
        dv_cnt = C["dv"]
        em("tensor", "matmul", out=ps_m[:, 64:72], lhsT=ltri_sb[:, :], rhs=cnt[:, :],
           start=True, stop=True, waits=[("dv", dv_cnt)], incs=["pe"])
        em("vector", "tensor_copy", out=base_pe[:, :], in_=ps_m[:, 64:72],
           waits=[("pe", C["pe"])])
        em("tensor", "matmul", out=ps_m[0:1, 72:80], lhsT=o128[:, :], rhs=cnt[:, :],
           start=True, stop=True, waits=[("dv", C["dv"])], incs=["pe"])
        em("vector", "tensor_copy", out=tot[:, :], in_=ps_m[0:1, 72:80],
           waits=[("pe", C["pe"])])
        em("vector", "tensor_scalar", out=tr8[:, 0:E], in0=tot[:, :],
           scalar1=float(T3 - 1), scalar2=None, op0=ALU.add)
        em("vector", "tensor_copy", out=tr8i[:, 0:E], in_=tr8[:, 0:E])
        em("vector", "tensor_single_scalar", out=tr8i[:, E:2 * E], in_=tr8i[:, 0:E],
           scalar=T3 - 1, op=ALU.bitwise_and)
        em("vector", "tensor_copy", out=tr8[:, E:2 * E], in_=tr8i[:, E:2 * E])
        em("vector", "tensor_sub", out=tr8[:, 0:E], in0=tr8[:, 0:E], in1=tr8[:, E:2 * E])
        em("vector", "tensor_scalar", out=tr8[:, 0:E], in0=tr8[:, 0:E],
           scalar1=float(T3), scalar2=None, op0=ALU.max)
        em("vector", "memset", tr8[:, 2 * E:3 * E], 1.0)
        em("vector", "tensor_tensor_scan", out=tr8[:, E:2 * E], data0=tr8[:, 2 * E:3 * E],
           data1=tr8[:, 0:E], initial=0.0, op0=ALU.mult, op1=ALU.add)
        em("vector", "tensor_sub", out=tr8[:, 2 * E:3 * E], in0=tr8[:, E:2 * E],
           in1=tr8[:, 0:E])
        dv_seg = C["dv"]
        em("vector", "memset", etile[:, 0:NT3], 0.0)
        for e in range(E):
            em("vector", "tensor_scalar", out=etile[:, NT3:2 * NT3], in0=jtc_sb[:, :],
               scalar1=tr8[:, E + e:E + e + 1], scalar2=None, op0=ALU.is_ge)
            em("vector", "tensor_add", out=etile[:, 0:NT3], in0=etile[:, 0:NT3],
               in1=etile[:, NT3:2 * NT3])
        dv_etile = C["dv"]
        em("vector", "memset", etile[:, NT3:NT3 + 1], -1.0)
        em("vector", "tensor_copy", out=etile[:, NT3 + 1:2 * NT3], in_=etile[:, 0:NT3 - 1])
        em("vector", "tensor_tensor", out=etile[:, 2 * NT3:3 * NT3], in0=etile[:, 0:NT3],
           in1=etile[:, NT3:2 * NT3], op=ALU.not_equal)
        em("vector", "tensor_scalar", out=etile[:, 3 * NT3:4 * NT3], in0=etile[:, 0:NT3],
           scalar1=float(E - 1), scalar2=None, op0=ALU.is_le)
        em("vector", "tensor_tensor", out=etile[:, 2 * NT3:3 * NT3],
           in0=etile[:, 2 * NT3:3 * NT3], in1=etile[:, 3 * NT3:4 * NT3], op=ALU.mult)
        dv_diff = C["dv"]
        em("tensor", "matmul", out=ps_m[:, 80:80 + NT3], lhsT=o1r[:, :],
           rhs=etile[0:1, 0:NT3], start=True, stop=True,
           waits=[("dv", dv_etile)], incs=["pe"])
        em("vector", "tensor_copy", out=ebf[:, :], in_=ps_m[:, 80:80 + NT3],
           waits=[("pe", C["pe"])])
        dv_ebf = C["dv"]
        em("tensor", "matmul", out=ps_m[:, 80 + NT3:80 + 2 * NT3], lhsT=o1r[:, :],
           rhs=etile[0:1, 2 * NT3:3 * NT3], start=True, stop=True,
           waits=[("dv", dv_diff), ("dv", dv_ebf)], incs=["pe"])
        em("vector", "tensor_copy", out=dbf[:, :], in_=ps_m[:, 80 + NT3:80 + 2 * NT3],
           waits=[("pe", C["pe"])])
        dv_dbf = C["dv"]
        em("vector", "memset", etile[:, NT3:NT3 + 2], -1.0)
        em("vector", "tensor_copy", out=etile[:, NT3 + 2:2 * NT3], in_=etile[:, 0:NT3 - 2])
        em("vector", "tensor_tensor", out=etile[:, 2 * NT3:3 * NT3], in0=etile[:, 0:NT3],
           in1=etile[:, NT3:2 * NT3], op=ALU.not_equal)
        em("vector", "tensor_tensor", out=etile[:, 2 * NT3:3 * NT3],
           in0=etile[:, 2 * NT3:3 * NT3], in1=etile[:, 3 * NT3:4 * NT3], op=ALU.mult)
        dv_diff2 = C["dv"]
        em("tensor", "matmul", out=ps_m[:, 80 + NT3:80 + 2 * NT3], lhsT=o1r[:, :],
           rhs=etile[0:1, 2 * NT3:3 * NT3], start=True, stop=True,
           waits=[("dv", dv_diff2), ("dv", dv_dbf)], incs=["pe"])
        em("vector", "tensor_copy", out=dbf2[:, :], in_=ps_m[:, 80 + NT3:80 + 2 * NT3],
           waits=[("pe", C["pe"])])
        em("vector", "tensor_scalar", out=wrow[:, :], in0=ebf[:, :],
           scalar1=float(E - 1), scalar2=256.0, op0=ALU.min, op1=ALU.mult)
        em("vector", "tensor_scalar", out=wrow[:, :], in0=wrow[:, :],
           scalar1=pbase_sb[:, 2:3], scalar2=None, op0=ALU.add)
        for half in range(2):
            em("vector", "tensor_scalar", out=idxw[:, half * NT3:(half + 1) * NT3],
               in0=wrow[:, :], scalar1=float(half * 128) - BIGF, scalar2=None, op0=ALU.add)
            em("vector", "tensor_tensor", out=idxw[:, half * NT3:(half + 1) * NT3],
               in0=idxw[:, half * NT3:(half + 1) * NT3], in1=dbf2[:, :], op=ALU.mult)
            em("vector", "tensor_scalar", out=idxw[:, half * NT3:(half + 1) * NT3],
               in0=idxw[:, half * NT3:(half + 1) * NT3], scalar1=BIGF, scalar2=None,
               op0=ALU.add)
        em("vector", "tensor_copy", out=idxw_i[:, :], in_=idxw[:, :])
        em("vector", "tensor_scalar", out=wrow[:, :], in0=ebf[:, :],
           scalar1=float(E - 1), scalar2=128.0, op0=ALU.min, op1=ALU.mult)
        em("vector", "tensor_scalar", out=wrow[:, :], in0=wrow[:, :],
           scalar1=pbase_sb[:, 2:3], scalar2=2.0, op0=ALU.add, op1=ALU.mult)
        for half in range(2):
            em("vector", "tensor_scalar", out=idxw2[:, half * NT3:(half + 1) * NT3],
               in0=wrow[:, :], scalar1=float(half) - BIGF, scalar2=None, op0=ALU.add)
            em("vector", "tensor_tensor", out=idxw2[:, half * NT3:(half + 1) * NT3],
               in0=idxw2[:, half * NT3:(half + 1) * NT3], in1=dbf[:, :], op=ALU.mult)
            em("vector", "tensor_scalar", out=idxw2[:, half * NT3:(half + 1) * NT3],
               in0=idxw2[:, half * NT3:(half + 1) * NT3], scalar1=BIGF, scalar2=None,
               op0=ALU.add)
        em("vector", "tensor_copy", out=idxw2_i[:, :], in_=idxw2[:, :])
        dv_idxw = C["dv"]
        em("vector", "tensor_scalar", out=idxb[:, :], in0=ebf[:, :],
           scalar1=float(E - 1), scalar2=pbase_sb[:, 1:2], op0=ALU.min, op1=ALU.add)
        em("vector", "tensor_copy", out=idxb_i[:, :], in_=idxb[:, :])
        em("vector", "tensor_scalar", out=wrow[:, :], in0=ebf[:, :],
           scalar1=float(E - 1), scalar2=None, op0=ALU.min)
        em("vector", "tensor_copy", out=idxb2_i[:, :], in_=wrow[:, :])
        dv_idxb = C["dv"]
        em("tensor", "matmul", out=ps_m[:, 160:168], lhsT=o1r[:, :],
           rhs=tr8[0:1, 2 * E:3 * E], start=True, stop=True,
           waits=[("dv", dv_seg), ("dv", dv_idxb)], incs=["pe"])
        em("vector", "tensor_copy", out=gbase[:, :], in_=ps_m[:, 160:168],
           waits=[("pe", C["pe"])])
        em("vector", "tensor_add", out=gbase[:, :], in0=gbase[:, :], in1=base_pe[:, :])
        em("vector", "tensor_sub", out=didx[:, :], in0=incl[:, :], in1=sel[:, :],
           waits=[("zf", zf_total)])
        for e in range(E):
            em("vector", "tensor_scalar", out=didx[:, e * GC:(e + 1) * GC],
               in0=didx[:, e * GC:(e + 1) * GC], scalar1=gbase[:, e:e + 1],
               scalar2=None, op0=ALU.add)
        em("vector", "tensor_scalar", out=didx[:, :], in0=didx[:, :],
           scalar1=BIGF, scalar2=None, op0=ALU.subtract)
        em("vector", "tensor_tensor", out=didx[:, :], in0=didx[:, :], in1=sel[:, :],
           op=ALU.mult)
        em("vector", "tensor_scalar", out=didx[:, :], in0=didx[:, :],
           scalar1=BIGF, scalar2=None, op0=ALU.add)
        em("vector", "memset", runsb[:, :], 0.0)
        for e in range(E):
            em("vector", "tensor_copy", out=slot[:, e * GC:(e + 1) * GC], in_=runsb[:, :])
            if e < E - 1:
                em("vector", "tensor_add", out=runsb[:, :], in0=runsb[:, :],
                   in1=sel[:, e * GC:(e + 1) * GC])
        em("vector", "tensor_scalar", out=mm1[:, :], in0=slot[:, :],
           scalar1=1.0, scalar2=None, op0=ALU.min)
        em("vector", "tensor_scalar", out=mm1[:, :], in0=mm1[:, :],
           scalar1=-1.0, scalar2=1.0, op0=ALU.mult, op1=ALU.add)
        em("vector", "tensor_tensor", out=mm1[:, :], in0=mm1[:, :], in1=sel[:, :],
           op=ALU.mult)
        em("vector", "tensor_sub", out=mm2[:, :], in0=sel[:, :], in1=mm1[:, :])
        first_accs = True
        for r, mk in ((0, mm1), (1, mm2)):
            em("vector", "tensor_tensor", out=acc_s[:, :], in0=didx[:, 0:GC],
               in1=mk[:, 0:GC], op=ALU.mult,
               waits=([("zf", zf_total)] if first_accs else []))
            first_accs = False
            em("vector", "tensor_copy", out=didx_r[:, r * GC:(r + 1) * GC], in_=acc_s[:, :])
            for e in range(1, E):
                em("vector", "tensor_tensor", out=acc_s[:, :],
                   in0=didx[:, e * GC:(e + 1) * GC], in1=mk[:, e * GC:(e + 1) * GC],
                   op=ALU.mult)
                em("vector", "tensor_add", out=didx_r[:, r * GC:(r + 1) * GC],
                   in0=didx_r[:, r * GC:(r + 1) * GC], in1=acc_s[:, :])
            em("vector", "tensor_tensor", out=acc_s[:, :], in0=w_all[:, 0:GC],
               in1=mk[:, 0:GC], op=ALU.mult)
            em("vector", "tensor_copy", out=w_r[:, r * GC:(r + 1) * GC], in_=acc_s[:, :])
            for e in range(1, E):
                em("vector", "tensor_tensor", out=acc_s[:, :],
                   in0=w_all[:, e * GC:(e + 1) * GC], in1=mk[:, e * GC:(e + 1) * GC],
                   op=ALU.mult)
                em("vector", "tensor_add", out=w_r[:, r * GC:(r + 1) * GC],
                   in0=w_r[:, r * GC:(r + 1) * GC], in1=acc_s[:, :])
        em("vector", "tensor_copy", out=didx_ri[:, :], in_=didx_r[:, :])
        em("vector", "tensor_scalar", out=dstc[:, 0:GC], in0=toks_sb[:, :],
           scalar1=2.0, scalar2=None, op0=ALU.mult)
        em("vector", "tensor_scalar", out=dstc[:, GC:2 * GC], in0=dstc[:, 0:GC],
           scalar1=1.0, scalar2=None, op0=ALU.add)
        for r in range(2):
            em("vector", "tensor_copy", out=recs_v[:, r, :, 0], in_=toks_sb[:, :])
            em("vector", "tensor_copy", out=recs_v[:, r, :, 1],
               in_=w_r[:, r * GC:(r + 1) * GC])
            em("vector", "tensor_copy", out=recs_v[:, r, :, 2],
               in_=dstc[:, r * GC:(r + 1) * GC])
        dv_recs = C["dv"]
        for r in range(2):
            for gc in range(GC):
                em("gpsimd", "indirect_dma_start",
                   LREC.ap(),
                   IndirectOffsetOnAxis(didx_ri[:, r * GC + gc:r * GC + gc + 1], 0),
                   recs[:, (r * GC + gc) * 3:(r * GC + gc) * 3 + 3], None,
                   bounds_check=NR - 1, oob_is_err=False,
                   waits=([("dv", dv_recs), ("zf", zf_total)] if (r == 0 and gc == 0) else []),
                   incs=["sct"])
        sct_total = C["sct"]

        ac_hbank = {0: 0, 1: 0}
        h_done = {}
        dv_ybank = {0: 0, 1: 0, 2: dv_trowbc, 3: dv_trowbc}
        dv_b2bc = {0: 0, 1: 0}
        pe_free = {"w1a0": pe_wvT[0], "w1b0": pe_woT[0], "w1a1": 0, "w1b1": 0,
                   "w2a": pe_u, "w2b": pe_t}
        b1_ac = {0: 0, 1: 0}
        pe_gx = {0: 0, 1: 0}
        gx_done = {0: 0, 1: 0}
        rc_done = {0: 0, 1: 0, 2: 0, 3: 0}
        dv_rec_slot = {0: 0, 1: 0, 2: 0, 3: 0}
        dv_gxT = dv_trbank
        pe_b1use = 0
        pe_b2use = 0
        dv_zt = 0

        def emit_prefetch_a(j):
            sl = j % 4
            rcsem = ("rc0", "rc1", "rc2", "rc3")[sl]
            em("sync", "dma_start",
               out=rec_sb_v[:, sl, :, :],
               in_=A(LREC, j * T3 * 3, [[3, 128], [128 * 3, 2], [1, 3]]),
               waits=([("sct", sct_total)] if j == 0 else [("dv", dv_rec_slot[sl])]),
               incs=[rcsem])
            rc_done[sl] = C[rcsem]

        def emit_prefetch_b(j):
            sl = j % 4
            sl2 = j % 2
            gxsem = "gx0" if sl2 == 0 else "gx1"
            wt0 = [(("rc0", "rc1", "rc2", "rc3")[sl], rc_done[sl])]
            if j >= 4:
                wt0.append(zs_done[j - 4])
            if j >= 2:
                wt0.append((gxsem, gx_done[sl2]))
            em("vector", "tensor_copy", out=tok_i_v[:, sl, :], in_=rec_sb_v[:, sl, :, 0],
               waits=wt0)
            em("vector", "tensor_copy", out=w_f_v[:, sl, :], in_=rec_sb_v[:, sl, :, 1])
            em("vector", "tensor_copy", out=dst_i_v[:, sl, :], in_=rec_sb_v[:, sl, :, 2])
            dv_rec_slot[sl] = C["dv"]
            dv_rec = C["dv"]
            for h in range(2):
                wt = []
                if h == 0:
                    wt = [("dv", dv_rec), ("xs", C["xs"]), ("xs1", C["xs1"])]
                    if pe_gx[sl2]:
                        wt.append(("pe", pe_gx[sl2]))
                em("gpsimd", "indirect_dma_start",
                   gx_v[:, sl2, h * D:(h + 1) * D], None,
                   XNBF.ap(), IndirectOffsetOnAxis(tok_i_v[:, sl, h:h + 1], 0),
                   bounds_check=SH - 1, oob_is_err=False,
                   waits=wt, incs=[gxsem])
            gx_done[sl2] = C[gxsem]

        def emit_b1(j):
            slw = j % 2
            em("gpsimd", "indirect_dma_start",
               b1_t_v[:, slw, :], None,
               B1T.ap(), IndirectOffsetOnAxis(idxb_i[:, j:j + 1], 0),
               bounds_check=128 * E - 1, oob_is_err=False,
               waits=[("dv", dv_idxb), ("xs", C["xs"]), ("xs1", C["xs1"]),
                      ("ac", b1_ac[slw]),
                      ("b1g", 16 * j)],
               incs=["b1g"])

        def emit_weights_w1(j):
            slw = j % 2
            sfx = "" if slw == 0 else "1"
            for wsem, buf, ihalf in (
                    ("w1a" + sfx, (W1a, W1a1)[slw], 0), ("w1b" + sfx, (W1b, W1b1)[slw], 1)):
                wt = [("dv", dv_idxw), ("pe", pe_free[("w1a" if ihalf == 0 else "w1b") + str(slw)])]
                em("gpsimd", "indirect_dma_start",
                   buf[:, :], None,
                   W1.ap(), IndirectOffsetOnAxis(idxw_i[:, ihalf * NT3 + j:ihalf * NT3 + j + 1], 0),
                   bounds_check=E * 128 * 2 - 1, oob_is_err=False,
                   waits=wt, incs=[wsem])

        def emit_weights_w2(j):
            slw = j % 2
            em("gpsimd", "indirect_dma_start",
               b2bc[:, slw * D:(slw + 1) * D], None,
               b2.ap(), IndirectOffsetOnAxis(idxb2_i[:, j:j + 1], 0),
               bounds_check=E - 1, oob_is_err=False,
               waits=[("dv", dv_b2bc[slw]), ("b2g", 16 * j)] +
                     ([("pe", pe_bc1)] if j < 2 else []),
               incs=["b2g"])
            for wsem, buf, ihalf in (("w2a", W2a, 0), ("w2b", W2b, 1)):
                wt = [("dv", dv_idxw), ("pe", pe_free[wsem])]
                em("gpsimd", "indirect_dma_start",
                   buf[:, :], None,
                   W2.ap(), IndirectOffsetOnAxis(idxw2_i[:, ihalf * NT3 + j:ihalf * NT3 + j + 1], 0),
                   bounds_check=E * 128 * 2 - 1, oob_is_err=False,
                   waits=wt, incs=[wsem])

        def emit_gxtr(j):
            nonlocal dv_gxT
            sl = j % 2
            gxsem = "gx0" if sl == 0 else "gx1"
            gx_j = gx_done[sl]
            for h in range(2):
                for c in range(KD):
                    wt = []
                    if c == 0:
                        wt = [(gxsem, gx_j), ("dv", dv_gxT)] if h == 0 else [("dv", dv_gxT)]
                    em("tensor", "transpose", out=ps_tr[:, c * 128:(c + 1) * 128],
                       in_=gx_v[:, sl, h * D + c * 128:h * D + (c + 1) * 128],
                       identity=idb[:, :], waits=wt, incs=(["pe"] if c == KD - 1 else []))
                if h == 1:
                    pe_gx[sl] = C["pe"]
                em("vector", "tensor_copy", out=gxT_v[:, :, h * 128:(h + 1) * 128],
                   in_=ps_tr_v[:, :, :], waits=[("pe", C["pe"])])
                dv_gxT = C["dv"]

        zs_done = {}
        emit_prefetch_a(0)
        emit_prefetch_b(0)
        emit_prefetch_a(1)
        emit_prefetch_b(1)
        emit_b1(0)
        emit_weights_w1(0)
        emit_weights_w1(1)
        emit_weights_w2(0)
        emit_gxtr(0)
        for j in range(NT3):
            sl = j % 2
            gxsem = "gx0" if sl == 0 else "gx1"
            if j + 2 < NT3:
                emit_prefetch_a(j + 2)
            if j >= 1:
                emit_weights_w2(j)
            b1g_j = 16 * (j + 1)
            b2g_j = 16 * (j + 1)
            for m in range(KH):
                bank = m % 2
                wt = []
                if m == 0:
                    wt += [("w1a" + ("" if j % 2 == 0 else "1"), 16 * (j // 2 + 1)),
                           ("dv", dv_gxT), ("b1g", b1g_j)]
                if m == KD:
                    wt.append(("w1b" + ("" if j % 2 == 0 else "1"), 16 * (j // 2 + 1)))
                if ac_hbank[bank] != 0:
                    wt.append(ac_hbank[bank])
                for k in range(KD):
                    em("tensor", "matmul", out=ps_h[bank][:, 0:T3],
                       lhsT=W1vp[j % 2][m // KD][:, k, (m % KD) * 128:(m % KD + 1) * 128],
                       rhs=gxT_v[:, k, :],
                       start=(k == 0), stop=(k == KD - 1),
                       waits=(wt if k == 0 else []),
                       incs=(["pe"] if k == KD - 1 else []))
                if m == KD - 1:
                    pe_free["w1a" + str(j % 2)] = C["pe"]
                if m == KH - 1:
                    pe_free["w1b" + str(j % 2)] = C["pe"]
                pe_h = C["pe"]
                em("scalar", "activation", out=h_bf[:, m * T3:(m + 1) * T3],
                   in_=ps_h[bank][:, 0:T3], func=AF.Relu,
                   bias=b1_t[:, sl * KH + m:sl * KH + m + 1],
                   waits=[("pe", pe_h)], incs=["ac"])
                h_done[m] = ("ac", C["ac"])
                ac_hbank[bank] = h_done[m]
            ac_h_all = C["ac"]
            b1_ac[j % 2] = C["ac"]
            if j + 1 < NT3:
                if j + 2 < NT3:
                    emit_prefetch_b(j + 2)
                emit_b1(j + 1)
                if j + 2 < NT3:
                    emit_weights_w1(j + 2)
                emit_gxtr(j + 1)
            ybanks = [ps_y[0], ps_y[1], ps_b0, ps_b1]
            for h in range(2):
                for nb in range(2):
                    bi = h * 2 + nb
                    wt = []
                    if bi == 0:
                        wt += [("w2a", 16 * (j + 1)), ("ac", ac_h_all)]
                    if dv_ybank[bi]:
                        wt.append(("dv", dv_ybank[bi]))
                    for k in range(KD):
                        em("tensor", "matmul", out=ybanks[bi][:, :],
                           lhsT=hv[:, k, h * 128:(h + 1) * 128],
                           rhs=W2v[0][:, k, nb * 512:(nb + 1) * 512],
                           start=(k == 0), stop=False,
                           waits=(wt if k == 0 else []),
                           incs=(["pe"] if bi == 3 and k == KD - 1 else []))
            pe_free["w2a"] = C["pe"]
            for h in range(2):
                for nb in range(2):
                    bi = h * 2 + nb
                    wt = [("w2b", 16 * (j + 1))] if bi == 0 else []
                    for k in range(KD):
                        em("tensor", "matmul", out=ybanks[bi][:, :],
                           lhsT=hv[:, KD + k, h * 128:(h + 1) * 128],
                           rhs=W2v[1][:, k, nb * 512:(nb + 1) * 512],
                           start=False, stop=(k == KD - 1),
                           waits=(wt if k == 0 else []),
                           incs=(["pe"] if k == KD - 1 else []))
                    if bi == 3:
                        pe_free["w2b"] = C["pe"]
                    pe_ye = C["pe"]
                    wt2 = [("pe", pe_ye), ("b2g", b2g_j)]
                    if dv_zt and bi == 0:
                        wt2.append(zs_done[j - 1])
                    em("vector", "tensor_tensor",
                       out=zt[:, h * D + nb * 512:h * D + (nb + 1) * 512],
                       in0=ybanks[bi][:, :], in1=b2bc[:, sl * D + nb * 512:sl * D + (nb + 1) * 512],
                       op=ALU.add, waits=wt2)
                    em("vector", "tensor_scalar",
                       out=zt[:, h * D + nb * 512:h * D + (nb + 1) * 512],
                       in0=zt[:, h * D + nb * 512:h * D + (nb + 1) * 512],
                       scalar1=w_f_v[:, j % 4, h:h + 1], scalar2=None, op0=ALU.mult)
                    dv_ybank[bi] = C["dv"]
            dv_zt = C["dv"]
            dv_b2bc[sl] = C["dv"]
            zsem = "zs" if j % 2 == 0 else "zs1"
            for h in range(2):
                em("gpsimd", "indirect_dma_start",
                   ZD.ap(), IndirectOffsetOnAxis(dst_i_v[:, j % 4, h:h + 1], 0),
                   zt[:, h * D:(h + 1) * D], None,
                   bounds_check=2 * SH - 1, oob_is_err=False,
                   waits=([("dv", dv_zt), ("zf", zf_total)] if h == 0 else []),
                   incs=[zsem])
            zs_done[j] = (zsem, C[zsem])
        zs_tw = [("zs", C["zs"]), ("zs1", C["zs1"])]

        st_slot = {0: 0, 1: 0}
        dv_zl = {0: 0, 1: 0, 2: 0, 3: 0}
        x4bufs = [g_grp, scr1, zt[:, 0:D], zt[:, D:2 * D]]
        for t in range(NT):
            for g in range(NG):
                gi = t * NG + g
                sl4 = gi % 4
                rcsem = ("rc0", "rc1", "rc2", "rc3")[sl4]
                tokbase = t * TILE + g * 128
                gc4 = t * NG + g
                em("sync", "dma_start",
                   out=x_tv.tensor if False else zn_v[:, 2 * sl4:2 * sl4 + 2, :],
                   in_=A(ZD, 2 * tokbase * D, [[2 * D, 128], [D, 2], [1, D]]),
                   waits=(zs_tw if gi == 0 else []) +
                         ([("dv", dv_zl[sl4])] if dv_zl[sl4] else []),
                   incs=[rcsem])
                em("sync", "dma_start",
                   out=x4bufs[sl4][:, :] if sl4 < 2 else x4bufs[sl4],
                   in_=A(x_in, tokbase * D, [[D, 128], [1, D]]),
                   waits=([("dv", dv_sgrp)] if gi < 2 else []), incs=[rcsem])
                rc_g = C[rcsem]
                z0 = zn_v[:, 2 * sl4, :]
                z1 = zn_v[:, 2 * sl4 + 1, :]
                xb4 = x4bufs[sl4][:, :] if sl4 < 2 else x4bufs[sl4]
                em("vector", "tensor_add", out=z0, in0=z0, in1=z1, waits=[(rcsem, rc_g)])
                dv_y4 = C["dv"]
                em("scalar", "activation", out=z1, in_=z0, func=AF.Silu,
                   waits=[("dv", dv_y4)], incs=["ac"])
                ac_g4 = C["ac"]
                if t == 7 and g == SPECIAL_GROUP:
                    em("vector", "tensor_tensor", out=z0, in0=z1, in1=s_grp[:, :],
                       op=ALU.mult, waits=[("ac", ac_g4)])
                else:
                    em("vector", "tensor_scalar", out=z0, in0=xb4,
                       scalar1=mr_all[:, gc4:gc4 + 1],
                       scalar2=mr_all[:, GC + gc4:GC + gc4 + 1],
                       op0=ALU.subtract, op1=ALU.mult, waits=[("ac", ac_g4)])
                    em("vector", "tensor_tensor", out=z0, in0=z0, in1=z1, op=ALU.mult)
                ssem = "st0" if gi % 2 == 0 else "st1"
                wt = []
                if st_slot[gi % 2]:
                    wt.append((ssem, st_slot[gi % 2]))
                em("vector", "tensor_add", out=out_g[:, (gi % 2) * D:(gi % 2 + 1) * D],
                   in0=z0, in1=xb4, waits=wt)
                dv_zl[sl4] = C["dv"]
                em("scalar", "dma_start", out=A(out, tokbase * D, [[D, 128], [1, D]]),
                   in_=out_g[:, (gi % 2) * D:(gi % 2 + 1) * D],
                   waits=[("dv", dv_zl[sl4])], incs=[ssem])
                st_slot[gi % 2] = C[ssem]

        em("scalar", "activation", out=t_f[:, 0:1], in_=t_f[:, 0:1], func=AF.Copy,
           waits=[("st0", C["st0"]), ("st1", C["st1"])])

        with nc.Block() as block:
            def run(eng_name):
                def body(eng):
                    bndregs = {}
                    if eng_name == "gpsimd":
                        for bv in (NR - 1, SH - 1, 128 * E - 1, 2 * SH - 1,
                                   E * 128 * 2 - 1, E - 1):
                            r = es.enter_context(eng.register(f"bnd{bv}"))
                            eng.reg_mov(r, bv)
                            bndregs[bv] = r
                    for method, args, kw, ws, incs in sched[eng_name]:
                        for s, v in ws:
                            eng.wait_ge(s, v)
                        if "bounds_check" in kw and isinstance(kw["bounds_check"], int):
                            kw = dict(kw, bounds_check=bndregs[kw["bounds_check"]])
                        inst = getattr(eng, method)(*args, **kw)
                        for s in incs:
                            inst.then_inc(s, 16 if "dma_start" in method else 1)
                return body
            block.sync(run("sync"))
            block.gpsimd(run("gpsimd"))
            block.scalar(run("scalar"))
            block.vector(run("vector"))
            block.tensor(run("tensor"))

    return nc


def kernel(**inputs):
    x = np.ascontiguousarray(np.asarray(inputs["L0_attn_inputs"], dtype=np.float32))
    toks = (np.arange(GC)[None, :] * 128 + np.arange(128)[:, None]).astype(np.float32)
    segmask = np.ones((128, E * GC), dtype=np.float32)
    segmask[:, 0::GC] = 0.0
    common = {
        "Wv": np.ascontiguousarray(np.asarray(inputs["L0_attn_Wv"], dtype=np.float32)),
        "Wo": np.ascontiguousarray(np.asarray(inputs["L0_attn_Wo"], dtype=np.float32)),
        "Wg": np.ascontiguousarray(np.asarray(inputs["L0_ffn_Wg"], dtype=np.float32)),
        "W1": np.ascontiguousarray(
            np.asarray(inputs["L0_ffn_W1"], dtype=np.float32)
            .reshape(8, 8, 128, 2, 1024).transpose(0, 3, 2, 1, 4).reshape(8 * 256, 8192)),
        "b1": np.ascontiguousarray(np.asarray(inputs["L0_ffn_b1"], dtype=np.float32)),
        "W2": np.ascontiguousarray(
            np.asarray(inputs["L0_ffn_W2"], dtype=np.float32)
            .reshape(8, 2, 8, 128, 1024).transpose(0, 3, 1, 2, 4).reshape(8 * 256, 8192)),
        "b2": np.ascontiguousarray(np.asarray(inputs["L0_ffn_b2"], dtype=np.float32)),
        "ident": np.eye(128, dtype=np.float32),
        "ltri": np.triu(np.ones((128, 128), dtype=np.float32), 1),
        "ones128": np.ones((128, 1), dtype=np.float32),
        "ones1r": np.ones((1, 128), dtype=np.float32),
        "toksc": toks,
        "pbase": np.stack([np.arange(128) * 128, np.arange(128) * 8, np.arange(128)], axis=1).astype(np.float32),
        "jtc": (np.arange(NT3)[None, :] * T3).astype(np.float32),
        "segm": segmask,
        "onehot": np.repeat(np.eye(8, dtype=np.float32), 128, axis=1),
        "tmask": np.zeros((128, 1), dtype=np.float32),
    }
    tmask_on = np.ones((128, 1), dtype=np.float32)
    nc = build_nc()
    in_maps = []
    for c in range(8):
        b, half = c // 2, c % 2
        m = dict(common)
        m["x"] = np.ascontiguousarray(x[b, half * SH:(half + 1) * SH, :])
        m["tmask"] = tmask_on if half == 1 else common["tmask"]
        in_maps.append(m)
    res = run_bass_kernel_spmd(nc, in_maps, core_ids=list(range(8)))
    outp = np.empty((4, 8192, D), dtype=np.float32)
    for c in range(8):
        b, half = c // 2, c % 2
        outp[b, half * SH:(half + 1) * SH, :] = res.results[c]["out"]
    return outp


# version 49 (final)
# speedup vs baseline: 1.0994x; 1.0060x over previous
import contextlib
import numpy as np
import concourse.bass as bass
import concourse.mybir as mybir
from concourse.bass_utils import run_bass_kernel_spmd
from concourse.bass_core_rust import IndirectOffsetOnAxis

F32 = mybir.dt.float32
BF16 = mybir.dt.bfloat16
I32 = mybir.dt.int32
AF = mybir.ActivationFunctionType
ALU = mybir.AluOpType
AX = mybir.AxisListType

D = 1024
E = 8
HID = 2048
SH = 4096
TILE = 512
NT = SH // TILE
NG = TILE // 128
KD = D // 128
KH = HID // 128
GC = NT * NG
T3 = 256
NT3 = (2 * SH) // T3 + E
NR = NT3 * T3
BIGF = float(2 ** 20)
TILE_ORDER = [7, 0, 1, 2, 3, 4, 5, 6]
SPECIAL_COL = 384
SPECIAL_GROUP = 3


def build_nc():
    nc = bass.Bass("TRN2", target_bir_lowering=False)

    x_in = nc.dram_tensor("x", [SH, D], F32, kind="ExternalInput")
    Wv = nc.dram_tensor("Wv", [D, D], F32, kind="ExternalInput")
    Wo = nc.dram_tensor("Wo", [D, D], F32, kind="ExternalInput")
    Wg = nc.dram_tensor("Wg", [D, E], F32, kind="ExternalInput")
    W1 = nc.dram_tensor("W1", [E * 256, KD * 1024], F32, kind="ExternalInput")
    b1 = nc.dram_tensor("b1", [E, HID], F32, kind="ExternalInput")
    W2 = nc.dram_tensor("W2", [E * 256, KH * 512], F32, kind="ExternalInput")
    b2 = nc.dram_tensor("b2", [E, D], F32, kind="ExternalInput")
    ident = nc.dram_tensor("ident", [128, 128], F32, kind="ExternalInput")
    onehot = nc.dram_tensor("onehot", [E, E * 128], F32, kind="ExternalInput")
    tmask = nc.dram_tensor("tmask", [128, 1], F32, kind="ExternalInput")
    ltri = nc.dram_tensor("ltri", [128, 128], F32, kind="ExternalInput")
    ones128 = nc.dram_tensor("ones128", [128, 1], F32, kind="ExternalInput")
    ones1r = nc.dram_tensor("ones1r", [1, 128], F32, kind="ExternalInput")
    toksc = nc.dram_tensor("toksc", [128, GC], F32, kind="ExternalInput")
    pbase = nc.dram_tensor("pbase", [128, 3], F32, kind="ExternalInput")
    jtc = nc.dram_tensor("jtc", [1, NT3], F32, kind="ExternalInput")
    segm = nc.dram_tensor("segm", [128, E * GC], F32, kind="ExternalInput")
    out = nc.dram_tensor("out", [SH, D], F32, kind="ExternalOutput")

    XNBF = nc.dram_tensor("XNBF", [SH, D], BF16)
    B1T = nc.dram_tensor("B1T", [128 * E, KH], F32)
    LREC = nc.dram_tensor("LREC", [NR, 3], F32)
    ZD = nc.dram_tensor("ZD", [2 * SH, D], F32)

    es = contextlib.ExitStack()
    with es:
        sb = lambda n, s, d: es.enter_context(nc.sbuf_tensor(n, s, d))
        pst = lambda n, s, d: es.enter_context(nc.psum_tensor(n, s, d))
        mksem = lambda n: es.enter_context(nc.semaphore(n))

        x_t = sb("x_t", [128, 2 * NG * D], F32)
        xnb = sb("xnb", [128, 2 * NG * D], BF16)
        mr_all = sb("mr_all", [128, 2 * GC], F32)
        xnT = sb("xnT", [128, KD * TILE], BF16)
        xn_col = sb("xn_col", [128, KD], BF16)
        cols = sb("cols", [128, 64], F32)
        lg = sb("lg", [128, NG * E], F32)
        sme = sb("sme", [128, NG * E], F32)
        m8 = sb("m8", [128, NG * E], F32)
        w_t = sb("w_t", [128, NG * E], F32)
        w_all = sb("w_all", [128, E * GC], F32)
        s_grp = sb("s_grp", [128, D], F32)
        scr1 = sb("scr1", [128, D], F32)
        g_grp = sb("g_grp", [128, D], F32)
        out_g = sb("out_g", [128, 2 * D], F32)
        sel = sb("sel", [128, E * GC], F32)
        incl = sb("incl", [128, E * GC], F32)
        slot = sb("slot", [128, E * GC], F32)
        mm1 = sb("mm1", [128, E * GC], F32)
        mm2 = sb("mm2", [128, E * GC], F32)
        didx = sb("didx", [128, E * GC], F32)
        runsb = sb("runsb", [128, GC], F32)
        cnt = sb("cnt", [128, E], F32)
        base_pe = sb("base_pe", [128, E], F32)
        gbase = sb("gbase", [128, E], F32)
        tot = sb("tot", [1, E], F32)
        tr8 = sb("tr8", [1, 3 * E], F32)
        tr8i = sb("tr8i", [1, 2 * E], I32)
        etile = sb("etile", [1, 4 * NT3], F32)
        ebf = sb("ebf", [128, NT3], F32)
        dbf = sb("dbf", [128, NT3], F32)
        dbf2 = sb("dbf2", [128, NT3], F32)
        wrow = sb("wrow", [128, NT3], F32)
        idxw = sb("idxw", [128, 2 * NT3], F32)
        idxw_i = sb("idxw_i", [128, 2 * NT3], I32)
        idxw2 = sb("idxw2", [128, 2 * NT3], F32)
        idxw2_i = sb("idxw2_i", [128, 2 * NT3], I32)
        idxb = sb("idxb", [128, NT3], F32)
        idxb_i = sb("idxb_i", [128, NT3], I32)
        idxb2_i = sb("idxb2_i", [128, NT3], I32)
        didx_r = sb("didx_r", [128, 2 * GC], F32)
        didx_ri = sb("didx_ri", [128, 2 * GC], I32)
        w_r = sb("w_r", [128, 2 * GC], F32)
        dstc = sb("dstc", [128, 2 * GC], F32)
        recs = sb("recs", [128, 2 * GC * 3], F32)
        acc_s = sb("acc_s", [128, GC], F32)
        W1a = sb("W1a", [128, KD * 1024], BF16)
        W1b = sb("W1b", [128, KD * 1024], BF16)
        W1a1 = sb("W1a1", [128, KD * 1024], BF16)
        W1b1 = sb("W1b1", [128, KD * 1024], BF16)
        W2a = sb("W2a", [128, KD * 1024], BF16)
        W2b = sb("W2b", [128, KD * 1024], BF16)
        zt = sb("zt", [128, 2 * D], F32)
        rec_sb = sb("rec_sb", [128, 4 * 6], F32)
        tok_i = sb("tok_i", [128, 4 * 2], I32)
        dst_i = sb("dst_i", [128, 4 * 2], I32)
        w_f = sb("w_f", [128, 4 * 2], F32)
        b1_t = sb("b1_t", [128, 2 * KH], F32)
        b2bc = sb("b2bc", [128, 2 * D], F32)
        u_bf = sb("u_bf", [128, E], BF16)
        t_f = sb("t_f", [128, E], F32)
        t8 = sb("t8", [8, 128], F32)
        b1_sb = sb("b1_sb", [128, E * KH], F32)
        wg_f = sb("wg_f", [128, KD * E], F32)
        wg_b = sb("wg_b", [128, KD * E], BF16)
        idf = sb("idf", [128, 128], F32)
        idb = sb("idb", [128, 128], BF16)
        tm_sb = sb("tm_sb", [128, 1], F32)
        eps_c = sb("eps_c", [128, 1], F32)
        mh_c = sb("mh_c", [128, 1], F32)
        ltri_sb = sb("ltri_sb", [128, 128], F32)
        o128 = sb("o128", [128, 1], F32)
        o1r = sb("o1r", [1, 128], F32)
        toks_sb = sb("toks_sb", [128, GC], F32)
        pbase_sb = sb("pbase_sb", [128, 3], F32)
        jtc_sb = sb("jtc_sb", [1, NT3], F32)
        segm_sb = sb("segm_sb", [128, E * GC], F32)


        ps_h = [pst("ps_h0", [128, 512], F32), pst("ps_h1", [128, 512], F32)]
        ps_y = [pst("ps_y0", [128, 512], F32), pst("ps_y1", [128, 512], F32)]
        ps_tr = pst("ps_tr", [128, 1024], BF16)
        ps_b0 = pst("ps_b0", [128, 512], F32)
        ps_b1 = pst("ps_b1", [128, 512], F32)
        ps_m = pst("ps_m", [128, 512], F32)

        SEM = {k: mksem("s_" + k) for k in (
            "ld", "sd", "xs", "zf", "cvo", "cast", "sct", "rc0", "rc1", "rc2", "rc3",
            "gx0", "gx1", "w1a", "w1b", "w1a1", "w1b1", "w2a", "w2b", "b1g", "b2g",
            "zs", "zs1", "sd1", "xs1",
            "st0", "st1", "pe", "dv", "ac")}
        C = {k: 0 for k in SEM}
        DMA = {k for k in SEM if k not in ("pe", "dv", "ac")}

        sched = {k: [] for k in ("sync", "gpsimd", "scalar", "vector", "tensor")}

        def em(eng, method, *args, waits=(), incs=(), **kw):
            waits = list(waits)
            if eng == "vector" and C["dv"] > 0:
                waits.append(("dv", C["dv"]))
            ws = [(SEM[n], v) for (n, v) in waits if v > 0]
            if eng == "vector" and "dv" not in incs:
                incs = list(incs) + ["dv"]
            sched[eng].append((method, args, kw, ws, [SEM[n] for n in incs]))
            for n in incs:
                C[n] += 16 if n in DMA else 1
            return {n: C[n] for n in incs}

        A = bass.AP
        x_tv = x_t.ap().rearrange("p (s g d) -> p s g d", s=2, g=NG)
        xnT_v = xnT.ap().rearrange("p (c t2) -> p c t2", c=KD)
        ps_tr_v = ps_tr.ap().rearrange("p (c j) -> p c j", c=KD)
        WvN = W1a.ap().rearrange("p (r c j) -> p r c j", r=KD, c=KD)
        WoN = W1b.ap().rearrange("p (r c j) -> p r c j", r=KD, c=KD)
        WvT = W2a.ap().rearrange("p (k m j) -> p k m j", k=KD, m=KD)
        WoT = W2b.ap().rearrange("p (k m j) -> p k m j", k=KD, m=KD)
        W1vp = [[W1a.ap().rearrange("p (k h) -> p k h", k=KD),
                 W1b.ap().rearrange("p (k h) -> p k h", k=KD)],
                [W1a1.ap().rearrange("p (k h) -> p k h", k=KD),
                 W1b1.ap().rearrange("p (k h) -> p k h", k=KD)]]
        W2v = [W2a.ap().rearrange("p (k n) -> p k n", k=KD),
               W2b.ap().rearrange("p (k n) -> p k n", k=KD)]
        w_all_v = w_all.ap().rearrange("p (e gc) -> p e gc", e=E)
        incl_v = incl.ap().rearrange("p (e gc) -> p e gc", e=E)
        recs_v = recs.ap().rearrange("p (r gc c3) -> p r gc c3", r=2, gc=GC)
        rec_sb_v = rec_sb.ap().rearrange("p (s h c3) -> p s h c3", s=4, h=2)
        tok_i_v = tok_i.ap().rearrange("p (s h) -> p s h", s=4)
        dst_i_v = dst_i.ap().rearrange("p (s h) -> p s h", s=4)
        w_f_v = w_f.ap().rearrange("p (s h) -> p s h", s=4)
        b1_t_v = b1_t.ap().rearrange("p (s m) -> p s m", s=2)
        gx_v = xnb.ap().rearrange("p (s q) -> p s q", s=4)
        gxT_v = xnb.ap().rearrange("p (s c t2) -> p s c t2", s=4, c=KD)[:, 2, :, :]
        h_bf = xnT
        hv = h_bf.ap().rearrange("p (k t2) -> p k t2", k=KH)
        zn_v = x_t.ap().rearrange("p (s d2) -> p s d2", s=8)

        em("sync", "dma_start", out=idf[:, :], in_=ident[:, :], incs=["ld"])
        em("sync", "dma_start", out=tm_sb[:, :], in_=tmask[:, :], incs=["ld"])
        em("sync", "dma_start", out=ltri_sb[:, :], in_=ltri[:, :], incs=["ld"])
        em("sync", "dma_start", out=o128[:, :], in_=ones128[:, :], incs=["ld"])
        em("sync", "dma_start", out=o1r[:, :], in_=ones1r[:, :], incs=["ld"])
        em("sync", "dma_start", out=toks_sb[:, :], in_=toksc[:, :], incs=["ld"])
        em("sync", "dma_start", out=pbase_sb[:, :], in_=pbase[:, :], incs=["ld"])
        em("sync", "dma_start", out=jtc_sb[:, :], in_=jtc[:, :], incs=["ld"])
        em("sync", "dma_start", out=segm_sb[:, :], in_=segm[:, :], incs=["ld"])
        em("sync", "dma_start", out=wg_f.ap().rearrange("p (k e) -> p k e", k=KD),
           in_=A(Wg, 0, [[E, 128], [128 * E, KD], [1, E]]), incs=["ld"])
        b_raw = out_g
        em("sync", "dma_start", out=b_raw[0:16, 0:E * 128].rearrange("i (e j) -> i e j", e=E),
           in_=A(b1, 0, [[128, 16], [HID, E], [1, 128]]), incs=["ld"])
        oh_sb = b2bc
        em("sync", "dma_start", out=oh_sb[0:8, 0:E * 128], in_=onehot[:, :], incs=["ld"])
        ld_consts = C["ld"]

        em("vector", "memset", zt[:, :], 0.0)
        dv_z0 = C["dv"]
        for q in range(32):
            em("sync", "dma_start",
               out=A(ZD, q * 256 * D, [[2 * D, 128], [D, 2], [1, D]]),
               in_=zt.ap().rearrange("p (h d2) -> p h d2", h=2),
               waits=([("dv", dv_z0)] if q == 0 else []), incs=["zf"])
        em("vector", "memset", recs[:, 0:192], BIGF)
        em("vector", "memset", didx[:, 0:48], BIGF)
        dv_lrp = C["dv"]
        em("sync", "dma_start", out=A(LREC, 0, [[192, 128], [1, 192]]),
           in_=recs[:, 0:192], waits=[("dv", dv_lrp)], incs=["zf"])
        em("sync", "dma_start", out=A(LREC, 128 * 192, [[48, 128], [1, 48]]),
           in_=didx[:, 0:48], incs=["zf"])
        zf_total = C["zf"]

        em("gpsimd", "dma_start", out=W1a.ap().rearrange("p (r d2) -> p r d2", r=KD),
           in_=A(Wv, 0, [[D, 128], [128 * D, KD], [1, D]]), incs=["cvo"])
        cvo_wv = C["cvo"]
        em("gpsimd", "dma_start", out=W1b.ap().rearrange("p (r d2) -> p r d2", r=KD),
           in_=A(Wo, 0, [[D, 128], [128 * D, KD], [1, D]]), incs=["cvo"])
        cvo_wo = C["cvo"]


        em("vector", "memset", eps_c[:, :], 1e-5)
        em("vector", "memset", mh_c[:, :], -0.5)
        em("scalar", "activation", out=idb[:, :], in_=idf[:, :], func=AF.Copy,
           waits=[("ld", ld_consts)], incs=["ac"])
        ac_idb = C["ac"]
        em("scalar", "activation", out=wg_b[:, :], in_=wg_f[:, :], func=AF.Copy, incs=["ac"])
        ac_wg = C["ac"]

        for e in range(E):
            em("tensor", "transpose", out=ps_b0[:, e * 16:(e + 1) * 16],
               in_=b_raw[0:16, e * 128:(e + 1) * 128], identity=idf[0:16, 0:16],
               waits=([("ld", ld_consts), ("ac", ac_idb)] if e == 0 else []),
               incs=(["pe"] if e == E - 1 else []))
        pe_b1t = C["pe"]
        em("vector", "tensor_copy", out=b1_sb[:, :], in_=ps_b0[:, 0:128],
           waits=[("pe", pe_b1t)])
        dv_b1 = C["dv"]
        em("sync", "dma_start", out=A(B1T, 0, [[E * KH, 128], [1, E * KH]]),
           in_=b1_sb[:, :], waits=[("dv", dv_b1)], incs=["xs"])

        dv_trbank = 0
        ac_trbank = 0
        tr_last = {0: ("ac", 0), 1: ("dv", 0)}
        pe_wvT = [0]
        pe_woT = [0]

        def emit_wvwo():
            nonlocal dv_trbank
            for c in range(KD):
                for r in range(KD):
                    wt = []
                    if c == 0 and r == 0:
                        wt = [("cvo", cvo_wo), ("ac", ac_idb), tr_last[0], tr_last[1]]
                    if r == 0 and dv_trbank:
                        wt.append(("dv", dv_trbank))
                    em("tensor", "transpose", out=ps_tr[:, r * 128:(r + 1) * 128],
                       in_=WvN[:, r, c, :], identity=idb[:, :], waits=wt,
                       incs=(["pe"] if r == KD - 1 else []))
                em("vector", "tensor_copy", out=WvT[:, c, :, :], in_=ps_tr_v[:, :, :],
                   waits=[("pe", C["pe"])])
                dv_trbank = C["dv"]
            pe_wvT[0] = C["pe"]
            for c in range(KD):
                for r in range(KD):
                    wt = []
                    if c == 0 and r == 0:
                        wt = [("cvo", cvo_wo)]
                    if r == 0:
                        wt.append(("dv", dv_trbank))
                    em("tensor", "transpose", out=ps_tr[:, r * 128:(r + 1) * 128],
                       in_=WoN[:, r, c, :], identity=idb[:, :], waits=wt,
                       incs=(["pe"] if r == KD - 1 else []))
                em("vector", "tensor_copy", out=WoT[:, c, :, :], in_=ps_tr_v[:, :, :],
                   waits=[("pe", C["pe"])])
                dv_trbank = C["dv"]
            pe_woT[0] = C["pe"]

        pe_xnb_free = {0: 0, 1: 0}
        dv_xfree = {0: 0, 1: 0}
        ac_xfree = {0: 0, 1: 0}
        xs_xnb = {0: 0, 1: 0}
        dv_lgbank = {0: dv_b1, 1: 0}
        pe_lg_done = {}
        dv_m = 0
        dv_xncol = 0

        def emit_softmax(t):
            par = (TILE_ORDER.index(t)) % 2
            pbank = ps_b0 if par == 0 else ps_b1
            em("vector", "tensor_copy", out=lg[:, :], in_=pbank[:, 0:NG * E],
               waits=[("pe", pe_lg_done[t])])
            dv_lgbank[par] = C["dv"]
            for g in range(NG):
                em("vector", "reduce_max", out=cols[:, 28 + g:29 + g],
                   in_=lg[:, g * E:(g + 1) * E], axis=AX.X)
            for g in range(NG):
                em("vector", "tensor_scalar", out=sme[:, g * E:(g + 1) * E],
                   in0=lg[:, g * E:(g + 1) * E], scalar1=cols[:, 28 + g:29 + g],
                   scalar2=None, op0=ALU.subtract)
            dvp = C["dv"]
            em("scalar", "activation", out=sme[:, :], in_=sme[:, :], func=AF.Exp,
               waits=[("dv", dvp)], incs=["ac"])
            acp = C["ac"]
            for g in range(NG):
                em("vector", "reduce_sum", out=cols[:, 28 + g:29 + g],
                   in_=sme[:, g * E:(g + 1) * E], axis=AX.X,
                   waits=([("ac", acp)] if g == 0 else []))
            em("vector", "reciprocal", out=cols[:, 28:32], in_=cols[:, 28:32])
            for g in range(NG):
                em("vector", "tensor_scalar", out=sme[:, g * E:(g + 1) * E],
                   in0=sme[:, g * E:(g + 1) * E], scalar1=cols[:, 28 + g:29 + g],
                   scalar2=None, op0=ALU.mult)
            for g in range(NG):
                em("vector", "max", out=m8[:, g * E:(g + 1) * E],
                   in_=sme[:, g * E:(g + 1) * E])
            for g in range(NG):
                em("vector", "tensor_scalar", out=w_t[:, g * E:(g + 1) * E],
                   in0=sme[:, g * E:(g + 1) * E], scalar1=m8[:, g * E + 1:g * E + 2],
                   scalar2=None, op0=ALU.is_ge)
            em("vector", "tensor_tensor", out=w_t[:, :], in0=w_t[:, :], in1=sme[:, :],
               op=ALU.mult)
            em("vector", "tensor_copy", out=w_all_v[:, :, t * NG:(t + 1) * NG],
               in_=w_t.ap().rearrange("p (g e) -> p e g", g=NG))

        sd_tile = {}
        sdsem = lambda p: "sd" if p == 0 else "sd1"
        dv_nmr_t = {}

        def emit_stats(ti):
            t = TILE_ORDER[ti]
            par = ti % 2
            co = par * 32
            sd_x = sd_tile[ti]
            xsl = x_t[:, par * NG * D:(par + 1) * NG * D]
            em("vector", "reduce_sum", out=cols[:, co:co + 4],
               in_=x_tv[:, par, :, :], axis=AX.X, waits=[sd_x])
            dv_xfree[par] = C["dv"]
            for g in range(NG):
                wt = [sd_x] if g == 0 else [("ac", C["ac"])]
                em("scalar", "activation", out=g_grp[:, :],
                   in_=xsl[:, g * D:(g + 1) * D],
                   func=AF.Square, accum_out=cols[:, co + 4 + g:co + 5 + g],
                   waits=wt, incs=["ac"])
            ac_ssq = C["ac"]
            em("vector", "tensor_scalar", out=cols[:, co + 8:co + 12],
               in0=cols[:, co:co + 4], scalar1=1.0 / D, scalar2=None, op0=ALU.mult)
            em("vector", "tensor_tensor", out=cols[:, co + 12:co + 16],
               in0=cols[:, co + 8:co + 12], in1=cols[:, co + 8:co + 12], op=ALU.mult)
            em("vector", "scalar_tensor_tensor", out=cols[:, co + 16:co + 20],
               in0=cols[:, co + 4:co + 8], scalar=1.0 / D, in1=cols[:, co + 12:co + 16],
               op0=ALU.mult, op1=ALU.subtract, waits=[("ac", ac_ssq)])
            dv_var = C["dv"]
            em("scalar", "activation", out=cols[:, co + 16:co + 20],
               in_=cols[:, co + 16:co + 20], func=AF.Ln,
               bias=eps_c[:, 0:1], waits=[("dv", dv_var)], incs=["ac"])
            em("scalar", "activation", out=cols[:, co + 20:co + 24],
               in_=cols[:, co + 16:co + 20], func=AF.Exp,
               scale=mh_c[:, 0:1], incs=["ac"])
            ac_rstd = C["ac"]
            em("vector", "scalar_tensor_tensor", out=cols[:, co + 24:co + 28],
               in0=cols[:, co + 8:co + 12], scalar=-1.0, in1=cols[:, co + 20:co + 24],
               op0=ALU.mult, op1=ALU.mult, waits=[("ac", ac_rstd)])
            dv_nmr_t[ti] = C["dv"]
            em("vector", "tensor_copy", out=mr_all[:, t * NG:(t + 1) * NG],
               in_=cols[:, co + 8:co + 12])
            em("vector", "tensor_copy", out=mr_all[:, GC + t * NG:GC + (t + 1) * NG],
               in_=cols[:, co + 20:co + 24])

        em("scalar", "dma_start", out=x_tv[:, 0, :, :],
           in_=A(x_in, TILE_ORDER[0] * TILE * D, [[D, 128], [128 * D, NG], [1, D]]),
           incs=[sdsem(0)])
        sd_tile[0] = (sdsem(0), C[sdsem(0)])
        emit_stats(0)
        for ti, t in enumerate(TILE_ORDER):
            par = ti % 2
            base = t * TILE
            co = par * 32
            sd_x = sd_tile[ti]
            xsl = x_t[:, par * NG * D:(par + 1) * NG * D]
            if ti + 1 < NT:
                t2 = TILE_ORDER[ti + 1]
                p2 = (ti + 1) % 2
                em("scalar", "dma_start", out=x_tv[:, p2, :, :],
                   in_=A(x_in, t2 * TILE * D, [[D, 128], [128 * D, NG], [1, D]]),
                   waits=([("dv", dv_xfree[p2]), ("ac", ac_xfree[p2])] if ti >= 1 else []),
                   incs=[sdsem(p2)])
                sd_tile[ti + 1] = (sdsem(p2), C[sdsem(p2)])
            dv_nmr = dv_nmr_t[ti]
            ac_xnb = {}
            lbank = ps_b0 if par == 0 else ps_b1
            for g in range(NG):
                wt = [("dv", dv_nmr)] if g == 0 else []
                if g == 0:
                    if pe_xnb_free[par]:
                        wt.append(("pe", pe_xnb_free[par]))
                    if xs_xnb[par]:
                        wt.append(xs_xnb[par])
                em("scalar", "activation",
                   out=xnb[:, (par * NG + g) * D:(par * NG + g + 1) * D],
                   in_=xsl[:, g * D:(g + 1) * D], func=AF.Identity,
                   bias=cols[:, co + 24 + g:co + 25 + g],
                   scale=cols[:, co + 20 + g:co + 21 + g],
                   waits=wt, incs=["ac"])
                ac_xnb[g] = C["ac"]
                for c in range(KD):
                    wt = []
                    if c == 0:
                        wt = [("ac", ac_xnb[g]), tr_last[0], tr_last[1]]
                        if dv_trbank:
                            wt.append(("dv", dv_trbank))
                    em("tensor", "transpose", out=ps_tr[:, c * 128:(c + 1) * 128],
                       in_=xnb[:, (par * NG + g) * D + c * 128:(par * NG + g) * D + (c + 1) * 128],
                       identity=idb[:, :], waits=wt, incs=(["pe"] if c == KD - 1 else []))
                pe_xnb_free[par] = C["pe"]
                em("scalar", "activation", out=xnT_v[:, :, g * 128:(g + 1) * 128],
                   in_=ps_tr_v[:, :, :], func=AF.Copy,
                   waits=[("pe", C["pe"])], incs=["ac"])
                trb = ("ac", C["ac"])
                ac_trbank = C["ac"]
                tr_last[g % 2] = trb
                for k in range(KD):
                    wt = []
                    if k == 0:
                        wt = [trb]
                        if g == 0 and dv_lgbank[par]:
                            wt.append(("dv", dv_lgbank[par]))
                    em("tensor", "matmul", out=lbank[:, g * E:(g + 1) * E],
                       lhsT=xnT[:, k * TILE + g * 128:k * TILE + (g + 1) * 128],
                       rhs=wg_b[:, k * E:(k + 1) * E], start=(k == 0), stop=(k == KD - 1),
                       waits=wt, incs=(["pe"] if k == KD - 1 else []))
            pe_lg_done[t] = C["pe"]
            ac_xfree[par] = C["ac"]
            if ti + 1 < NT:
                emit_stats(ti + 1)
            xssem = "xs" if par == 0 else "xs1"
            em("sync", "dma_start",
               out=A(XNBF, base * D, [[D, 128], [128 * D, NG], [1, D]]),
               in_=xnb.ap().rearrange("p (s g d) -> p s g d", s=2, g=NG)[:, par],
               waits=[("ac", ac_xnb[NG - 1])], incs=[xssem])
            xs_xnb[par] = (xssem, C[xssem])
            if t == 7:
                em("vector", "tensor_copy", out=xn_col[:, :],
                   in_=xnT_v[:, :, SPECIAL_COL:SPECIAL_COL + 1],
                   waits=[tr_last[0], tr_last[1]])
                dv_xncol = C["dv"]
            if ti > 0:
                emit_softmax(TILE_ORDER[ti - 1])
            if ti == 1:
                emit_wvwo()
        emit_softmax(TILE_ORDER[-1])
        dv_wall = C["dv"]

        for m in range(KD):
            for k in range(KD):
                wt = [("dv", dv_trbank), ("dv", dv_m), ("dv", dv_xncol)] \
                    if (m == 0 and k == 0) else []
                em("tensor", "matmul", out=ps_m[:, 40 + m:41 + m],
                   lhsT=WvT[:, k, m, :], rhs=xn_col[:, k:k + 1],
                   start=(k == 0), stop=(k == KD - 1), waits=wt,
                   incs=(["pe"] if (m == KD - 1 and k == KD - 1) else []))
        pe_u = C["pe"]
        em("scalar", "activation", out=u_bf[:, :], in_=ps_m[:, 40:48], func=AF.Copy,
           waits=[("pe", pe_u)], incs=["ac"])
        ac_u = C["ac"]
        for m in range(KD):
            for k in range(KD):
                wt = [("ac", ac_u)] if (m == 0 and k == 0) else []
                em("tensor", "matmul", out=ps_m[:, 48 + m:49 + m],
                   lhsT=WoT[:, k, m, :], rhs=u_bf[:, k:k + 1],
                   start=(k == 0), stop=(k == KD - 1), waits=wt,
                   incs=(["pe"] if (m == KD - 1 and k == KD - 1) else []))
        pe_t = C["pe"]
        em("scalar", "activation", out=t_f[:, :], in_=ps_m[:, 48:56], func=AF.Copy,
           waits=[("pe", pe_t)], incs=["ac"])
        ac_tf = C["ac"]
        em("tensor", "transpose", out=ps_b0[0:8, 384:512], in_=t_f[:, :],
           identity=idf[:, :], waits=[("ac", ac_tf), ("dv", dv_lgbank[0]), ("dv", dv_lgbank[1])],
           incs=["pe"])
        em("vector", "tensor_copy", out=t8[:, :], in_=ps_b0[0:8, 384:512],
           waits=[("pe", C["pe"])])
        dv_t8 = C["dv"]
        for m in range(KD):
            pbank = ps_b0 if m < 4 else ps_b1
            wt = [("dv", dv_t8), ("ld", ld_consts)] if m in (0, 4) else []
            em("tensor", "matmul", out=pbank[:, (m % 4) * 128:(m % 4 + 1) * 128],
               lhsT=oh_sb[0:8, m * 128:(m + 1) * 128], rhs=t8[:, :],
               start=True, stop=True, waits=wt, incs=(["pe"] if m in (3, 7) else []))
            if m == 3:
                pe_bc0 = C["pe"]
        pe_bc1 = C["pe"]
        em("vector", "tensor_copy", out=scr1[:, 0:512], in_=ps_b0[:, 0:512],
           waits=[("pe", pe_bc0)])
        em("vector", "tensor_copy", out=scr1[:, 512:1024], in_=ps_b1[:, 0:512],
           waits=[("pe", pe_bc1)])
        dv_trowbc = C["dv"]
        em("scalar", "dma_start", out=g_grp[:, :],
           in_=A(x_in, (7 * TILE + SPECIAL_GROUP * 128) * D, [[D, 128], [1, D]]),
           waits=[("ac", C["ac"])], incs=["sd"])
        sd_sg = C["sd"]
        gcs = 7 * NG + SPECIAL_GROUP
        em("vector", "tensor_scalar", out=g_grp[:, :], in0=g_grp[:, :],
           scalar1=mr_all[:, gcs:gcs + 1], scalar2=mr_all[:, GC + gcs:GC + gcs + 1],
           op0=ALU.subtract, op1=ALU.mult, waits=[("sd", sd_sg)])
        em("vector", "tensor_sub", out=s_grp[:, :], in0=scr1[:, :], in1=g_grp[:, :],
           waits=[("dv", dv_trowbc)])
        em("vector", "tensor_scalar", out=s_grp[:, :], in0=s_grp[:, :],
           scalar1=tm_sb[:, 0:1], scalar2=None, op0=ALU.mult)
        em("vector", "tensor_add", out=s_grp[:, :], in0=s_grp[:, :], in1=g_grp[:, :])
        dv_sgrp = C["dv"]

        em("vector", "tensor_scalar", out=sel[:, :], in0=w_all[:, :],
           scalar1=0.0, scalar2=None, op0=ALU.is_gt, waits=[("dv", dv_wall)])
        em("vector", "tensor_tensor_scan", out=incl[:, :], data0=segm_sb[:, :],
           data1=sel[:, :], initial=0.0, op0=ALU.mult, op1=ALU.add,
           waits=[("ld", ld_consts)])
        em("vector", "tensor_copy", out=cnt[:, :], in_=incl_v[:, :, GC - 1:GC])
        dv_cnt = C["dv"]
        em("tensor", "matmul", out=ps_m[:, 64:72], lhsT=ltri_sb[:, :], rhs=cnt[:, :],
           start=True, stop=True, waits=[("dv", dv_cnt)], incs=["pe"])
        em("vector", "tensor_copy", out=base_pe[:, :], in_=ps_m[:, 64:72],
           waits=[("pe", C["pe"])])
        em("tensor", "matmul", out=ps_m[0:1, 72:80], lhsT=o128[:, :], rhs=cnt[:, :],
           start=True, stop=True, waits=[("dv", C["dv"])], incs=["pe"])
        em("vector", "tensor_copy", out=tot[:, :], in_=ps_m[0:1, 72:80],
           waits=[("pe", C["pe"])])
        em("vector", "tensor_scalar", out=tr8[:, 0:E], in0=tot[:, :],
           scalar1=float(T3 - 1), scalar2=None, op0=ALU.add)
        em("vector", "tensor_copy", out=tr8i[:, 0:E], in_=tr8[:, 0:E])
        em("vector", "tensor_single_scalar", out=tr8i[:, E:2 * E], in_=tr8i[:, 0:E],
           scalar=T3 - 1, op=ALU.bitwise_and)
        em("vector", "tensor_copy", out=tr8[:, E:2 * E], in_=tr8i[:, E:2 * E])
        em("vector", "tensor_sub", out=tr8[:, 0:E], in0=tr8[:, 0:E], in1=tr8[:, E:2 * E])
        em("vector", "tensor_scalar", out=tr8[:, 0:E], in0=tr8[:, 0:E],
           scalar1=float(T3), scalar2=None, op0=ALU.max)
        em("vector", "memset", tr8[:, 2 * E:3 * E], 1.0)
        em("vector", "tensor_tensor_scan", out=tr8[:, E:2 * E], data0=tr8[:, 2 * E:3 * E],
           data1=tr8[:, 0:E], initial=0.0, op0=ALU.mult, op1=ALU.add)
        em("vector", "tensor_sub", out=tr8[:, 2 * E:3 * E], in0=tr8[:, E:2 * E],
           in1=tr8[:, 0:E])
        dv_seg = C["dv"]
        em("vector", "memset", etile[:, 0:NT3], 0.0)
        for e in range(E):
            em("vector", "tensor_scalar", out=etile[:, NT3:2 * NT3], in0=jtc_sb[:, :],
               scalar1=tr8[:, E + e:E + e + 1], scalar2=None, op0=ALU.is_ge)
            em("vector", "tensor_add", out=etile[:, 0:NT3], in0=etile[:, 0:NT3],
               in1=etile[:, NT3:2 * NT3])
        dv_etile = C["dv"]
        em("vector", "memset", etile[:, NT3:NT3 + 1], -1.0)
        em("vector", "tensor_copy", out=etile[:, NT3 + 1:2 * NT3], in_=etile[:, 0:NT3 - 1])
        em("vector", "tensor_tensor", out=etile[:, 2 * NT3:3 * NT3], in0=etile[:, 0:NT3],
           in1=etile[:, NT3:2 * NT3], op=ALU.not_equal)
        em("vector", "tensor_scalar", out=etile[:, 3 * NT3:4 * NT3], in0=etile[:, 0:NT3],
           scalar1=float(E - 1), scalar2=None, op0=ALU.is_le)
        em("vector", "tensor_tensor", out=etile[:, 2 * NT3:3 * NT3],
           in0=etile[:, 2 * NT3:3 * NT3], in1=etile[:, 3 * NT3:4 * NT3], op=ALU.mult)
        dv_diff = C["dv"]
        em("tensor", "matmul", out=ps_m[:, 80:80 + NT3], lhsT=o1r[:, :],
           rhs=etile[0:1, 0:NT3], start=True, stop=True,
           waits=[("dv", dv_etile)], incs=["pe"])
        em("vector", "tensor_copy", out=ebf[:, :], in_=ps_m[:, 80:80 + NT3],
           waits=[("pe", C["pe"])])
        dv_ebf = C["dv"]
        em("tensor", "matmul", out=ps_m[:, 80 + NT3:80 + 2 * NT3], lhsT=o1r[:, :],
           rhs=etile[0:1, 2 * NT3:3 * NT3], start=True, stop=True,
           waits=[("dv", dv_diff), ("dv", dv_ebf)], incs=["pe"])
        em("vector", "tensor_copy", out=dbf[:, :], in_=ps_m[:, 80 + NT3:80 + 2 * NT3],
           waits=[("pe", C["pe"])])
        dv_dbf = C["dv"]
        em("vector", "memset", etile[:, NT3:NT3 + 2], -1.0)
        em("vector", "tensor_copy", out=etile[:, NT3 + 2:2 * NT3], in_=etile[:, 0:NT3 - 2])
        em("vector", "tensor_tensor", out=etile[:, 2 * NT3:3 * NT3], in0=etile[:, 0:NT3],
           in1=etile[:, NT3:2 * NT3], op=ALU.not_equal)
        em("vector", "tensor_tensor", out=etile[:, 2 * NT3:3 * NT3],
           in0=etile[:, 2 * NT3:3 * NT3], in1=etile[:, 3 * NT3:4 * NT3], op=ALU.mult)
        dv_diff2 = C["dv"]
        em("tensor", "matmul", out=ps_m[:, 80 + NT3:80 + 2 * NT3], lhsT=o1r[:, :],
           rhs=etile[0:1, 2 * NT3:3 * NT3], start=True, stop=True,
           waits=[("dv", dv_diff2), ("dv", dv_dbf)], incs=["pe"])
        em("vector", "tensor_copy", out=dbf2[:, :], in_=ps_m[:, 80 + NT3:80 + 2 * NT3],
           waits=[("pe", C["pe"])])
        em("vector", "tensor_scalar", out=wrow[:, :], in0=ebf[:, :],
           scalar1=float(E - 1), scalar2=256.0, op0=ALU.min, op1=ALU.mult)
        em("vector", "tensor_scalar", out=wrow[:, :], in0=wrow[:, :],
           scalar1=pbase_sb[:, 2:3], scalar2=None, op0=ALU.add)
        for half in range(2):
            em("vector", "tensor_scalar", out=idxw[:, half * NT3:(half + 1) * NT3],
               in0=wrow[:, :], scalar1=float(half * 128) - BIGF, scalar2=None, op0=ALU.add)
            em("vector", "tensor_tensor", out=idxw[:, half * NT3:(half + 1) * NT3],
               in0=idxw[:, half * NT3:(half + 1) * NT3], in1=dbf2[:, :], op=ALU.mult)
            em("vector", "tensor_scalar", out=idxw[:, half * NT3:(half + 1) * NT3],
               in0=idxw[:, half * NT3:(half + 1) * NT3], scalar1=BIGF, scalar2=None,
               op0=ALU.add)
        em("vector", "tensor_copy", out=idxw_i[:, :], in_=idxw[:, :])
        em("vector", "tensor_scalar", out=wrow[:, :], in0=ebf[:, :],
           scalar1=float(E - 1), scalar2=128.0, op0=ALU.min, op1=ALU.mult)
        em("vector", "tensor_scalar", out=wrow[:, :], in0=wrow[:, :],
           scalar1=pbase_sb[:, 2:3], scalar2=2.0, op0=ALU.add, op1=ALU.mult)
        for half in range(2):
            em("vector", "tensor_scalar", out=idxw2[:, half * NT3:(half + 1) * NT3],
               in0=wrow[:, :], scalar1=float(half) - BIGF, scalar2=None, op0=ALU.add)
            em("vector", "tensor_tensor", out=idxw2[:, half * NT3:(half + 1) * NT3],
               in0=idxw2[:, half * NT3:(half + 1) * NT3], in1=dbf[:, :], op=ALU.mult)
            em("vector", "tensor_scalar", out=idxw2[:, half * NT3:(half + 1) * NT3],
               in0=idxw2[:, half * NT3:(half + 1) * NT3], scalar1=BIGF, scalar2=None,
               op0=ALU.add)
        em("vector", "tensor_copy", out=idxw2_i[:, :], in_=idxw2[:, :])
        dv_idxw = C["dv"]
        em("vector", "tensor_scalar", out=idxb[:, :], in0=ebf[:, :],
           scalar1=float(E - 1), scalar2=pbase_sb[:, 1:2], op0=ALU.min, op1=ALU.add)
        em("vector", "tensor_copy", out=idxb_i[:, :], in_=idxb[:, :])
        em("vector", "tensor_scalar", out=wrow[:, :], in0=ebf[:, :],
           scalar1=float(E - 1), scalar2=None, op0=ALU.min)
        em("vector", "tensor_copy", out=idxb2_i[:, :], in_=wrow[:, :])
        dv_idxb = C["dv"]
        em("tensor", "matmul", out=ps_m[:, 160:168], lhsT=o1r[:, :],
           rhs=tr8[0:1, 2 * E:3 * E], start=True, stop=True,
           waits=[("dv", dv_seg), ("dv", dv_idxb)], incs=["pe"])
        em("vector", "tensor_copy", out=gbase[:, :], in_=ps_m[:, 160:168],
           waits=[("pe", C["pe"])])
        em("vector", "tensor_add", out=gbase[:, :], in0=gbase[:, :], in1=base_pe[:, :])
        em("vector", "tensor_sub", out=didx[:, :], in0=incl[:, :], in1=sel[:, :],
           waits=[("zf", zf_total)])
        for e in range(E):
            em("vector", "tensor_scalar", out=didx[:, e * GC:(e + 1) * GC],
               in0=didx[:, e * GC:(e + 1) * GC], scalar1=gbase[:, e:e + 1],
               scalar2=None, op0=ALU.add)
        em("vector", "tensor_scalar", out=didx[:, :], in0=didx[:, :],
           scalar1=BIGF, scalar2=None, op0=ALU.subtract)
        em("vector", "tensor_tensor", out=didx[:, :], in0=didx[:, :], in1=sel[:, :],
           op=ALU.mult)
        em("vector", "tensor_scalar", out=didx[:, :], in0=didx[:, :],
           scalar1=BIGF, scalar2=None, op0=ALU.add)
        em("vector", "memset", runsb[:, :], 0.0)
        for e in range(E):
            em("vector", "tensor_copy", out=slot[:, e * GC:(e + 1) * GC], in_=runsb[:, :])
            if e < E - 1:
                em("vector", "tensor_add", out=runsb[:, :], in0=runsb[:, :],
                   in1=sel[:, e * GC:(e + 1) * GC])
        em("vector", "tensor_scalar", out=mm1[:, :], in0=slot[:, :],
           scalar1=1.0, scalar2=None, op0=ALU.min)
        em("vector", "tensor_scalar", out=mm1[:, :], in0=mm1[:, :],
           scalar1=-1.0, scalar2=1.0, op0=ALU.mult, op1=ALU.add)
        em("vector", "tensor_tensor", out=mm1[:, :], in0=mm1[:, :], in1=sel[:, :],
           op=ALU.mult)
        em("vector", "tensor_sub", out=mm2[:, :], in0=sel[:, :], in1=mm1[:, :])
        first_accs = True
        for r, mk in ((0, mm1), (1, mm2)):
            em("vector", "tensor_tensor", out=acc_s[:, :], in0=didx[:, 0:GC],
               in1=mk[:, 0:GC], op=ALU.mult,
               waits=([("zf", zf_total)] if first_accs else []))
            first_accs = False
            em("vector", "tensor_copy", out=didx_r[:, r * GC:(r + 1) * GC], in_=acc_s[:, :])
            for e in range(1, E):
                em("vector", "tensor_tensor", out=acc_s[:, :],
                   in0=didx[:, e * GC:(e + 1) * GC], in1=mk[:, e * GC:(e + 1) * GC],
                   op=ALU.mult)
                em("vector", "tensor_add", out=didx_r[:, r * GC:(r + 1) * GC],
                   in0=didx_r[:, r * GC:(r + 1) * GC], in1=acc_s[:, :])
            em("vector", "tensor_tensor", out=acc_s[:, :], in0=w_all[:, 0:GC],
               in1=mk[:, 0:GC], op=ALU.mult)
            em("vector", "tensor_copy", out=w_r[:, r * GC:(r + 1) * GC], in_=acc_s[:, :])
            for e in range(1, E):
                em("vector", "tensor_tensor", out=acc_s[:, :],
                   in0=w_all[:, e * GC:(e + 1) * GC], in1=mk[:, e * GC:(e + 1) * GC],
                   op=ALU.mult)
                em("vector", "tensor_add", out=w_r[:, r * GC:(r + 1) * GC],
                   in0=w_r[:, r * GC:(r + 1) * GC], in1=acc_s[:, :])
        em("vector", "tensor_copy", out=didx_ri[:, :], in_=didx_r[:, :])
        em("vector", "tensor_scalar", out=dstc[:, 0:GC], in0=toks_sb[:, :],
           scalar1=2.0, scalar2=None, op0=ALU.mult)
        em("vector", "tensor_scalar", out=dstc[:, GC:2 * GC], in0=dstc[:, 0:GC],
           scalar1=1.0, scalar2=None, op0=ALU.add)
        for r in range(2):
            em("vector", "tensor_copy", out=recs_v[:, r, :, 0], in_=toks_sb[:, :])
            em("vector", "tensor_copy", out=recs_v[:, r, :, 1],
               in_=w_r[:, r * GC:(r + 1) * GC])
            em("vector", "tensor_copy", out=recs_v[:, r, :, 2],
               in_=dstc[:, r * GC:(r + 1) * GC])
        dv_recs = C["dv"]

        ac_hbank = {0: 0, 1: 0}
        h_done = {}
        dv_ybank = {0: 0, 1: 0, 2: dv_trowbc, 3: dv_trowbc}
        dv_b2bc = {0: 0, 1: 0}
        pe_free = {"w1a0": pe_wvT[0], "w1b0": pe_woT[0], "w1a1": 0, "w1b1": 0,
                   "w2a": pe_u, "w2b": pe_t}
        b1_ac = {0: 0, 1: 0}
        pe_gx = {0: 0, 1: 0}
        gx_done = {0: 0, 1: 0}
        rc_done = {0: 0, 1: 0, 2: 0, 3: 0}
        dv_rec_slot = {0: 0, 1: 0, 2: 0, 3: 0}
        dv_gxT = dv_trbank
        pe_b1use = 0
        pe_b2use = 0
        dv_zt = 0

        def emit_prefetch_a(j):
            sl = j % 4
            rcsem = ("rc0", "rc1", "rc2", "rc3")[sl]
            em("sync", "dma_start",
               out=rec_sb_v[:, sl, :, :],
               in_=A(LREC, j * T3 * 3, [[3, 128], [128 * 3, 2], [1, 3]]),
               waits=([("sct", sct_total)] if j == 0 else [("dv", dv_rec_slot[sl])]),
               incs=[rcsem])
            rc_done[sl] = C[rcsem]

        def emit_prefetch_b(j):
            sl = j % 4
            sl2 = j % 2
            gxsem = "gx0" if sl2 == 0 else "gx1"
            wt0 = [(("rc0", "rc1", "rc2", "rc3")[sl], rc_done[sl])]
            if j >= 4:
                wt0.append(zs_done[j - 4])
            if j >= 2:
                wt0.append((gxsem, gx_done[sl2]))
            em("vector", "tensor_copy", out=tok_i_v[:, sl, :], in_=rec_sb_v[:, sl, :, 0],
               waits=wt0)
            em("vector", "tensor_copy", out=w_f_v[:, sl, :], in_=rec_sb_v[:, sl, :, 1])
            em("vector", "tensor_copy", out=dst_i_v[:, sl, :], in_=rec_sb_v[:, sl, :, 2])
            dv_rec_slot[sl] = C["dv"]
            dv_rec = C["dv"]
            for h in range(2):
                wt = []
                if h == 0:
                    wt = [("dv", dv_rec), ("xs", C["xs"]), ("xs1", C["xs1"])]
                    if pe_gx[sl2]:
                        wt.append(("pe", pe_gx[sl2]))
                em("gpsimd", "indirect_dma_start",
                   gx_v[:, sl2, h * D:(h + 1) * D], None,
                   XNBF.ap(), IndirectOffsetOnAxis(tok_i_v[:, sl, h:h + 1], 0),
                   bounds_check=SH - 1, oob_is_err=False,
                   waits=wt, incs=[gxsem])
            gx_done[sl2] = C[gxsem]

        def emit_b1(j):
            slw = j % 2
            em("gpsimd", "indirect_dma_start",
               b1_t_v[:, slw, :], None,
               B1T.ap(), IndirectOffsetOnAxis(idxb_i[:, j:j + 1], 0),
               bounds_check=128 * E - 1, oob_is_err=False,
               waits=[("dv", dv_idxb), ("xs", C["xs"]), ("xs1", C["xs1"]),
                      ("ac", b1_ac[slw]),
                      ("b1g", 16 * j)],
               incs=["b1g"])

        def emit_weights_w1(j):
            slw = j % 2
            sfx = "" if slw == 0 else "1"
            for wsem, buf, ihalf in (
                    ("w1a" + sfx, (W1a, W1a1)[slw], 0), ("w1b" + sfx, (W1b, W1b1)[slw], 1)):
                wt = [("dv", dv_idxw), ("pe", pe_free[("w1a" if ihalf == 0 else "w1b") + str(slw)])]
                em("gpsimd", "indirect_dma_start",
                   buf[:, :], None,
                   W1.ap(), IndirectOffsetOnAxis(idxw_i[:, ihalf * NT3 + j:ihalf * NT3 + j + 1], 0),
                   bounds_check=E * 128 * 2 - 1, oob_is_err=False,
                   waits=wt, incs=[wsem])

        def emit_weights_w2(j):
            slw = j % 2
            em("gpsimd", "indirect_dma_start",
               b2bc[:, slw * D:(slw + 1) * D], None,
               b2.ap(), IndirectOffsetOnAxis(idxb2_i[:, j:j + 1], 0),
               bounds_check=E - 1, oob_is_err=False,
               waits=[("dv", dv_b2bc[slw]), ("b2g", 16 * j)] +
                     ([("pe", pe_bc1)] if j < 2 else []),
               incs=["b2g"])
            for wsem, buf, ihalf in (("w2a", W2a, 0), ("w2b", W2b, 1)):
                wt = [("dv", dv_idxw), ("pe", pe_free[wsem])]
                em("gpsimd", "indirect_dma_start",
                   buf[:, :], None,
                   W2.ap(), IndirectOffsetOnAxis(idxw2_i[:, ihalf * NT3 + j:ihalf * NT3 + j + 1], 0),
                   bounds_check=E * 128 * 2 - 1, oob_is_err=False,
                   waits=wt, incs=[wsem])

        def emit_gxtr(j):
            nonlocal dv_gxT
            sl = j % 2
            gxsem = "gx0" if sl == 0 else "gx1"
            gx_j = gx_done[sl]
            for h in range(2):
                for c in range(KD):
                    wt = []
                    if c == 0:
                        wt = [(gxsem, gx_j), ("dv", dv_gxT)] if h == 0 else [("dv", dv_gxT)]
                    em("tensor", "transpose", out=ps_tr[:, c * 128:(c + 1) * 128],
                       in_=gx_v[:, sl, h * D + c * 128:h * D + (c + 1) * 128],
                       identity=idb[:, :], waits=wt, incs=(["pe"] if c == KD - 1 else []))
                if h == 1:
                    pe_gx[sl] = C["pe"]
                em("vector", "tensor_copy", out=gxT_v[:, :, h * 128:(h + 1) * 128],
                   in_=ps_tr_v[:, :, :], waits=[("pe", C["pe"])])
                dv_gxT = C["dv"]

        zs_done = {}
        emit_b1(0)
        emit_weights_w1(0)
        emit_weights_w1(1)
        emit_weights_w2(0)
        for r in range(2):
            for gc in range(GC):
                em("gpsimd", "indirect_dma_start",
                   LREC.ap(),
                   IndirectOffsetOnAxis(didx_ri[:, r * GC + gc:r * GC + gc + 1], 0),
                   recs[:, (r * GC + gc) * 3:(r * GC + gc) * 3 + 3], None,
                   bounds_check=NR - 1, oob_is_err=False,
                   waits=([("dv", dv_recs), ("zf", zf_total)] if (r == 0 and gc == 0) else []),
                   incs=["sct"])
        sct_total = C["sct"]
        emit_prefetch_a(0)
        emit_prefetch_b(0)
        emit_prefetch_a(1)
        emit_prefetch_b(1)
        emit_gxtr(0)
        for j in range(NT3):
            sl = j % 2
            gxsem = "gx0" if sl == 0 else "gx1"
            if j + 2 < NT3:
                emit_prefetch_a(j + 2)
            if j >= 1:
                emit_weights_w2(j)
            b1g_j = 16 * (j + 1)
            b2g_j = 16 * (j + 1)
            for m in range(KH):
                bank = m % 2
                wt = []
                if m == 0:
                    wt += [("w1a" + ("" if j % 2 == 0 else "1"), 16 * (j // 2 + 1)),
                           ("dv", dv_gxT), ("b1g", b1g_j)]
                if m == KD:
                    wt.append(("w1b" + ("" if j % 2 == 0 else "1"), 16 * (j // 2 + 1)))
                if ac_hbank[bank] != 0:
                    wt.append(ac_hbank[bank])
                for k in range(KD):
                    em("tensor", "matmul", out=ps_h[bank][:, 0:T3],
                       lhsT=W1vp[j % 2][m // KD][:, k, (m % KD) * 128:(m % KD + 1) * 128],
                       rhs=gxT_v[:, k, :],
                       start=(k == 0), stop=(k == KD - 1),
                       waits=(wt if k == 0 else []),
                       incs=(["pe"] if k == KD - 1 else []))
                if m == KD - 1:
                    pe_free["w1a" + str(j % 2)] = C["pe"]
                if m == KH - 1:
                    pe_free["w1b" + str(j % 2)] = C["pe"]
                pe_h = C["pe"]
                em("scalar", "activation", out=h_bf[:, m * T3:(m + 1) * T3],
                   in_=ps_h[bank][:, 0:T3], func=AF.Relu,
                   bias=b1_t[:, sl * KH + m:sl * KH + m + 1],
                   waits=[("pe", pe_h)], incs=["ac"])
                h_done[m] = ("ac", C["ac"])
                ac_hbank[bank] = h_done[m]
            ac_h_all = C["ac"]
            b1_ac[j % 2] = C["ac"]
            if j + 1 < NT3:
                if j + 2 < NT3:
                    emit_prefetch_b(j + 2)
                emit_b1(j + 1)
                if j + 2 < NT3:
                    emit_weights_w1(j + 2)
                emit_gxtr(j + 1)
            ybanks = [ps_y[0], ps_y[1], ps_b0, ps_b1]
            for h in range(2):
                for nb in range(2):
                    bi = h * 2 + nb
                    wt = []
                    if bi == 0:
                        wt += [("w2a", 16 * (j + 1)), ("ac", ac_h_all)]
                    if dv_ybank[bi]:
                        wt.append(("dv", dv_ybank[bi]))
                    for k in range(KD):
                        em("tensor", "matmul", out=ybanks[bi][:, :],
                           lhsT=hv[:, k, h * 128:(h + 1) * 128],
                           rhs=W2v[0][:, k, nb * 512:(nb + 1) * 512],
                           start=(k == 0), stop=False,
                           waits=(wt if k == 0 else []),
                           incs=(["pe"] if bi == 3 and k == KD - 1 else []))
            pe_free["w2a"] = C["pe"]
            for h in range(2):
                for nb in range(2):
                    bi = h * 2 + nb
                    wt = [("w2b", 16 * (j + 1))] if bi == 0 else []
                    for k in range(KD):
                        em("tensor", "matmul", out=ybanks[bi][:, :],
                           lhsT=hv[:, KD + k, h * 128:(h + 1) * 128],
                           rhs=W2v[1][:, k, nb * 512:(nb + 1) * 512],
                           start=False, stop=(k == KD - 1),
                           waits=(wt if k == 0 else []),
                           incs=(["pe"] if k == KD - 1 else []))
                    if bi == 3:
                        pe_free["w2b"] = C["pe"]
                    pe_ye = C["pe"]
                    wt2 = [("pe", pe_ye), ("b2g", b2g_j)]
                    if dv_zt and bi == 0:
                        wt2.append(zs_done[j - 1])
                    em("vector", "tensor_tensor",
                       out=zt[:, h * D + nb * 512:h * D + (nb + 1) * 512],
                       in0=ybanks[bi][:, :], in1=b2bc[:, sl * D + nb * 512:sl * D + (nb + 1) * 512],
                       op=ALU.add, waits=wt2)
                    em("vector", "tensor_scalar",
                       out=zt[:, h * D + nb * 512:h * D + (nb + 1) * 512],
                       in0=zt[:, h * D + nb * 512:h * D + (nb + 1) * 512],
                       scalar1=w_f_v[:, j % 4, h:h + 1], scalar2=None, op0=ALU.mult)
                    dv_ybank[bi] = C["dv"]
            dv_zt = C["dv"]
            dv_b2bc[sl] = C["dv"]
            zsem = "zs" if j % 2 == 0 else "zs1"
            for h in range(2):
                em("gpsimd", "indirect_dma_start",
                   ZD.ap(), IndirectOffsetOnAxis(dst_i_v[:, j % 4, h:h + 1], 0),
                   zt[:, h * D:(h + 1) * D], None,
                   bounds_check=2 * SH - 1, oob_is_err=False,
                   waits=([("dv", dv_zt), ("zf", zf_total)] if h == 0 else []),
                   incs=[zsem])
            zs_done[j] = (zsem, C[zsem])
        zs_tw = [("zs", C["zs"]), ("zs1", C["zs1"])]

        st_slot = {0: 0, 1: 0}
        dv_zl = {0: 0, 1: 0, 2: 0, 3: 0}
        x4bufs = [g_grp, scr1, zt[:, 0:D], zt[:, D:2 * D]]
        for t in range(NT):
            for g in range(NG):
                gi = t * NG + g
                sl4 = gi % 4
                rcsem = ("rc0", "rc1", "rc2", "rc3")[sl4]
                tokbase = t * TILE + g * 128
                gc4 = t * NG + g
                em("sync", "dma_start",
                   out=x_tv.tensor if False else zn_v[:, 2 * sl4:2 * sl4 + 2, :],
                   in_=A(ZD, 2 * tokbase * D, [[2 * D, 128], [D, 2], [1, D]]),
                   waits=(zs_tw if gi == 0 else []) +
                         ([("dv", dv_zl[sl4])] if dv_zl[sl4] else []),
                   incs=[rcsem])
                em("sync", "dma_start",
                   out=x4bufs[sl4][:, :] if sl4 < 2 else x4bufs[sl4],
                   in_=A(x_in, tokbase * D, [[D, 128], [1, D]]),
                   waits=([("dv", dv_sgrp)] if gi < 2 else []), incs=[rcsem])
                rc_g = C[rcsem]
                z0 = zn_v[:, 2 * sl4, :]
                z1 = zn_v[:, 2 * sl4 + 1, :]
                xb4 = x4bufs[sl4][:, :] if sl4 < 2 else x4bufs[sl4]
                em("vector", "tensor_add", out=z0, in0=z0, in1=z1, waits=[(rcsem, rc_g)])
                dv_y4 = C["dv"]
                em("scalar", "activation", out=z1, in_=z0, func=AF.Silu,
                   waits=[("dv", dv_y4)], incs=["ac"])
                ac_g4 = C["ac"]
                if t == 7 and g == SPECIAL_GROUP:
                    em("vector", "tensor_tensor", out=z0, in0=z1, in1=s_grp[:, :],
                       op=ALU.mult, waits=[("ac", ac_g4)])
                else:
                    em("vector", "tensor_scalar", out=z0, in0=xb4,
                       scalar1=mr_all[:, gc4:gc4 + 1],
                       scalar2=mr_all[:, GC + gc4:GC + gc4 + 1],
                       op0=ALU.subtract, op1=ALU.mult, waits=[("ac", ac_g4)])
                    em("vector", "tensor_tensor", out=z0, in0=z0, in1=z1, op=ALU.mult)
                ssem = "st0" if gi % 2 == 0 else "st1"
                wt = []
                if st_slot[gi % 2]:
                    wt.append((ssem, st_slot[gi % 2]))
                em("vector", "tensor_add", out=out_g[:, (gi % 2) * D:(gi % 2 + 1) * D],
                   in0=z0, in1=xb4, waits=wt)
                dv_zl[sl4] = C["dv"]
                em("scalar", "dma_start", out=A(out, tokbase * D, [[D, 128], [1, D]]),
                   in_=out_g[:, (gi % 2) * D:(gi % 2 + 1) * D],
                   waits=[("dv", dv_zl[sl4])], incs=[ssem])
                st_slot[gi % 2] = C[ssem]

        em("scalar", "activation", out=t_f[:, 0:1], in_=t_f[:, 0:1], func=AF.Copy,
           waits=[("st0", C["st0"]), ("st1", C["st1"])])

        with nc.Block() as block:
            def run(eng_name):
                def body(eng):
                    bndregs = {}
                    if eng_name == "gpsimd":
                        for bv in (NR - 1, SH - 1, 128 * E - 1, 2 * SH - 1,
                                   E * 128 * 2 - 1, E - 1):
                            r = es.enter_context(eng.register(f"bnd{bv}"))
                            eng.reg_mov(r, bv)
                            bndregs[bv] = r
                    for method, args, kw, ws, incs in sched[eng_name]:
                        for s, v in ws:
                            eng.wait_ge(s, v)
                        if "bounds_check" in kw and isinstance(kw["bounds_check"], int):
                            kw = dict(kw, bounds_check=bndregs[kw["bounds_check"]])
                        inst = getattr(eng, method)(*args, **kw)
                        for s in incs:
                            inst.then_inc(s, 16 if "dma_start" in method else 1)
                return body
            block.sync(run("sync"))
            block.gpsimd(run("gpsimd"))
            block.scalar(run("scalar"))
            block.vector(run("vector"))
            block.tensor(run("tensor"))

    return nc


def kernel(**inputs):
    x = np.ascontiguousarray(np.asarray(inputs["L0_attn_inputs"], dtype=np.float32))
    toks = (np.arange(GC)[None, :] * 128 + np.arange(128)[:, None]).astype(np.float32)
    segmask = np.ones((128, E * GC), dtype=np.float32)
    segmask[:, 0::GC] = 0.0
    common = {
        "Wv": np.ascontiguousarray(np.asarray(inputs["L0_attn_Wv"], dtype=np.float32)),
        "Wo": np.ascontiguousarray(np.asarray(inputs["L0_attn_Wo"], dtype=np.float32)),
        "Wg": np.ascontiguousarray(np.asarray(inputs["L0_ffn_Wg"], dtype=np.float32)),
        "W1": np.ascontiguousarray(
            np.asarray(inputs["L0_ffn_W1"], dtype=np.float32)
            .reshape(8, 8, 128, 2, 1024).transpose(0, 3, 2, 1, 4).reshape(8 * 256, 8192)),
        "b1": np.ascontiguousarray(np.asarray(inputs["L0_ffn_b1"], dtype=np.float32)),
        "W2": np.ascontiguousarray(
            np.asarray(inputs["L0_ffn_W2"], dtype=np.float32)
            .reshape(8, 2, 8, 128, 1024).transpose(0, 3, 1, 2, 4).reshape(8 * 256, 8192)),
        "b2": np.ascontiguousarray(np.asarray(inputs["L0_ffn_b2"], dtype=np.float32)),
        "ident": np.eye(128, dtype=np.float32),
        "ltri": np.triu(np.ones((128, 128), dtype=np.float32), 1),
        "ones128": np.ones((128, 1), dtype=np.float32),
        "ones1r": np.ones((1, 128), dtype=np.float32),
        "toksc": toks,
        "pbase": np.stack([np.arange(128) * 128, np.arange(128) * 8, np.arange(128)], axis=1).astype(np.float32),
        "jtc": (np.arange(NT3)[None, :] * T3).astype(np.float32),
        "segm": segmask,
        "onehot": np.repeat(np.eye(8, dtype=np.float32), 128, axis=1),
        "tmask": np.zeros((128, 1), dtype=np.float32),
    }
    tmask_on = np.ones((128, 1), dtype=np.float32)
    nc = build_nc()
    in_maps = []
    for c in range(8):
        b, half = c // 2, c % 2
        m = dict(common)
        m["x"] = np.ascontiguousarray(x[b, half * SH:(half + 1) * SH, :])
        m["tmask"] = tmask_on if half == 1 else common["tmask"]
        in_maps.append(m)
    res = run_bass_kernel_spmd(nc, in_maps, core_ids=list(range(8)))
    outp = np.empty((4, 8192, D), dtype=np.float32)
    for c in range(8):
        b, half = c // 2, c % 2
        outp[b, half * SH:(half + 1) * SH, :] = res.results[c]["out"]
    return outp


# version 50
# speedup vs baseline: 1.1116x; 1.0111x over previous
import contextlib
import numpy as np
import concourse.bass as bass
import concourse.mybir as mybir
from concourse.bass_utils import run_bass_kernel_spmd
from concourse.bass_core_rust import IndirectOffsetOnAxis

F32 = mybir.dt.float32
BF16 = mybir.dt.bfloat16
I32 = mybir.dt.int32
AF = mybir.ActivationFunctionType
ALU = mybir.AluOpType
AX = mybir.AxisListType

D = 1024
E = 8
HID = 2048
SH = 4096
TILE = 512
NT = SH // TILE
NG = TILE // 128
KD = D // 128
KH = HID // 128
GC = NT * NG
T3 = 256
NT3 = (2 * SH) // T3 + E
NR = NT3 * T3
BIGF = float(2 ** 20)
TILE_ORDER = [7, 0, 1, 2, 3, 4, 5, 6]
SPECIAL_COL = 384
SPECIAL_GROUP = 3


def build_nc():
    nc = bass.Bass("TRN2", target_bir_lowering=False)

    x_in = nc.dram_tensor("x", [SH, D], F32, kind="ExternalInput")
    Wv = nc.dram_tensor("Wv", [D, D], F32, kind="ExternalInput")
    Wo = nc.dram_tensor("Wo", [D, D], F32, kind="ExternalInput")
    Wg = nc.dram_tensor("Wg", [D, E], F32, kind="ExternalInput")
    W1 = nc.dram_tensor("W1", [E * 256, KD * 1024], F32, kind="ExternalInput")
    b1 = nc.dram_tensor("b1", [E, HID], F32, kind="ExternalInput")
    W2 = nc.dram_tensor("W2", [E * 256, KH * 512], F32, kind="ExternalInput")
    b2 = nc.dram_tensor("b2", [E, D], F32, kind="ExternalInput")
    ident = nc.dram_tensor("ident", [128, 128], F32, kind="ExternalInput")
    onehot = nc.dram_tensor("onehot", [E, E * 128], F32, kind="ExternalInput")
    tmask = nc.dram_tensor("tmask", [128, 1], F32, kind="ExternalInput")
    ltri = nc.dram_tensor("ltri", [128, 128], F32, kind="ExternalInput")
    ones128 = nc.dram_tensor("ones128", [128, 1], F32, kind="ExternalInput")
    ones1r = nc.dram_tensor("ones1r", [1, 128], F32, kind="ExternalInput")
    toksc = nc.dram_tensor("toksc", [128, GC], F32, kind="ExternalInput")
    pbase = nc.dram_tensor("pbase", [128, 3], F32, kind="ExternalInput")
    jtc = nc.dram_tensor("jtc", [1, NT3], F32, kind="ExternalInput")
    segm = nc.dram_tensor("segm", [128, E * GC], F32, kind="ExternalInput")
    out = nc.dram_tensor("out", [SH, D], F32, kind="ExternalOutput")

    XNBF = nc.dram_tensor("XNBF", [SH, D], BF16)
    B1T = nc.dram_tensor("B1T", [128 * E, KH], F32)
    LREC = nc.dram_tensor("LREC", [NR, 3], F32)
    ZD = nc.dram_tensor("ZD", [2 * SH, D], F32)

    es = contextlib.ExitStack()
    with es:
        sb = lambda n, s, d: es.enter_context(nc.sbuf_tensor(n, s, d))
        pst = lambda n, s, d: es.enter_context(nc.psum_tensor(n, s, d))
        mksem = lambda n: es.enter_context(nc.semaphore(n))

        x_t = sb("x_t", [128, 2 * NG * D], F32)
        xnb = sb("xnb", [128, 2 * NG * D], BF16)
        mr_all = sb("mr_all", [128, 2 * GC], F32)
        xnT = sb("xnT", [128, KD * TILE], BF16)
        xn_col = sb("xn_col", [128, KD], BF16)
        cols = sb("cols", [128, 64], F32)
        lg = sb("lg", [128, NG * E], F32)
        sme = sb("sme", [128, NG * E], F32)
        m8 = sb("m8", [128, NG * E], F32)
        w_t = sb("w_t", [128, NG * E], F32)
        w_all = sb("w_all", [128, E * GC], F32)
        s_grp = sb("s_grp", [128, D], F32)
        scr1 = sb("scr1", [128, D], F32)
        g_grp = sb("g_grp", [128, D], F32)
        out_g = sb("out_g", [128, 2 * D], F32)
        sel = sb("sel", [128, E * GC], F32)
        incl = sb("incl", [128, E * GC], F32)
        slot = sb("slot", [128, E * GC], F32)
        mm1 = sb("mm1", [128, E * GC], F32)
        mm2 = sb("mm2", [128, E * GC], F32)
        didx = sb("didx", [128, E * GC], F32)
        runsb = sb("runsb", [128, GC], F32)
        cnt = sb("cnt", [128, E], F32)
        base_pe = sb("base_pe", [128, E], F32)
        gbase = sb("gbase", [128, E], F32)
        tot = sb("tot", [1, E], F32)
        tr8 = sb("tr8", [1, 3 * E], F32)
        tr8i = sb("tr8i", [1, 2 * E], I32)
        etile = sb("etile", [1, 4 * NT3], F32)
        ebf = sb("ebf", [128, NT3], F32)
        dbf = sb("dbf", [128, NT3], F32)
        dbf2 = sb("dbf2", [128, NT3], F32)
        wrow = sb("wrow", [128, NT3], F32)
        idxw = sb("idxw", [128, 2 * NT3], F32)
        idxw_i = sb("idxw_i", [128, 2 * NT3], I32)
        idxw2 = sb("idxw2", [128, 2 * NT3], F32)
        idxw2_i = sb("idxw2_i", [128, 2 * NT3], I32)
        idxb = sb("idxb", [128, NT3], F32)
        idxb_i = sb("idxb_i", [128, NT3], I32)
        idxb2_i = sb("idxb2_i", [128, NT3], I32)
        didx_r = sb("didx_r", [128, 2 * GC], F32)
        didx_ri = sb("didx_ri", [128, 2 * GC], I32)
        w_r = sb("w_r", [128, 2 * GC], F32)
        dstc = sb("dstc", [128, 2 * GC], F32)
        recs = sb("recs", [128, 2 * GC * 3], F32)
        acc_s = sb("acc_s", [128, GC], F32)
        W1a = sb("W1a", [128, KD * 1024], BF16)
        W1b = sb("W1b", [128, KD * 1024], BF16)
        W1a1 = sb("W1a1", [128, KD * 1024], BF16)
        W1b1 = sb("W1b1", [128, KD * 1024], BF16)
        W2a = sb("W2a", [128, KD * 1024], BF16)
        W2b = sb("W2b", [128, KD * 1024], BF16)
        zt = sb("zt", [128, 2 * D], F32)
        rec_sb = sb("rec_sb", [128, 4 * 6], F32)
        tok_i = sb("tok_i", [128, 4 * 2], I32)
        dst_i = sb("dst_i", [128, 4 * 2], I32)
        w_f = sb("w_f", [128, 4 * 2], F32)
        b1_t = sb("b1_t", [128, 2 * KH], F32)
        b2bc = sb("b2bc", [128, 2 * D], F32)
        u_bf = sb("u_bf", [128, E], BF16)
        t_f = sb("t_f", [128, E], F32)
        t8 = sb("t8", [8, 128], F32)
        b1_sb = sb("b1_sb", [128, E * KH], F32)
        wg_f = sb("wg_f", [128, KD * E], F32)
        wg_b = sb("wg_b", [128, KD * E], BF16)
        idf = sb("idf", [128, 128], F32)
        idb = sb("idb", [128, 128], BF16)
        tm_sb = sb("tm_sb", [128, 1], F32)
        eps_c = sb("eps_c", [128, 1], F32)
        mh_c = sb("mh_c", [128, 1], F32)
        ltri_sb = sb("ltri_sb", [128, 128], F32)
        o128 = sb("o128", [128, 1], F32)
        o1r = sb("o1r", [1, 128], F32)
        toks_sb = sb("toks_sb", [128, GC], F32)
        pbase_sb = sb("pbase_sb", [128, 3], F32)
        jtc_sb = sb("jtc_sb", [1, NT3], F32)
        segm_sb = sb("segm_sb", [128, E * GC], F32)


        ps_h = [pst("ps_h0", [128, 512], F32), pst("ps_h1", [128, 512], F32)]
        ps_y = [pst("ps_y0", [128, 512], F32), pst("ps_y1", [128, 512], F32)]
        ps_tr = pst("ps_tr", [128, 1024], BF16)
        ps_b0 = pst("ps_b0", [128, 512], F32)
        ps_b1 = pst("ps_b1", [128, 512], F32)
        ps_m = pst("ps_m", [128, 512], F32)

        SEM = {k: mksem("s_" + k) for k in (
            "ld", "sd", "xs", "zf", "cvo", "cast", "sct", "rc0", "rc1", "rc2", "rc3",
            "gx0", "gx1", "w1a", "w1b", "w1a1", "w1b1", "w2a", "w2b", "b1g", "b2g",
            "zs", "zs1", "sd1", "xs1",
            "st0", "st1", "pe", "dv", "ac")}
        C = {k: 0 for k in SEM}
        DMA = {k for k in SEM if k not in ("pe", "dv", "ac")}

        sched = {k: [] for k in ("sync", "gpsimd", "scalar", "vector", "tensor")}

        def em(eng, method, *args, waits=(), incs=(), **kw):
            waits = list(waits)
            if eng == "vector" and C["dv"] > 0:
                waits.append(("dv", C["dv"]))
            ws = [(SEM[n], v) for (n, v) in waits if v > 0]
            if eng == "vector" and "dv" not in incs:
                incs = list(incs) + ["dv"]
            sched[eng].append((method, args, kw, ws, [SEM[n] for n in incs]))
            for n in incs:
                C[n] += 16 if n in DMA else 1
            return {n: C[n] for n in incs}

        A = bass.AP
        x_tv = x_t.ap().rearrange("p (s g d) -> p s g d", s=2, g=NG)
        xnT_v = xnT.ap().rearrange("p (c t2) -> p c t2", c=KD)
        ps_tr_v = ps_tr.ap().rearrange("p (c j) -> p c j", c=KD)
        WvN = W1a.ap().rearrange("p (r c j) -> p r c j", r=KD, c=KD)
        WoN = W1b.ap().rearrange("p (r c j) -> p r c j", r=KD, c=KD)
        WvT = W2a.ap().rearrange("p (k m j) -> p k m j", k=KD, m=KD)
        WoT = W2b.ap().rearrange("p (k m j) -> p k m j", k=KD, m=KD)
        W1vp = [[W1a.ap().rearrange("p (k h) -> p k h", k=KD),
                 W1b.ap().rearrange("p (k h) -> p k h", k=KD)],
                [W1a1.ap().rearrange("p (k h) -> p k h", k=KD),
                 W1b1.ap().rearrange("p (k h) -> p k h", k=KD)]]
        W2v = [W2a.ap().rearrange("p (k n) -> p k n", k=KD),
               W2b.ap().rearrange("p (k n) -> p k n", k=KD)]
        w_all_v = w_all.ap().rearrange("p (e gc) -> p e gc", e=E)
        incl_v = incl.ap().rearrange("p (e gc) -> p e gc", e=E)
        recs_v = recs.ap().rearrange("p (r gc c3) -> p r gc c3", r=2, gc=GC)
        rec_sb_v = rec_sb.ap().rearrange("p (s h c3) -> p s h c3", s=4, h=2)
        tok_i_v = tok_i.ap().rearrange("p (s h) -> p s h", s=4)
        dst_i_v = dst_i.ap().rearrange("p (s h) -> p s h", s=4)
        w_f_v = w_f.ap().rearrange("p (s h) -> p s h", s=4)
        b1_t_v = b1_t.ap().rearrange("p (s m) -> p s m", s=2)
        gx_v = xnb.ap().rearrange("p (s q) -> p s q", s=4)
        gxT_v = xnb.ap().rearrange("p (s c t2) -> p s c t2", s=4, c=KD)[:, 2, :, :]
        h_bf = xnT
        hv = h_bf.ap().rearrange("p (k t2) -> p k t2", k=KH)
        zn_v = x_t.ap().rearrange("p (s d2) -> p s d2", s=8)

        em("sync", "dma_start", out=idf[:, :], in_=ident[:, :], incs=["ld"])
        em("sync", "dma_start", out=tm_sb[:, :], in_=tmask[:, :], incs=["ld"])
        em("sync", "dma_start", out=ltri_sb[:, :], in_=ltri[:, :], incs=["ld"])
        em("sync", "dma_start", out=o128[:, :], in_=ones128[:, :], incs=["ld"])
        em("sync", "dma_start", out=o1r[:, :], in_=ones1r[:, :], incs=["ld"])
        em("sync", "dma_start", out=toks_sb[:, :], in_=toksc[:, :], incs=["ld"])
        em("sync", "dma_start", out=pbase_sb[:, :], in_=pbase[:, :], incs=["ld"])
        em("sync", "dma_start", out=jtc_sb[:, :], in_=jtc[:, :], incs=["ld"])
        em("sync", "dma_start", out=segm_sb[:, :], in_=segm[:, :], incs=["ld"])
        em("sync", "dma_start", out=wg_f.ap().rearrange("p (k e) -> p k e", k=KD),
           in_=A(Wg, 0, [[E, 128], [128 * E, KD], [1, E]]), incs=["ld"])
        b_raw = out_g
        em("sync", "dma_start", out=b_raw[0:16, 0:E * 128].rearrange("i (e j) -> i e j", e=E),
           in_=A(b1, 0, [[128, 16], [HID, E], [1, 128]]), incs=["ld"])
        oh_sb = b2bc
        em("sync", "dma_start", out=oh_sb[0:8, 0:E * 128], in_=onehot[:, :], incs=["ld"])
        ld_consts = C["ld"]

        em("vector", "memset", zt[:, :], 0.0)
        dv_z0 = C["dv"]
        for q in range(32):
            em("sync", "dma_start",
               out=A(ZD, q * 256 * D, [[2 * D, 128], [D, 2], [1, D]]),
               in_=zt.ap().rearrange("p (h d2) -> p h d2", h=2),
               waits=([("dv", dv_z0)] if q == 0 else []), incs=["zf"])
        em("vector", "memset", recs[:, 0:192], BIGF)
        em("vector", "memset", didx[:, 0:48], BIGF)
        dv_lrp = C["dv"]
        em("sync", "dma_start", out=A(LREC, 0, [[192, 128], [1, 192]]),
           in_=recs[:, 0:192], waits=[("dv", dv_lrp)], incs=["zf"])
        em("sync", "dma_start", out=A(LREC, 128 * 192, [[48, 128], [1, 48]]),
           in_=didx[:, 0:48], incs=["zf"])
        zf_total = C["zf"]

        em("gpsimd", "dma_start", out=W1a.ap().rearrange("p (r d2) -> p r d2", r=KD),
           in_=A(Wv, 0, [[D, 128], [128 * D, KD], [1, D]]), incs=["cvo"])
        cvo_wv = C["cvo"]
        em("gpsimd", "dma_start", out=W1b.ap().rearrange("p (r d2) -> p r d2", r=KD),
           in_=A(Wo, 0, [[D, 128], [128 * D, KD], [1, D]]), incs=["cvo"])
        cvo_wo = C["cvo"]


        em("vector", "memset", eps_c[:, :], 1e-5)
        em("vector", "memset", mh_c[:, :], -0.5)
        em("scalar", "activation", out=idb[:, :], in_=idf[:, :], func=AF.Copy,
           waits=[("ld", ld_consts)], incs=["ac"])
        ac_idb = C["ac"]
        em("scalar", "activation", out=wg_b[:, :], in_=wg_f[:, :], func=AF.Copy, incs=["ac"])
        ac_wg = C["ac"]

        for e in range(E):
            em("tensor", "transpose", out=ps_b0[:, e * 16:(e + 1) * 16],
               in_=b_raw[0:16, e * 128:(e + 1) * 128], identity=idf[0:16, 0:16],
               waits=([("ld", ld_consts), ("ac", ac_idb)] if e == 0 else []),
               incs=(["pe"] if e == E - 1 else []))
        pe_b1t = C["pe"]
        em("vector", "tensor_copy", out=b1_sb[:, :], in_=ps_b0[:, 0:128],
           waits=[("pe", pe_b1t)])
        dv_b1 = C["dv"]
        em("sync", "dma_start", out=A(B1T, 0, [[E * KH, 128], [1, E * KH]]),
           in_=b1_sb[:, :], waits=[("dv", dv_b1)], incs=["xs"])

        dv_trbank = 0
        ac_trbank = 0
        tr_last = {0: ("ac", 0), 1: ("dv", 0)}
        pe_wvT = [0]
        pe_woT = [0]

        def emit_wvwo():
            nonlocal dv_trbank
            for c in range(KD):
                for r in range(KD):
                    wt = []
                    if c == 0 and r == 0:
                        wt = [("cvo", cvo_wo), ("ac", ac_idb), tr_last[0], tr_last[1]]
                    if r == 0 and dv_trbank:
                        wt.append(("dv", dv_trbank))
                    em("tensor", "transpose", out=ps_tr[:, r * 128:(r + 1) * 128],
                       in_=WvN[:, r, c, :], identity=idb[:, :], waits=wt,
                       incs=(["pe"] if r == KD - 1 else []))
                em("vector", "tensor_copy", out=WvT[:, c, :, :], in_=ps_tr_v[:, :, :],
                   waits=[("pe", C["pe"])])
                dv_trbank = C["dv"]
            pe_wvT[0] = C["pe"]
            for c in range(KD):
                for r in range(KD):
                    wt = []
                    if c == 0 and r == 0:
                        wt = [("cvo", cvo_wo)]
                    if r == 0:
                        wt.append(("dv", dv_trbank))
                    em("tensor", "transpose", out=ps_tr[:, r * 128:(r + 1) * 128],
                       in_=WoN[:, r, c, :], identity=idb[:, :], waits=wt,
                       incs=(["pe"] if r == KD - 1 else []))
                em("vector", "tensor_copy", out=WoT[:, c, :, :], in_=ps_tr_v[:, :, :],
                   waits=[("pe", C["pe"])])
                dv_trbank = C["dv"]
            pe_woT[0] = C["pe"]

        pe_xnb_free = {0: 0, 1: 0}
        dv_xfree = {0: 0, 1: 0}
        ac_xfree = {0: 0, 1: 0}
        xs_xnb = {0: 0, 1: 0}
        dv_lgbank = {0: dv_b1, 1: 0}
        pe_lg_done = {}
        dv_m = 0
        dv_xncol = 0

        def emit_softmax(t):
            par = (TILE_ORDER.index(t)) % 2
            pbank = ps_b0 if par == 0 else ps_b1
            em("vector", "tensor_copy", out=lg[:, :], in_=pbank[:, 0:NG * E],
               waits=[("pe", pe_lg_done[t])])
            dv_lgbank[par] = C["dv"]
            for g in range(NG):
                em("vector", "reduce_max", out=cols[:, 28 + g:29 + g],
                   in_=lg[:, g * E:(g + 1) * E], axis=AX.X)
            for g in range(NG):
                em("vector", "tensor_scalar", out=sme[:, g * E:(g + 1) * E],
                   in0=lg[:, g * E:(g + 1) * E], scalar1=cols[:, 28 + g:29 + g],
                   scalar2=None, op0=ALU.subtract)
            dvp = C["dv"]
            em("scalar", "activation", out=sme[:, :], in_=sme[:, :], func=AF.Exp,
               waits=[("dv", dvp)], incs=["ac"])
            acp = C["ac"]
            for g in range(NG):
                em("vector", "reduce_sum", out=cols[:, 28 + g:29 + g],
                   in_=sme[:, g * E:(g + 1) * E], axis=AX.X,
                   waits=([("ac", acp)] if g == 0 else []))
            em("vector", "reciprocal", out=cols[:, 28:32], in_=cols[:, 28:32])
            for g in range(NG):
                em("vector", "tensor_scalar", out=sme[:, g * E:(g + 1) * E],
                   in0=sme[:, g * E:(g + 1) * E], scalar1=cols[:, 28 + g:29 + g],
                   scalar2=None, op0=ALU.mult)
            for g in range(NG):
                em("vector", "max", out=m8[:, g * E:(g + 1) * E],
                   in_=sme[:, g * E:(g + 1) * E])
            for g in range(NG):
                em("vector", "tensor_scalar", out=w_t[:, g * E:(g + 1) * E],
                   in0=sme[:, g * E:(g + 1) * E], scalar1=m8[:, g * E + 1:g * E + 2],
                   scalar2=None, op0=ALU.is_ge)
            em("vector", "tensor_tensor", out=w_t[:, :], in0=w_t[:, :], in1=sme[:, :],
               op=ALU.mult)
            em("vector", "tensor_copy", out=w_all_v[:, :, t * NG:(t + 1) * NG],
               in_=w_t.ap().rearrange("p (g e) -> p e g", g=NG))

        sd_tile = {}
        sdsem = lambda p: "sd" if p == 0 else "sd1"
        dv_nmr_t = {}

        def emit_stats(ti):
            t = TILE_ORDER[ti]
            par = ti % 2
            co = par * 32
            sd_x = sd_tile[ti]
            xsl = x_t[:, par * NG * D:(par + 1) * NG * D]
            em("vector", "reduce_sum", out=cols[:, co:co + 4],
               in_=x_tv[:, par, :, :], axis=AX.X, waits=[sd_x])
            dv_xfree[par] = C["dv"]
            for g in range(NG):
                wt = [sd_x] if g == 0 else [("ac", C["ac"])]
                em("scalar", "activation", out=g_grp[:, :],
                   in_=xsl[:, g * D:(g + 1) * D],
                   func=AF.Square, accum_out=cols[:, co + 4 + g:co + 5 + g],
                   waits=wt, incs=["ac"])
            ac_ssq = C["ac"]
            em("vector", "tensor_scalar", out=cols[:, co + 8:co + 12],
               in0=cols[:, co:co + 4], scalar1=1.0 / D, scalar2=None, op0=ALU.mult)
            em("vector", "tensor_tensor", out=cols[:, co + 12:co + 16],
               in0=cols[:, co + 8:co + 12], in1=cols[:, co + 8:co + 12], op=ALU.mult)
            em("vector", "scalar_tensor_tensor", out=cols[:, co + 16:co + 20],
               in0=cols[:, co + 4:co + 8], scalar=1.0 / D, in1=cols[:, co + 12:co + 16],
               op0=ALU.mult, op1=ALU.subtract, waits=[("ac", ac_ssq)])
            dv_var = C["dv"]
            em("scalar", "activation", out=cols[:, co + 16:co + 20],
               in_=cols[:, co + 16:co + 20], func=AF.Ln,
               bias=eps_c[:, 0:1], waits=[("dv", dv_var)], incs=["ac"])
            em("scalar", "activation", out=cols[:, co + 20:co + 24],
               in_=cols[:, co + 16:co + 20], func=AF.Exp,
               scale=mh_c[:, 0:1], incs=["ac"])
            ac_rstd = C["ac"]
            em("vector", "scalar_tensor_tensor", out=cols[:, co + 24:co + 28],
               in0=cols[:, co + 8:co + 12], scalar=-1.0, in1=cols[:, co + 20:co + 24],
               op0=ALU.mult, op1=ALU.mult, waits=[("ac", ac_rstd)])
            dv_nmr_t[ti] = C["dv"]
            em("vector", "tensor_copy", out=mr_all[:, t * NG:(t + 1) * NG],
               in_=cols[:, co + 8:co + 12])
            em("vector", "tensor_copy", out=mr_all[:, GC + t * NG:GC + (t + 1) * NG],
               in_=cols[:, co + 20:co + 24])

        em("scalar", "dma_start", out=x_tv[:, 0, :, :],
           in_=A(x_in, TILE_ORDER[0] * TILE * D, [[D, 128], [128 * D, NG], [1, D]]),
           incs=[sdsem(0)])
        sd_tile[0] = (sdsem(0), C[sdsem(0)])
        emit_stats(0)
        for ti, t in enumerate(TILE_ORDER):
            par = ti % 2
            base = t * TILE
            co = par * 32
            sd_x = sd_tile[ti]
            xsl = x_t[:, par * NG * D:(par + 1) * NG * D]
            if ti + 1 < NT:
                t2 = TILE_ORDER[ti + 1]
                p2 = (ti + 1) % 2
                em("scalar", "dma_start", out=x_tv[:, p2, :, :],
                   in_=A(x_in, t2 * TILE * D, [[D, 128], [128 * D, NG], [1, D]]),
                   waits=([("dv", dv_xfree[p2]), ("ac", ac_xfree[p2])] if ti >= 1 else []),
                   incs=[sdsem(p2)])
                sd_tile[ti + 1] = (sdsem(p2), C[sdsem(p2)])
            dv_nmr = dv_nmr_t[ti]
            ac_xnb = {}
            lbank = ps_b0 if par == 0 else ps_b1
            for g in range(NG):
                wt = [("dv", dv_nmr)] if g == 0 else []
                if g == 0:
                    if pe_xnb_free[par]:
                        wt.append(("pe", pe_xnb_free[par]))
                    if xs_xnb[par]:
                        wt.append(xs_xnb[par])
                em("scalar", "activation",
                   out=xnb[:, (par * NG + g) * D:(par * NG + g + 1) * D],
                   in_=xsl[:, g * D:(g + 1) * D], func=AF.Identity,
                   bias=cols[:, co + 24 + g:co + 25 + g],
                   scale=cols[:, co + 20 + g:co + 21 + g],
                   waits=wt, incs=["ac"])
                ac_xnb[g] = C["ac"]
                for c in range(KD):
                    wt = []
                    if c == 0:
                        wt = [("ac", ac_xnb[g]), tr_last[0], tr_last[1]]
                        if dv_trbank:
                            wt.append(("dv", dv_trbank))
                    em("tensor", "transpose", out=ps_tr[:, c * 128:(c + 1) * 128],
                       in_=xnb[:, (par * NG + g) * D + c * 128:(par * NG + g) * D + (c + 1) * 128],
                       identity=idb[:, :], waits=wt, incs=(["pe"] if c == KD - 1 else []))
                pe_xnb_free[par] = C["pe"]
                em("scalar", "activation", out=xnT_v[:, :, g * 128:(g + 1) * 128],
                   in_=ps_tr_v[:, :, :], func=AF.Copy,
                   waits=[("pe", C["pe"])], incs=["ac"])
                trb = ("ac", C["ac"])
                ac_trbank = C["ac"]
                tr_last[g % 2] = trb
                for k in range(KD):
                    wt = []
                    if k == 0:
                        wt = [trb]
                        if g == 0 and dv_lgbank[par]:
                            wt.append(("dv", dv_lgbank[par]))
                    em("tensor", "matmul", out=lbank[:, g * E:(g + 1) * E],
                       lhsT=xnT[:, k * TILE + g * 128:k * TILE + (g + 1) * 128],
                       rhs=wg_b[:, k * E:(k + 1) * E], start=(k == 0), stop=(k == KD - 1),
                       waits=wt, incs=(["pe"] if k == KD - 1 else []))
            pe_lg_done[t] = C["pe"]
            ac_xfree[par] = C["ac"]
            if ti + 1 < NT:
                emit_stats(ti + 1)
            xssem = "xs" if par == 0 else "xs1"
            em("sync", "dma_start",
               out=A(XNBF, base * D, [[D, 128], [128 * D, NG], [1, D]]),
               in_=xnb.ap().rearrange("p (s g d) -> p s g d", s=2, g=NG)[:, par],
               waits=[("ac", ac_xnb[NG - 1])], incs=[xssem])
            xs_xnb[par] = (xssem, C[xssem])
            if t == 7:
                em("vector", "tensor_copy", out=xn_col[:, :],
                   in_=xnT_v[:, :, SPECIAL_COL:SPECIAL_COL + 1],
                   waits=[tr_last[0], tr_last[1]])
                dv_xncol = C["dv"]
            if ti > 0:
                emit_softmax(TILE_ORDER[ti - 1])
            if ti == 1:
                emit_wvwo()
        emit_softmax(TILE_ORDER[-1])
        dv_wall = C["dv"]

        for m in range(KD):
            for k in range(KD):
                wt = [("dv", dv_trbank), ("dv", dv_m), ("dv", dv_xncol)] \
                    if (m == 0 and k == 0) else []
                em("tensor", "matmul", out=ps_m[:, 40 + m:41 + m],
                   lhsT=WvT[:, k, m, :], rhs=xn_col[:, k:k + 1],
                   start=(k == 0), stop=(k == KD - 1), waits=wt,
                   incs=(["pe"] if (m == KD - 1 and k == KD - 1) else []))
        pe_u = C["pe"]
        em("scalar", "activation", out=u_bf[:, :], in_=ps_m[:, 40:48], func=AF.Copy,
           waits=[("pe", pe_u)], incs=["ac"])
        ac_u = C["ac"]
        for m in range(KD):
            for k in range(KD):
                wt = [("ac", ac_u)] if (m == 0 and k == 0) else []
                em("tensor", "matmul", out=ps_m[:, 48 + m:49 + m],
                   lhsT=WoT[:, k, m, :], rhs=u_bf[:, k:k + 1],
                   start=(k == 0), stop=(k == KD - 1), waits=wt,
                   incs=(["pe"] if (m == KD - 1 and k == KD - 1) else []))
        pe_t = C["pe"]
        em("scalar", "activation", out=t_f[:, :], in_=ps_m[:, 48:56], func=AF.Copy,
           waits=[("pe", pe_t)], incs=["ac"])
        ac_tf = C["ac"]
        em("tensor", "transpose", out=ps_b0[0:8, 384:512], in_=t_f[:, :],
           identity=idf[:, :], waits=[("ac", ac_tf), ("dv", dv_lgbank[0]), ("dv", dv_lgbank[1])],
           incs=["pe"])
        em("vector", "tensor_copy", out=t8[:, :], in_=ps_b0[0:8, 384:512],
           waits=[("pe", C["pe"])])
        dv_t8 = C["dv"]
        for m in range(KD):
            pbank = ps_b0 if m < 4 else ps_b1
            wt = [("dv", dv_t8), ("ld", ld_consts)] if m in (0, 4) else []
            em("tensor", "matmul", out=pbank[:, (m % 4) * 128:(m % 4 + 1) * 128],
               lhsT=oh_sb[0:8, m * 128:(m + 1) * 128], rhs=t8[:, :],
               start=True, stop=True, waits=wt, incs=(["pe"] if m in (3, 7) else []))
            if m == 3:
                pe_bc0 = C["pe"]
        pe_bc1 = C["pe"]
        em("vector", "tensor_copy", out=scr1[:, 0:512], in_=ps_b0[:, 0:512],
           waits=[("pe", pe_bc0)])
        em("vector", "tensor_copy", out=scr1[:, 512:1024], in_=ps_b1[:, 0:512],
           waits=[("pe", pe_bc1)])
        dv_trowbc = C["dv"]
        em("scalar", "dma_start", out=g_grp[:, :],
           in_=A(x_in, (7 * TILE + SPECIAL_GROUP * 128) * D, [[D, 128], [1, D]]),
           waits=[("ac", C["ac"])], incs=["sd"])
        sd_sg = C["sd"]
        gcs = 7 * NG + SPECIAL_GROUP
        em("vector", "tensor_scalar", out=g_grp[:, :], in0=g_grp[:, :],
           scalar1=mr_all[:, gcs:gcs + 1], scalar2=mr_all[:, GC + gcs:GC + gcs + 1],
           op0=ALU.subtract, op1=ALU.mult, waits=[("sd", sd_sg)])
        em("vector", "tensor_sub", out=s_grp[:, :], in0=scr1[:, :], in1=g_grp[:, :],
           waits=[("dv", dv_trowbc)])
        em("vector", "tensor_scalar", out=s_grp[:, :], in0=s_grp[:, :],
           scalar1=tm_sb[:, 0:1], scalar2=None, op0=ALU.mult)
        em("vector", "tensor_add", out=s_grp[:, :], in0=s_grp[:, :], in1=g_grp[:, :])
        dv_sgrp = C["dv"]

        em("vector", "tensor_scalar", out=sel[:, :], in0=w_all[:, :],
           scalar1=0.0, scalar2=None, op0=ALU.is_gt, waits=[("dv", dv_wall)])
        em("vector", "tensor_tensor_scan", out=incl[:, :], data0=segm_sb[:, :],
           data1=sel[:, :], initial=0.0, op0=ALU.mult, op1=ALU.add,
           waits=[("ld", ld_consts)])
        em("vector", "tensor_copy", out=cnt[:, :], in_=incl_v[:, :, GC - 1:GC])
        dv_cnt = C["dv"]
        em("tensor", "matmul", out=ps_m[:, 64:72], lhsT=ltri_sb[:, :], rhs=cnt[:, :],
           start=True, stop=True, waits=[("dv", dv_cnt)], incs=["pe"])
        em("vector", "tensor_copy", out=base_pe[:, :], in_=ps_m[:, 64:72],
           waits=[("pe", C["pe"])])
        em("tensor", "matmul", out=ps_m[0:1, 72:80], lhsT=o128[:, :], rhs=cnt[:, :],
           start=True, stop=True, waits=[("dv", C["dv"])], incs=["pe"])
        em("vector", "tensor_copy", out=tot[:, :], in_=ps_m[0:1, 72:80],
           waits=[("pe", C["pe"])])
        em("vector", "tensor_scalar", out=tr8[:, 0:E], in0=tot[:, :],
           scalar1=float(T3 - 1), scalar2=None, op0=ALU.add)
        em("vector", "tensor_copy", out=tr8i[:, 0:E], in_=tr8[:, 0:E])
        em("vector", "tensor_single_scalar", out=tr8i[:, E:2 * E], in_=tr8i[:, 0:E],
           scalar=T3 - 1, op=ALU.bitwise_and)
        em("vector", "tensor_copy", out=tr8[:, E:2 * E], in_=tr8i[:, E:2 * E])
        em("vector", "tensor_sub", out=tr8[:, 0:E], in0=tr8[:, 0:E], in1=tr8[:, E:2 * E])
        em("vector", "tensor_scalar", out=tr8[:, 0:E], in0=tr8[:, 0:E],
           scalar1=float(T3), scalar2=None, op0=ALU.max)
        em("vector", "memset", tr8[:, 2 * E:3 * E], 1.0)
        em("vector", "tensor_tensor_scan", out=tr8[:, E:2 * E], data0=tr8[:, 2 * E:3 * E],
           data1=tr8[:, 0:E], initial=0.0, op0=ALU.mult, op1=ALU.add)
        em("vector", "tensor_sub", out=tr8[:, 2 * E:3 * E], in0=tr8[:, E:2 * E],
           in1=tr8[:, 0:E])
        dv_seg = C["dv"]
        em("vector", "memset", etile[:, 0:NT3], 0.0)
        for e in range(E):
            em("vector", "tensor_scalar", out=etile[:, NT3:2 * NT3], in0=jtc_sb[:, :],
               scalar1=tr8[:, E + e:E + e + 1], scalar2=None, op0=ALU.is_ge)
            em("vector", "tensor_add", out=etile[:, 0:NT3], in0=etile[:, 0:NT3],
               in1=etile[:, NT3:2 * NT3])
        dv_etile = C["dv"]
        em("vector", "memset", etile[:, NT3:NT3 + 1], -1.0)
        em("vector", "tensor_copy", out=etile[:, NT3 + 1:2 * NT3], in_=etile[:, 0:NT3 - 1])
        em("vector", "tensor_tensor", out=etile[:, 2 * NT3:3 * NT3], in0=etile[:, 0:NT3],
           in1=etile[:, NT3:2 * NT3], op=ALU.not_equal)
        em("vector", "tensor_scalar", out=etile[:, 3 * NT3:4 * NT3], in0=etile[:, 0:NT3],
           scalar1=float(E - 1), scalar2=None, op0=ALU.is_le)
        em("vector", "tensor_tensor", out=etile[:, 2 * NT3:3 * NT3],
           in0=etile[:, 2 * NT3:3 * NT3], in1=etile[:, 3 * NT3:4 * NT3], op=ALU.mult)
        dv_diff = C["dv"]
        em("tensor", "matmul", out=ps_m[:, 80:80 + NT3], lhsT=o1r[:, :],
           rhs=etile[0:1, 0:NT3], start=True, stop=True,
           waits=[("dv", dv_etile)], incs=["pe"])
        em("vector", "tensor_copy", out=ebf[:, :], in_=ps_m[:, 80:80 + NT3],
           waits=[("pe", C["pe"])])
        dv_ebf = C["dv"]
        em("tensor", "matmul", out=ps_m[:, 80 + NT3:80 + 2 * NT3], lhsT=o1r[:, :],
           rhs=etile[0:1, 2 * NT3:3 * NT3], start=True, stop=True,
           waits=[("dv", dv_diff), ("dv", dv_ebf)], incs=["pe"])
        em("vector", "tensor_copy", out=dbf[:, :], in_=ps_m[:, 80 + NT3:80 + 2 * NT3],
           waits=[("pe", C["pe"])])
        dv_dbf = C["dv"]
        em("vector", "memset", etile[:, NT3:NT3 + 2], -1.0)
        em("vector", "tensor_copy", out=etile[:, NT3 + 2:2 * NT3], in_=etile[:, 0:NT3 - 2])
        em("vector", "tensor_tensor", out=etile[:, 2 * NT3:3 * NT3], in0=etile[:, 0:NT3],
           in1=etile[:, NT3:2 * NT3], op=ALU.not_equal)
        em("vector", "tensor_tensor", out=etile[:, 2 * NT3:3 * NT3],
           in0=etile[:, 2 * NT3:3 * NT3], in1=etile[:, 3 * NT3:4 * NT3], op=ALU.mult)
        dv_diff2 = C["dv"]
        em("tensor", "matmul", out=ps_m[:, 80 + NT3:80 + 2 * NT3], lhsT=o1r[:, :],
           rhs=etile[0:1, 2 * NT3:3 * NT3], start=True, stop=True,
           waits=[("dv", dv_diff2), ("dv", dv_dbf)], incs=["pe"])
        em("vector", "tensor_copy", out=dbf2[:, :], in_=ps_m[:, 80 + NT3:80 + 2 * NT3],
           waits=[("pe", C["pe"])])
        em("vector", "tensor_scalar", out=wrow[:, :], in0=ebf[:, :],
           scalar1=float(E - 1), scalar2=256.0, op0=ALU.min, op1=ALU.mult)
        em("vector", "tensor_scalar", out=wrow[:, :], in0=wrow[:, :],
           scalar1=pbase_sb[:, 2:3], scalar2=None, op0=ALU.add)
        for half in range(2):
            em("vector", "tensor_scalar", out=idxw[:, half * NT3:(half + 1) * NT3],
               in0=wrow[:, :], scalar1=float(half * 128) - BIGF, scalar2=None, op0=ALU.add)
            em("vector", "tensor_tensor", out=idxw[:, half * NT3:(half + 1) * NT3],
               in0=idxw[:, half * NT3:(half + 1) * NT3], in1=dbf2[:, :], op=ALU.mult)
            em("vector", "tensor_scalar", out=idxw[:, half * NT3:(half + 1) * NT3],
               in0=idxw[:, half * NT3:(half + 1) * NT3], scalar1=BIGF, scalar2=None,
               op0=ALU.add)
        em("vector", "tensor_copy", out=idxw_i[:, :], in_=idxw[:, :])
        em("vector", "tensor_scalar", out=wrow[:, :], in0=ebf[:, :],
           scalar1=float(E - 1), scalar2=128.0, op0=ALU.min, op1=ALU.mult)
        em("vector", "tensor_scalar", out=wrow[:, :], in0=wrow[:, :],
           scalar1=pbase_sb[:, 2:3], scalar2=2.0, op0=ALU.add, op1=ALU.mult)
        for half in range(2):
            em("vector", "tensor_scalar", out=idxw2[:, half * NT3:(half + 1) * NT3],
               in0=wrow[:, :], scalar1=float(half) - BIGF, scalar2=None, op0=ALU.add)
            em("vector", "tensor_tensor", out=idxw2[:, half * NT3:(half + 1) * NT3],
               in0=idxw2[:, half * NT3:(half + 1) * NT3], in1=dbf[:, :], op=ALU.mult)
            em("vector", "tensor_scalar", out=idxw2[:, half * NT3:(half + 1) * NT3],
               in0=idxw2[:, half * NT3:(half + 1) * NT3], scalar1=BIGF, scalar2=None,
               op0=ALU.add)
        em("vector", "tensor_copy", out=idxw2_i[:, :], in_=idxw2[:, :])
        dv_idxw = C["dv"]
        em("vector", "tensor_scalar", out=idxb[:, :], in0=ebf[:, :],
           scalar1=float(E - 1), scalar2=pbase_sb[:, 1:2], op0=ALU.min, op1=ALU.add)
        em("vector", "tensor_copy", out=idxb_i[:, :], in_=idxb[:, :])
        em("vector", "tensor_scalar", out=wrow[:, :], in0=ebf[:, :],
           scalar1=float(E - 1), scalar2=None, op0=ALU.min)
        em("vector", "tensor_copy", out=idxb2_i[:, :], in_=wrow[:, :])
        dv_idxb = C["dv"]
        em("tensor", "matmul", out=ps_m[:, 160:168], lhsT=o1r[:, :],
           rhs=tr8[0:1, 2 * E:3 * E], start=True, stop=True,
           waits=[("dv", dv_seg), ("dv", dv_idxb)], incs=["pe"])
        em("vector", "tensor_copy", out=gbase[:, :], in_=ps_m[:, 160:168],
           waits=[("pe", C["pe"])])
        em("vector", "tensor_add", out=gbase[:, :], in0=gbase[:, :], in1=base_pe[:, :])
        em("vector", "tensor_sub", out=didx[:, :], in0=incl[:, :], in1=sel[:, :],
           waits=[("zf", zf_total)])
        for e in range(E):
            em("vector", "tensor_scalar", out=didx[:, e * GC:(e + 1) * GC],
               in0=didx[:, e * GC:(e + 1) * GC], scalar1=gbase[:, e:e + 1],
               scalar2=None, op0=ALU.add)
        em("vector", "tensor_scalar", out=didx[:, :], in0=didx[:, :],
           scalar1=BIGF, scalar2=None, op0=ALU.subtract)
        em("vector", "tensor_tensor", out=didx[:, :], in0=didx[:, :], in1=sel[:, :],
           op=ALU.mult)
        em("vector", "tensor_scalar", out=didx[:, :], in0=didx[:, :],
           scalar1=BIGF, scalar2=None, op0=ALU.add)
        em("vector", "memset", runsb[:, :], 0.0)
        for e in range(E):
            em("vector", "tensor_copy", out=slot[:, e * GC:(e + 1) * GC], in_=runsb[:, :])
            if e < E - 1:
                em("vector", "tensor_add", out=runsb[:, :], in0=runsb[:, :],
                   in1=sel[:, e * GC:(e + 1) * GC])
        em("vector", "tensor_scalar", out=mm1[:, :], in0=slot[:, :],
           scalar1=1.0, scalar2=None, op0=ALU.min)
        em("vector", "tensor_scalar", out=mm1[:, :], in0=mm1[:, :],
           scalar1=-1.0, scalar2=1.0, op0=ALU.mult, op1=ALU.add)
        em("vector", "tensor_tensor", out=mm1[:, :], in0=mm1[:, :], in1=sel[:, :],
           op=ALU.mult)
        em("vector", "tensor_sub", out=mm2[:, :], in0=sel[:, :], in1=mm1[:, :])
        em("vector", "tensor_scalar", out=dstc[:, 0:GC], in0=toks_sb[:, :],
           scalar1=2.0, scalar2=None, op0=ALU.mult)
        em("vector", "tensor_scalar", out=dstc[:, GC:2 * GC], in0=dstc[:, 0:GC],
           scalar1=1.0, scalar2=None, op0=ALU.add)
        first_accs = True
        dv_recs_r = {}
        for r, mk in ((0, mm1), (1, mm2)):
            em("vector", "tensor_tensor", out=acc_s[:, :], in0=didx[:, 0:GC],
               in1=mk[:, 0:GC], op=ALU.mult,
               waits=([("zf", zf_total)] if first_accs else []))
            first_accs = False
            em("vector", "tensor_copy", out=didx_r[:, r * GC:(r + 1) * GC], in_=acc_s[:, :])
            for e in range(1, E):
                em("vector", "tensor_tensor", out=acc_s[:, :],
                   in0=didx[:, e * GC:(e + 1) * GC], in1=mk[:, e * GC:(e + 1) * GC],
                   op=ALU.mult)
                em("vector", "tensor_add", out=didx_r[:, r * GC:(r + 1) * GC],
                   in0=didx_r[:, r * GC:(r + 1) * GC], in1=acc_s[:, :])
            em("vector", "tensor_tensor", out=acc_s[:, :], in0=w_all[:, 0:GC],
               in1=mk[:, 0:GC], op=ALU.mult)
            em("vector", "tensor_copy", out=w_r[:, r * GC:(r + 1) * GC], in_=acc_s[:, :])
            for e in range(1, E):
                em("vector", "tensor_tensor", out=acc_s[:, :],
                   in0=w_all[:, e * GC:(e + 1) * GC], in1=mk[:, e * GC:(e + 1) * GC],
                   op=ALU.mult)
                em("vector", "tensor_add", out=w_r[:, r * GC:(r + 1) * GC],
                   in0=w_r[:, r * GC:(r + 1) * GC], in1=acc_s[:, :])
            em("vector", "tensor_copy", out=didx_ri[:, r * GC:(r + 1) * GC],
               in_=didx_r[:, r * GC:(r + 1) * GC])
            em("vector", "tensor_copy", out=recs_v[:, r, :, 0], in_=toks_sb[:, :])
            em("vector", "tensor_copy", out=recs_v[:, r, :, 1],
               in_=w_r[:, r * GC:(r + 1) * GC])
            em("vector", "tensor_copy", out=recs_v[:, r, :, 2],
               in_=dstc[:, r * GC:(r + 1) * GC])
            dv_recs_r[r] = C["dv"]
        dv_recs = C["dv"]

        ac_hbank = {0: 0, 1: 0}
        h_done = {}
        dv_ybank = {0: 0, 1: 0, 2: dv_trowbc, 3: dv_trowbc}
        dv_b2bc = {0: 0, 1: 0}
        pe_free = {"w1a0": pe_wvT[0], "w1b0": pe_woT[0], "w1a1": 0, "w1b1": 0,
                   "w2a": pe_u, "w2b": pe_t}
        b1_ac = {0: 0, 1: 0}
        pe_gx = {0: 0, 1: 0}
        gx_done = {0: 0, 1: 0}
        rc_done = {0: 0, 1: 0, 2: 0, 3: 0}
        dv_rec_slot = {0: 0, 1: 0, 2: 0, 3: 0}
        dv_gxT = dv_trbank
        pe_b1use = 0
        pe_b2use = 0
        dv_zt = 0

        def emit_prefetch_a(j):
            sl = j % 4
            rcsem = ("rc0", "rc1", "rc2", "rc3")[sl]
            em("sync", "dma_start",
               out=rec_sb_v[:, sl, :, :],
               in_=A(LREC, j * T3 * 3, [[3, 128], [128 * 3, 2], [1, 3]]),
               waits=([("sct", sct_total)] if j == 0 else [("dv", dv_rec_slot[sl])]),
               incs=[rcsem])
            rc_done[sl] = C[rcsem]

        def emit_prefetch_b(j):
            sl = j % 4
            sl2 = j % 2
            gxsem = "gx0" if sl2 == 0 else "gx1"
            wt0 = [(("rc0", "rc1", "rc2", "rc3")[sl], rc_done[sl])]
            if j >= 4:
                wt0.append(zs_done[j - 4])
            if j >= 2:
                wt0.append((gxsem, gx_done[sl2]))
            em("vector", "tensor_copy", out=tok_i_v[:, sl, :], in_=rec_sb_v[:, sl, :, 0],
               waits=wt0)
            em("vector", "tensor_copy", out=w_f_v[:, sl, :], in_=rec_sb_v[:, sl, :, 1])
            em("vector", "tensor_copy", out=dst_i_v[:, sl, :], in_=rec_sb_v[:, sl, :, 2])
            dv_rec_slot[sl] = C["dv"]
            dv_rec = C["dv"]
            for h in range(2):
                wt = []
                if h == 0:
                    wt = [("dv", dv_rec), ("xs", C["xs"]), ("xs1", C["xs1"])]
                    if pe_gx[sl2]:
                        wt.append(("pe", pe_gx[sl2]))
                em("gpsimd", "indirect_dma_start",
                   gx_v[:, sl2, h * D:(h + 1) * D], None,
                   XNBF.ap(), IndirectOffsetOnAxis(tok_i_v[:, sl, h:h + 1], 0),
                   bounds_check=SH - 1, oob_is_err=False,
                   waits=wt, incs=[gxsem])
            gx_done[sl2] = C[gxsem]

        def emit_b1(j):
            slw = j % 2
            em("gpsimd", "indirect_dma_start",
               b1_t_v[:, slw, :], None,
               B1T.ap(), IndirectOffsetOnAxis(idxb_i[:, j:j + 1], 0),
               bounds_check=128 * E - 1, oob_is_err=False,
               waits=[("dv", dv_idxb), ("xs", C["xs"]), ("xs1", C["xs1"]),
                      ("ac", b1_ac[slw]),
                      ("b1g", 16 * j)],
               incs=["b1g"])

        def emit_weights_w1(j):
            slw = j % 2
            sfx = "" if slw == 0 else "1"
            for wsem, buf, ihalf in (
                    ("w1a" + sfx, (W1a, W1a1)[slw], 0), ("w1b" + sfx, (W1b, W1b1)[slw], 1)):
                wt = [("dv", dv_idxw), ("pe", pe_free[("w1a" if ihalf == 0 else "w1b") + str(slw)])]
                em("gpsimd", "indirect_dma_start",
                   buf[:, :], None,
                   W1.ap(), IndirectOffsetOnAxis(idxw_i[:, ihalf * NT3 + j:ihalf * NT3 + j + 1], 0),
                   bounds_check=E * 128 * 2 - 1, oob_is_err=False,
                   waits=wt, incs=[wsem])

        def emit_weights_w2(j):
            slw = j % 2
            em("gpsimd", "indirect_dma_start",
               b2bc[:, slw * D:(slw + 1) * D], None,
               b2.ap(), IndirectOffsetOnAxis(idxb2_i[:, j:j + 1], 0),
               bounds_check=E - 1, oob_is_err=False,
               waits=[("dv", dv_b2bc[slw]), ("b2g", 16 * j)] +
                     ([("pe", pe_bc1)] if j < 2 else []),
               incs=["b2g"])
            for wsem, buf, ihalf in (("w2a", W2a, 0), ("w2b", W2b, 1)):
                wt = [("dv", dv_idxw), ("pe", pe_free[wsem])]
                em("gpsimd", "indirect_dma_start",
                   buf[:, :], None,
                   W2.ap(), IndirectOffsetOnAxis(idxw2_i[:, ihalf * NT3 + j:ihalf * NT3 + j + 1], 0),
                   bounds_check=E * 128 * 2 - 1, oob_is_err=False,
                   waits=wt, incs=[wsem])

        def emit_gxtr(j):
            nonlocal dv_gxT
            sl = j % 2
            gxsem = "gx0" if sl == 0 else "gx1"
            gx_j = gx_done[sl]
            for h in range(2):
                for c in range(KD):
                    wt = []
                    if c == 0:
                        wt = [(gxsem, gx_j), ("dv", dv_gxT)] if h == 0 else [("dv", dv_gxT)]
                    em("tensor", "transpose", out=ps_tr[:, c * 128:(c + 1) * 128],
                       in_=gx_v[:, sl, h * D + c * 128:h * D + (c + 1) * 128],
                       identity=idb[:, :], waits=wt, incs=(["pe"] if c == KD - 1 else []))
                if h == 1:
                    pe_gx[sl] = C["pe"]
                em("vector", "tensor_copy", out=gxT_v[:, :, h * 128:(h + 1) * 128],
                   in_=ps_tr_v[:, :, :], waits=[("pe", C["pe"])])
                dv_gxT = C["dv"]

        zs_done = {}
        emit_b1(0)
        emit_weights_w1(0)
        emit_weights_w1(1)
        emit_weights_w2(0)
        for r in range(2):
            for gc in range(GC):
                em("gpsimd", "indirect_dma_start",
                   LREC.ap(),
                   IndirectOffsetOnAxis(didx_ri[:, r * GC + gc:r * GC + gc + 1], 0),
                   recs[:, (r * GC + gc) * 3:(r * GC + gc) * 3 + 3], None,
                   bounds_check=NR - 1, oob_is_err=False,
                   waits=([("dv", dv_recs_r[r])] if gc == 0 else []) +
                         ([("zf", zf_total)] if (r == 0 and gc == 0) else []),
                   incs=["sct"])
        sct_total = C["sct"]
        emit_prefetch_a(0)
        emit_prefetch_b(0)
        emit_prefetch_a(1)
        emit_prefetch_b(1)
        emit_gxtr(0)
        for j in range(NT3):
            sl = j % 2
            gxsem = "gx0" if sl == 0 else "gx1"
            if j + 2 < NT3:
                emit_prefetch_a(j + 2)
            if j >= 1:
                emit_weights_w2(j)
            b1g_j = 16 * (j + 1)
            b2g_j = 16 * (j + 1)
            for m in range(KH):
                bank = m % 2
                wt = []
                if m == 0:
                    wt += [("w1a" + ("" if j % 2 == 0 else "1"), 16 * (j // 2 + 1)),
                           ("dv", dv_gxT), ("b1g", b1g_j)]
                if m == KD:
                    wt.append(("w1b" + ("" if j % 2 == 0 else "1"), 16 * (j // 2 + 1)))
                if ac_hbank[bank] != 0:
                    wt.append(ac_hbank[bank])
                for k in range(KD):
                    em("tensor", "matmul", out=ps_h[bank][:, 0:T3],
                       lhsT=W1vp[j % 2][m // KD][:, k, (m % KD) * 128:(m % KD + 1) * 128],
                       rhs=gxT_v[:, k, :],
                       start=(k == 0), stop=(k == KD - 1),
                       waits=(wt if k == 0 else []),
                       incs=(["pe"] if k == KD - 1 else []))
                if m == KD - 1:
                    pe_free["w1a" + str(j % 2)] = C["pe"]
                if m == KH - 1:
                    pe_free["w1b" + str(j % 2)] = C["pe"]
                pe_h = C["pe"]
                em("scalar", "activation", out=h_bf[:, m * T3:(m + 1) * T3],
                   in_=ps_h[bank][:, 0:T3], func=AF.Relu,
                   bias=b1_t[:, sl * KH + m:sl * KH + m + 1],
                   waits=[("pe", pe_h)], incs=["ac"])
                h_done[m] = ("ac", C["ac"])
                ac_hbank[bank] = h_done[m]
            ac_h_all = C["ac"]
            b1_ac[j % 2] = C["ac"]
            if j + 1 < NT3:
                if j + 2 < NT3:
                    emit_prefetch_b(j + 2)
                emit_b1(j + 1)
                if j + 2 < NT3:
                    emit_weights_w1(j + 2)
                emit_gxtr(j + 1)
            ybanks = [ps_y[0], ps_y[1], ps_b0, ps_b1]
            for h in range(2):
                for nb in range(2):
                    bi = h * 2 + nb
                    wt = []
                    if bi == 0:
                        wt += [("w2a", 16 * (j + 1)), ("ac", ac_h_all)]
                    if dv_ybank[bi]:
                        wt.append(("dv", dv_ybank[bi]))
                    for k in range(KD):
                        em("tensor", "matmul", out=ybanks[bi][:, :],
                           lhsT=hv[:, k, h * 128:(h + 1) * 128],
                           rhs=W2v[0][:, k, nb * 512:(nb + 1) * 512],
                           start=(k == 0), stop=False,
                           waits=(wt if k == 0 else []),
                           incs=(["pe"] if bi == 3 and k == KD - 1 else []))
            pe_free["w2a"] = C["pe"]
            for h in range(2):
                for nb in range(2):
                    bi = h * 2 + nb
                    wt = [("w2b", 16 * (j + 1))] if bi == 0 else []
                    for k in range(KD):
                        em("tensor", "matmul", out=ybanks[bi][:, :],
                           lhsT=hv[:, KD + k, h * 128:(h + 1) * 128],
                           rhs=W2v[1][:, k, nb * 512:(nb + 1) * 512],
                           start=False, stop=(k == KD - 1),
                           waits=(wt if k == 0 else []),
                           incs=(["pe"] if k == KD - 1 else []))
                    if bi == 3:
                        pe_free["w2b"] = C["pe"]
                    pe_ye = C["pe"]
                    wt2 = [("pe", pe_ye), ("b2g", b2g_j)]
                    if dv_zt and bi == 0:
                        wt2.append(zs_done[j - 1])
                    em("vector", "tensor_tensor",
                       out=zt[:, h * D + nb * 512:h * D + (nb + 1) * 512],
                       in0=ybanks[bi][:, :], in1=b2bc[:, sl * D + nb * 512:sl * D + (nb + 1) * 512],
                       op=ALU.add, waits=wt2)
                    em("vector", "tensor_scalar",
                       out=zt[:, h * D + nb * 512:h * D + (nb + 1) * 512],
                       in0=zt[:, h * D + nb * 512:h * D + (nb + 1) * 512],
                       scalar1=w_f_v[:, j % 4, h:h + 1], scalar2=None, op0=ALU.mult)
                    dv_ybank[bi] = C["dv"]
            dv_zt = C["dv"]
            dv_b2bc[sl] = C["dv"]
            zsem = "zs" if j % 2 == 0 else "zs1"
            for h in range(2):
                em("gpsimd", "indirect_dma_start",
                   ZD.ap(), IndirectOffsetOnAxis(dst_i_v[:, j % 4, h:h + 1], 0),
                   zt[:, h * D:(h + 1) * D], None,
                   bounds_check=2 * SH - 1, oob_is_err=False,
                   waits=([("dv", dv_zt), ("zf", zf_total)] if h == 0 else []),
                   incs=[zsem])
            zs_done[j] = (zsem, C[zsem])
        zs_tw = [("zs", C["zs"]), ("zs1", C["zs1"])]

        st_slot = {0: 0, 1: 0}
        dv_zl = {0: 0, 1: 0, 2: 0, 3: 0}
        x4bufs = [g_grp, scr1, zt[:, 0:D], zt[:, D:2 * D]]
        for t in range(NT):
            for g in range(NG):
                gi = t * NG + g
                sl4 = gi % 4
                rcsem = ("rc0", "rc1", "rc2", "rc3")[sl4]
                tokbase = t * TILE + g * 128
                gc4 = t * NG + g
                em("sync", "dma_start",
                   out=x_tv.tensor if False else zn_v[:, 2 * sl4:2 * sl4 + 2, :],
                   in_=A(ZD, 2 * tokbase * D, [[2 * D, 128], [D, 2], [1, D]]),
                   waits=(zs_tw if gi == 0 else []) +
                         ([("dv", dv_zl[sl4])] if dv_zl[sl4] else []),
                   incs=[rcsem])
                em("sync", "dma_start",
                   out=x4bufs[sl4][:, :] if sl4 < 2 else x4bufs[sl4],
                   in_=A(x_in, tokbase * D, [[D, 128], [1, D]]),
                   waits=([("dv", dv_sgrp)] if gi < 2 else []), incs=[rcsem])
                rc_g = C[rcsem]
                z0 = zn_v[:, 2 * sl4, :]
                z1 = zn_v[:, 2 * sl4 + 1, :]
                xb4 = x4bufs[sl4][:, :] if sl4 < 2 else x4bufs[sl4]
                em("vector", "tensor_add", out=z0, in0=z0, in1=z1, waits=[(rcsem, rc_g)])
                dv_y4 = C["dv"]
                em("scalar", "activation", out=z1, in_=z0, func=AF.Silu,
                   waits=[("dv", dv_y4)], incs=["ac"])
                ac_g4 = C["ac"]
                if t == 7 and g == SPECIAL_GROUP:
                    em("vector", "tensor_tensor", out=z0, in0=z1, in1=s_grp[:, :],
                       op=ALU.mult, waits=[("ac", ac_g4)])
                else:
                    em("vector", "tensor_scalar", out=z0, in0=xb4,
                       scalar1=mr_all[:, gc4:gc4 + 1],
                       scalar2=mr_all[:, GC + gc4:GC + gc4 + 1],
                       op0=ALU.subtract, op1=ALU.mult, waits=[("ac", ac_g4)])
                    em("vector", "tensor_tensor", out=z0, in0=z0, in1=z1, op=ALU.mult)
                ssem = "st0" if gi % 2 == 0 else "st1"
                wt = []
                if st_slot[gi % 2]:
                    wt.append((ssem, st_slot[gi % 2]))
                em("vector", "tensor_add", out=out_g[:, (gi % 2) * D:(gi % 2 + 1) * D],
                   in0=z0, in1=xb4, waits=wt)
                dv_zl[sl4] = C["dv"]
                em("scalar", "dma_start", out=A(out, tokbase * D, [[D, 128], [1, D]]),
                   in_=out_g[:, (gi % 2) * D:(gi % 2 + 1) * D],
                   waits=[("dv", dv_zl[sl4])], incs=[ssem])
                st_slot[gi % 2] = C[ssem]

        em("scalar", "activation", out=t_f[:, 0:1], in_=t_f[:, 0:1], func=AF.Copy,
           waits=[("st0", C["st0"]), ("st1", C["st1"])])

        with nc.Block() as block:
            def run(eng_name):
                def body(eng):
                    bndregs = {}
                    if eng_name == "gpsimd":
                        for bv in (NR - 1, SH - 1, 128 * E - 1, 2 * SH - 1,
                                   E * 128 * 2 - 1, E - 1):
                            r = es.enter_context(eng.register(f"bnd{bv}"))
                            eng.reg_mov(r, bv)
                            bndregs[bv] = r
                    for method, args, kw, ws, incs in sched[eng_name]:
                        for s, v in ws:
                            eng.wait_ge(s, v)
                        if "bounds_check" in kw and isinstance(kw["bounds_check"], int):
                            kw = dict(kw, bounds_check=bndregs[kw["bounds_check"]])
                        inst = getattr(eng, method)(*args, **kw)
                        for s in incs:
                            inst.then_inc(s, 16 if "dma_start" in method else 1)
                return body
            block.sync(run("sync"))
            block.gpsimd(run("gpsimd"))
            block.scalar(run("scalar"))
            block.vector(run("vector"))
            block.tensor(run("tensor"))

    return nc


def kernel(**inputs):
    x = np.ascontiguousarray(np.asarray(inputs["L0_attn_inputs"], dtype=np.float32))
    toks = (np.arange(GC)[None, :] * 128 + np.arange(128)[:, None]).astype(np.float32)
    segmask = np.ones((128, E * GC), dtype=np.float32)
    segmask[:, 0::GC] = 0.0
    common = {
        "Wv": np.ascontiguousarray(np.asarray(inputs["L0_attn_Wv"], dtype=np.float32)),
        "Wo": np.ascontiguousarray(np.asarray(inputs["L0_attn_Wo"], dtype=np.float32)),
        "Wg": np.ascontiguousarray(np.asarray(inputs["L0_ffn_Wg"], dtype=np.float32)),
        "W1": np.ascontiguousarray(
            np.asarray(inputs["L0_ffn_W1"], dtype=np.float32)
            .reshape(8, 8, 128, 2, 1024).transpose(0, 3, 2, 1, 4).reshape(8 * 256, 8192)),
        "b1": np.ascontiguousarray(np.asarray(inputs["L0_ffn_b1"], dtype=np.float32)),
        "W2": np.ascontiguousarray(
            np.asarray(inputs["L0_ffn_W2"], dtype=np.float32)
            .reshape(8, 2, 8, 128, 1024).transpose(0, 3, 1, 2, 4).reshape(8 * 256, 8192)),
        "b2": np.ascontiguousarray(np.asarray(inputs["L0_ffn_b2"], dtype=np.float32)),
        "ident": np.eye(128, dtype=np.float32),
        "ltri": np.triu(np.ones((128, 128), dtype=np.float32), 1),
        "ones128": np.ones((128, 1), dtype=np.float32),
        "ones1r": np.ones((1, 128), dtype=np.float32),
        "toksc": toks,
        "pbase": np.stack([np.arange(128) * 128, np.arange(128) * 8, np.arange(128)], axis=1).astype(np.float32),
        "jtc": (np.arange(NT3)[None, :] * T3).astype(np.float32),
        "segm": segmask,
        "onehot": np.repeat(np.eye(8, dtype=np.float32), 128, axis=1),
        "tmask": np.zeros((128, 1), dtype=np.float32),
    }
    tmask_on = np.ones((128, 1), dtype=np.float32)
    nc = build_nc()
    in_maps = []
    for c in range(8):
        b, half = c // 2, c % 2
        m = dict(common)
        m["x"] = np.ascontiguousarray(x[b, half * SH:(half + 1) * SH, :])
        m["tmask"] = tmask_on if half == 1 else common["tmask"]
        in_maps.append(m)
    res = run_bass_kernel_spmd(nc, in_maps, core_ids=list(range(8)))
    outp = np.empty((4, 8192, D), dtype=np.float32)
    for c in range(8):
        b, half = c // 2, c % 2
        outp[b, half * SH:(half + 1) * SH, :] = res.results[c]["out"]
    return outp
